# version 09
# baseline (speedup 1.0000x reference)
import contextlib
import numpy as np
import ml_dtypes
import concourse.bass as bass
import concourse.mybir as mybir
from concourse.bass_utils import run_bass_kernel_spmd

F32 = mybir.dt.float32
BF16 = mybir.dt.bfloat16
AF = mybir.ActivationFunctionType
ALU = mybir.AluOpType
AX = mybir.AxisListType.X

ENGINES = ("sync", "scalar", "tensor", "vector", "gpsimd")
ATT_SCALE = float(1.0 / np.sqrt(128.0))
D = 2048
T = 4096
NB1 = 16
B1 = 256
NQB = 8


class Sem:
    __slots__ = ("h", "n")

    def __init__(self, h):
        self.h = h
        self.n = 0


def _flat(ws, out):
    for w in ws:
        if w is None:
            continue
        if isinstance(w, list):
            _flat(w, out)
        else:
            out.append(w)


class Prog:
    def __init__(self):
        self.q = {e: [] for e in ENGINES}
        self.wm = {e: {} for e in ENGINES}

    def op(self, eng, fn, waits=(), sem=None, inc=1):
        raw = []
        _flat(list(waits), raw)
        ws = []
        for s, v in raw:
            if v <= 0:
                continue
            k = id(s)
            if self.wm[eng].get(k, 0) >= v:
                continue
            self.wm[eng][k] = v
            ws.append((s, v))
        self.q[eng].append((fn, ws, sem, inc))
        if sem is None:
            return None
        sem.n += inc
        return (sem, sem.n)

    def fence(self, eng, toks):
        self.op(eng, lambda e: None, waits=toks, sem=None)

    def flush(self, blk):
        for e in ENGINES:
            ops = self.q[e]
            if not ops:
                continue
            self.q[e] = []

            def mk(ops):
                def body(eng):
                    for fn, ws, sem, inc in ops:
                        for s, v in ws:
                            eng.wait_ge(s.h, v)
                        inst = fn(eng)
                        if sem is not None:
                            inst.then_inc(sem.h, inc)
                return body
            getattr(blk, e)(mk(ops))


def build_program(dbg=False, upto="full"):
    nc = bass.Bass("TRN2", target_bir_lowering=False)

    xT = nc.dram_tensor("xT", [D, T], F32, kind="ExternalInput")
    xTo = nc.dram_tensor("xTo", [D, D], F32, kind="ExternalInput")
    wqT = nc.dram_tensor("wqT", [512, 1024], F32, kind="ExternalInput")
    wkT = nc.dram_tensor("wkT", [512, 1024], F32, kind="ExternalInput")
    wvT = nc.dram_tensor("wvT", [512, 1024], F32, kind="ExternalInput")
    woT = nc.dram_tensor("woT", [256, D], F32, kind="ExternalInput")
    gwT = nc.dram_tensor("gwT", [D, 8], F32, kind="ExternalInput")
    w1T = nc.dram_tensor("w1T", [9, D, 1024], F32, kind="ExternalInput")
    w3T = nc.dram_tensor("w3T", [9, D, 1024], F32, kind="ExternalInput")
    w2T = nc.dram_tensor("w2T", [9, 1024, D], F32, kind="ExternalInput")
    c2s2 = nc.dram_tensor("c2s2", [256, T], F32, kind="ExternalInput")
    permM = nc.dram_tensor("permM", [128, 128], F32, kind="ExternalInput")
    masksI = nc.dram_tensor("masksI", [4, 128, 512], BF16, kind="ExternalInput")
    onescol_bf_i = nc.dram_tensor("onescol_bf", [128, 1], BF16, kind="ExternalInput")
    onescol_f_i = nc.dram_tensor("onescol_f", [128, 1], F32, kind="ExternalInput")
    onesrow_bf_i = nc.dram_tensor("onesrow_bf", [96, 128], BF16, kind="ExternalInput")
    onesrow_f_i = nc.dram_tensor("onesrow_f", [96, 128], F32, kind="ExternalInput")
    ident128_i = nc.dram_tensor("ident128", [128, 128], F32, kind="ExternalInput")
    ident8_i = nc.dram_tensor("ident8", [8, 8], F32, kind="ExternalInput")
    ident1_i = nc.dram_tensor("ident1", [1, 1], F32, kind="ExternalInput")
    ident1b_i = nc.dram_tensor("ident1b", [1, 1], BF16, kind="ExternalInput")

    out = nc.dram_tensor("out", [D, D], F32, kind="ExternalOutput")

    skind = dict(kind="ExternalOutput") if dbg else {}
    xh1_sp = nc.dram_tensor("xh1_sp", [16, 128, T], BF16, **skind)
    q_sp = nc.dram_tensor("q_sp", [8, 128, T], BF16, **skind)
    k_sp = nc.dram_tensor("k_sp", [8, 128, T], BF16, **skind)
    v_sp = nc.dram_tensor("v_sp", [8, 128, T], BF16, **skind)
    o_part = nc.dram_tensor("o_part", [4, 4096, 512], F32)
    or_red = nc.dram_tensor("or_red", [4, 2048, 512], F32)
    or_dbg = (nc.dram_tensor("or_dbg", [4, 2048, 512], F32,
                             kind="ExternalOutput") if dbg else None)
    comb_dram = nc.dram_tensor("comb_dram", [8, 1024], BF16, **skind)
    cw_in = {"q": nc.dram_tensor("cwq_in", [512, 1024], BF16),
             "k": nc.dram_tensor("cwk_in", [512, 1024], BF16),
             "v": nc.dram_tensor("cwv_in", [512, 1024], BF16),
             "o": nc.dram_tensor("cwo_in", [256, D], BF16)}
    cw_out = {"q": nc.dram_tensor("cwq_out", [D, 1024], BF16),
              "k": nc.dram_tensor("cwk_out", [D, 1024], BF16),
              "v": nc.dram_tensor("cwv_out", [D, 1024], BF16),
              "o": nc.dram_tensor("cwo_out", [1024, D], BF16)}

    P = Prog()

    with contextlib.ExitStack() as glob:
        ec = glob.enter_context
        sPE = Sem(ec(nc.semaphore("sPE")))
        sPL = Sem(ec(nc.semaphore("sPL")))
        sV = Sem(ec(nc.semaphore("sV")))
        sSC = Sem(ec(nc.semaphore("sSC")))
        dCON = Sem(ec(nc.semaphore("dCON")))
        dST = [Sem(ec(nc.semaphore(f"dST{i}"))) for i in range(2)]
        dXB = [Sem(ec(nc.semaphore(f"dXB{i}"))) for i in range(4)]
        dSPL = Sem(ec(nc.semaphore("dSPL")))
        dQB = [Sem(ec(nc.semaphore(f"dQB{i}"))) for i in range(2)]
        dKV = [Sem(ec(nc.semaphore(f"dKV{i}"))) for i in range(3)]
        dOP = [Sem(ec(nc.semaphore(f"dOP{i}"))) for i in range(4)]
        sCC = [Sem(ec(nc.semaphore(f"sCC{i}"))) for i in range(4)]
        dW13 = [Sem(ec(nc.semaphore(f"dW13{i}"))) for i in range(2)]
        dW2 = [Sem(ec(nc.semaphore(f"dW2{i}"))) for i in range(2)]
        dX3 = [Sem(ec(nc.semaphore(f"dX3{i}"))) for i in range(2)]
        dCB = Sem(ec(nc.semaphore("dCB")))
        sCW = [Sem(ec(nc.semaphore(f"sCW{i}"))) for i in range(4)]
        dWG = [Sem(ec(nc.semaphore(f"dWG{i}"))) for i in range(3)]
        dCBW = Sem(ec(nc.semaphore("dCBW")))
        dOUT = Sem(ec(nc.semaphore("dOUT")))

        onescol_bf = ec(nc.sbuf_tensor("onescol_bf_s", [128, 1], BF16))
        onescol_f = ec(nc.sbuf_tensor("onescol_f_s", [128, 1], F32))
        onesrow_bf = ec(nc.sbuf_tensor("onesrow_bf_s", [96, 128], BF16))
        onesrow_f = ec(nc.sbuf_tensor("onesrow_f_s", [96, 128], F32))
        ident128 = ec(nc.sbuf_tensor("ident128_s", [128, 128], F32))
        ident8 = ec(nc.sbuf_tensor("ident8_s", [8, 8], F32))
        ident1 = ec(nc.sbuf_tensor("ident1_s", [1, 1], F32))
        ident1b = ec(nc.sbuf_tensor("ident1b_s", [1, 1], BF16))
        perm_bf = ec(nc.sbuf_tensor("perm_bf", [128, 128], BF16))

        blk = glob.enter_context(nc.Block())

        for dst, src in [
            (onescol_bf[:, :], onescol_bf_i[:, :]),
            (onescol_f[:, :], onescol_f_i[:, :]),
            (onesrow_bf[:, :], onesrow_bf_i[:, :]),
            (onesrow_f[:, :], onesrow_f_i[:, :]),
            (ident128[:, :], ident128_i[:, :]),
            (ident8[:, :], ident8_i[:, :]),
            (ident1[:, :], ident1_i[:, :]),
            (ident1b[:, :], ident1b_i[:, :]),
        ]:
            P.op("sync", lambda e, d=dst, s=src: e.dma_start(out=d, in_=s),
                 sem=dCON, inc=16)
        P.op("gpsimd", lambda e: e.dma_start(out=perm_bf[:, :], in_=permM[:, :]),
             sem=dCON, inc=16)
        con_tok = (dCON, dCON.n)

        def self_tile(i, ic, oc, sl, ld, t_abs, ctx, w_bf, spill=None):
            stage, a_bf, s_scr, g1_scr, g2_scr, thr_rows, p_dqs, p_thr = ctx["bufs"]
            t_sum = P.op("tensor", lambda e, sl=sl:
                         e.matmul(p_dqs[0:1, 0:512],
                                  onescol_bf[:, :], a_bf[sl][:, :],
                                  start=True, stop=True, skip_group_check=True),
                         waits=[t_abs, con_tok, ctx["dqs_free"]], sem=sPE)
            ctx["abs_free"][sl] = (sPE, sPE.n)
            t_thr = P.op("vector", lambda e, sl=sl:
                         e.tensor_scalar(thr_rows[sl][:, :],
                                         p_dqs[0:1, 0:512],
                                         2.0 ** -8, 5e-9,
                                         op0=ALU.mult, op1=ALU.max),
                         waits=[t_sum], sem=sV)
            ctx["dqs_free"] = (sV, sV.n)
            t_bc = P.op("tensor", lambda e, sl=sl:
                        e.matmul(p_thr[sl][:, :], onesrow_f[0:1, :],
                                 thr_rows[sl][:, :],
                                 start=True, stop=True, skip_group_check=True),
                        waits=[t_thr, ctx["thr_free"][sl]], sem=sPE)
            t_cp = P.op("vector", lambda e, sl=sl:
                        e.tensor_copy(thr_sb[sl][:, :], p_thr[sl][:, :]),
                        waits=[t_bc, ctx["thrsb_free"][sl]], sem=sV)
            ctx["thr_free"][sl] = (sV, sV.n)
            t_s1 = P.op("gpsimd", lambda e, sl=sl:
                        e.tensor_tensor(s_scr[0][:, :], stage[sl][:, :],
                                        thr_sb[sl][:, :], op=ALU.subtract),
                        waits=[t_cp, ctx["sscr_free"][0]], sem=sPL)
            t_g1 = P.op("scalar", lambda e:
                        e.activation(g1_scr[:, :], s_scr[0][:, :], AF.Sign),
                        waits=[t_s1], sem=sSC)
            t_s2 = P.op("gpsimd", lambda e, sl=sl:
                        e.tensor_tensor(s_scr[1][:, :], stage[sl][:, :],
                                        thr_sb[sl][:, :], op=ALU.add),
                        waits=[t_g1, ctx["sscr_free"][1]], sem=sPL)
            ctx["stage_free"][sl] = (sPL, sPL.n)
            ctx["thrsb_free"][sl] = (sPL, sPL.n)
            t_g2 = P.op("scalar", lambda e:
                        e.activation(g2_scr[:, :], s_scr[1][:, :], AF.Sign),
                        waits=[t_s2], sem=sSC)
            ctx["sscr_free"][1] = (sSC, sSC.n)
            t_h = P.op("vector", lambda e:
                       e.tensor_tensor(s_scr[0][:, :], g1_scr[:, :],
                                       g2_scr[:, :], op=ALU.add),
                       waits=[t_g2], sem=sV)
            ctx["sscr_free"][0] = (sV, sV.n)
            if spill is None:
                t_wq = P.op("vector", lambda e, sl=sl, ic=ic, oc=oc:
                            e.tensor_tensor(w_bf[:, ic, oc * 512:(oc + 1) * 512],
                                            s_scr[0][:, :], thr_sb[sl][:, :],
                                            op=ALU.mult),
                            waits=[t_h], sem=sV)
                ctx["thrsb_free"][sl] = [ctx["thrsb_free"][sl], (sV, sV.n)]
                ctx["sscr_free"][0] = (sV, sV.n)
                return t_wq
            slots, spill_dram = spill
            t_wq = P.op("vector", lambda e, sl=sl:
                        e.tensor_tensor(slots[sl][:, :],
                                        s_scr[0][:, :], thr_sb[sl][:, :],
                                        op=ALU.mult),
                        waits=[t_h, ctx["wqs_free"][sl]], sem=sV)
            ctx["thrsb_free"][sl] = [ctx["thrsb_free"][sl], (sV, sV.n)]
            ctx["sscr_free"][0] = (sV, sV.n)
            sp = P.op("scalar", lambda e, sl=sl, ic=ic, oc=oc:
                      e.dma_start(out=spill_dram[ic * 128:(ic + 1) * 128,
                                                 oc * 512:(oc + 1) * 512],
                                  in_=slots[sl][:, :]),
                      waits=[t_wq], sem=dSPL, inc=16)
            ctx["wqs_free"][sl] = sp
            return sp

        def dequant_gen(w_in, w_bf, IC, OC, ctx, spill=None):
            tiles = [(ic, oc) for oc in range(OC // 512) for ic in range(IC)]
            stage = ctx["bufs"][0]
            a_bf = ctx["bufs"][1]
            for i, (ic, oc) in enumerate(tiles):
                sl = i % 2
                ld = P.op("sync", lambda e, sl=sl, ic=ic, oc=oc, w_in=w_in:
                          e.dma_start(out=stage[sl][:, :],
                                      in_=w_in[ic * 128:(ic + 1) * 128,
                                               oc * 512:(oc + 1) * 512]),
                          waits=[ctx["stage_free"][sl]], sem=dST[sl], inc=16)
                t_abs = P.op("scalar", lambda e, sl=sl:
                             e.activation(a_bf[sl][:, :], stage[sl][:, :],
                                          AF.Abs),
                             waits=[ld, ctx["abs_free"][sl]], sem=sSC)
                yield self_tile(i, ic, oc, sl, ld, t_abs, ctx, w_bf, spill)

        def dequant2(w_in, w_bf, IC, OC, ctx, spill=None):
            last = None
            for last in dequant_gen(w_in, w_bf, IC, OC, ctx, spill):
                pass
            return last

        with contextlib.ExitStack() as sa:
            ea = sa.enter_context
            wq_bf = ea(nc.sbuf_tensor("wq_bf", [128, 16, 1024], BF16))
            wk_bf = ea(nc.sbuf_tensor("wk_bf", [128, 16, 1024], BF16))
            wv_bf = ea(nc.sbuf_tensor("wv_bf", [128, 16, 1024], BF16))
            stage = [ea(nc.sbuf_tensor(f"stage{i}", [128, 512], F32)) for i in range(2)]
            a_bf = [ea(nc.sbuf_tensor(f"a_bf{i}", [128, 512], BF16)) for i in range(2)]
            s_scr = [ea(nc.sbuf_tensor(f"s_scr{i}", [128, 512], F32)) for i in range(2)]
            g1_scr = ea(nc.sbuf_tensor("g1_scr", [128, 512], F32))
            g2_scr = ea(nc.sbuf_tensor("g2_scr", [128, 512], F32))
            thr_rows = [ea(nc.sbuf_tensor(f"thr_r{i}", [1, 512], F32)) for i in range(2)]
            x_bf = [ea(nc.sbuf_tensor(f"x_bf{i}", [128, 16, B1], BF16)) for i in range(4)]
            xh = [ea(nc.sbuf_tensor(f"xh{i}", [128, 16, B1], BF16)) for i in range(2)]
            sq_scr = [ea(nc.sbuf_tensor(f"sq_scr{i}", [128, B1], BF16)) for i in range(2)]
            prj_bf = [ea(nc.sbuf_tensor(f"prj_bf{i}", [128, B1], BF16)) for i in range(2)]
            swp_bf = [ea(nc.sbuf_tensor(f"swp_bf{i}", [128, B1], BF16)) for i in range(2)]
            wqs = [ea(nc.sbuf_tensor(f"wqs{i}", [128, 512], BF16)) for i in range(2)]
            thr_sb = [ea(nc.sbuf_tensor(f"thr_sb{i}", [128, 512], F32)) for i in range(2)]
            c2_bf = ea(nc.sbuf_tensor("c2_bf", [128, T], BF16))
            s2t_bf = ea(nc.sbuf_tensor("s2t_bf", [128, T], BF16))
            rows_r = ea(nc.sbuf_tensor("rows_r", [1, T], BF16))
            rowf1 = ea(nc.sbuf_tensor("rowf1", [96, B1], F32))
            rowf2 = ea(nc.sbuf_tensor("rowf2", [64, B1], F32))
            rowbf = ea(nc.sbuf_tensor("rowbf", [64, B1], BF16))
            m_sb = ea(nc.sbuf_tensor("m_sb", [128, B1], BF16))
            r_sb = ea(nc.sbuf_tensor("r_sb", [128, B1], F32))

            p_swp = [ea(nc.psum_tensor(f"p_swp{i}", [128, B1], F32)) for i in range(2)]
            p_stats = [ea(nc.psum_tensor(f"p_stats{i}", [1, 512], F32)) for i in range(2)]
            p_dq = ea(nc.psum_tensor("p_dq", [1, 512], F32))
            p_thr = [ea(nc.psum_tensor(f"p_thr{i}", [128, 512], F32)) for i in range(2)]

            ln_t0 = [0]
            r_tok = [None]
            rows_r_free = None
            mrow = rowf1[0:1, :]
            msqr = rowf1[32:33, :]
            varr = rowf1[64:65, :]
            rstdr = rowf2[0:1, :]
            mrr = rowf2[32:33, :]
            rstdr_bf = rowbf[0:1, :]
            mrr_bf = rowbf[32:33, :]

            P.op("gpsimd", lambda e: e.dma_start(out=c2_bf[:, :],
                                                 in_=c2s2[0:128, :]),
                 sem=dCON, inc=16)
            P.op("gpsimd", lambda e: e.dma_start(out=s2t_bf[:, :],
                                                 in_=c2s2[128:256, :]),
                 sem=dCON, inc=16)
            rope_tok = (dCON, dCON.n)

            dq_ctx = {"bufs": (stage, a_bf, s_scr, g1_scr, g2_scr, thr_rows,
                               p_dq, p_thr),
                      "stage_free": [None, None], "abs_free": [None, None],
                      "dqs_free": None, "thr_free": [None, None],
                      "wqs_free": [None, None],
                      "thrsb_free": [None, None], "sscr_free": [None, None]}
            xb_free = [None, None, None, None]
            xld_toks = {}
            for i in range(4):
                xld_toks[i] = P.op("gpsimd", lambda e, sl=i, t0=i * B1:
                                   e.dma_start(out=x_bf[sl][:, :, :],
                                               in_=xT[:, t0:t0 + B1].rearrange(
                                                   "(c p) t -> p c t", p=128)),
                                   waits=[], sem=dXB[i], inc=16)

            wg_groups = [[0, 2, 4, 6], [1, 3, 5, 7]]
            gather_toks = {}
            for wi, (nm, w_in, ICg) in enumerate(
                    [("q", wqT, 4), ("k", wkT, 4), ("v", wvT, 4),
                     ("o", woT, 2)]):
                OCg = 1024 if nm != "o" else 2048
                dequant2(w_in[:, :], None, ICg, OCg, dq_ctx,
                         spill=(wqs, cw_in[nm]))
                gather_toks[nm] = P.op(
                    "gpsimd", lambda e, nm=nm:
                    e.collective_compute("AllGather", ALU.bypass,
                                         replica_groups=wg_groups,
                                         ins=[cw_in[nm][:, :]],
                                         outs=[cw_out[nm][:, :]]),
                    waits=[(dSPL, dSPL.n)], sem=sCW[wi])
            ld_q = P.op("sync", lambda e: e.dma_start(
                out=wq_bf[:, :, :],
                in_=cw_out["q"][:, :].rearrange("(c p) n -> p c n", p=128)),
                waits=[gather_toks["q"]], sem=dWG[0], inc=16)
            ld_k = P.op("sync", lambda e: e.dma_start(
                out=wk_bf[:, :, :],
                in_=cw_out["k"][:, :].rearrange("(c p) n -> p c n", p=128)),
                waits=[gather_toks["k"]], sem=dWG[1], inc=16)
            ld_v = P.op("sync", lambda e: e.dma_start(
                out=wv_bf[:, :, :],
                in_=cw_out["v"][:, :].rearrange("(c p) n -> p c n", p=128)),
                waits=[gather_toks["v"]], sem=dWG[2], inc=16)
            dq_wq = ld_q
            dq_kv_tok = [ld_k, ld_v]

            xh_free = [None, None]
            psw_free = [None, None]
            prj_free = [None, None]
            swb_free = [None, None]
            stats_free = [None, None]
            pv_free = None
            sq_free = [None, None]
            dqt = dq_ctx

            def ln_stats(par, xb_sl, xld):
                t = None
                for ic in range(16):
                    t = P.op("tensor", lambda e, ic=ic, sl=xb_sl, par=par:
                             e.matmul(p_stats[par][0:1, 0:B1], onescol_bf[:, :],
                                      x_bf[sl][:, ic, :], start=(ic == 0),
                                      stop=(ic == 15), skip_group_check=True),
                             waits=[xld, con_tok,
                                    stats_free[par] if ic == 0 else None],
                             sem=sPE)
                mean_mm = t
                for ic in range(16):
                    sl2 = ic % 2
                    tsq = P.op("vector", lambda e, ic=ic, sl=xb_sl, sl2=sl2:
                               e.tensor_tensor(sq_scr[sl2][:, :],
                                               x_bf[sl][:, ic, :],
                                               x_bf[sl][:, ic, :], op=ALU.mult),
                               waits=[xld, sq_free[sl2]], sem=sV)
                    t = P.op("tensor", lambda e, ic=ic, sl2=sl2, par=par:
                             e.matmul(p_stats[par][0:1, B1:2 * B1],
                                      onescol_bf[:, :],
                                      sq_scr[sl2][:, :], start=(ic == 0),
                                      stop=(ic == 15), skip_group_check=True),
                             waits=[tsq], sem=sPE)
                    sq_free[sl2] = (sPE, sPE.n)
                return (mean_mm, t)

            def ln_norm(par, xb_sl, xh_sl, mean_mm, sq_mm):
                t = P.op("vector", lambda e, par=par:
                         e.tensor_scalar(mrow, p_stats[par][0:1, 0:B1],
                                         1.0 / 2048.0, None, op0=ALU.mult),
                         waits=[mean_mm, sq_mm], sem=sV)
                t = P.op("vector", lambda e:
                         e.tensor_tensor(msqr, mrow, mrow, op=ALU.mult),
                         waits=[t], sem=sV)
                t = P.op("vector", lambda e, par=par:
                         e.scalar_tensor_tensor(varr, p_stats[par][0:1, B1:2 * B1],
                                                1.0 / 2048.0, msqr,
                                                op0=ALU.mult, op1=ALU.subtract),
                         waits=[t], sem=sV)
                stats_free[par] = (sV, sV.n)
                t = P.op("vector", lambda e:
                         e.tensor_scalar(varr, varr, 1e-5, None, op0=ALU.add),
                         waits=[t], sem=sV)
                t = P.op("scalar", lambda e:
                         e.activation(varr, varr, AF.Ln),
                         waits=[t], sem=sSC)
                t = P.op("scalar", lambda e:
                         e.activation(rstdr, varr, AF.Exp, scale=-0.5),
                         waits=[t], sem=sSC)
                t1 = P.op("vector", lambda e, t0=ln_t0[0]:
                          e.tensor_copy(rows_r[0:1, t0:t0 + B1], rstdr),
                          waits=[t, rows_r_free], sem=sV)

                t2 = P.op("vector", lambda e: e.tensor_copy(mrr_bf, mrow),
                          waits=[t], sem=sV)
                b1 = P.op("tensor", lambda e:
                          e.matmul(p_thr[0][:, 0:B1], onesrow_bf[32:33, :],
                                   mrr_bf,
                                   start=True, stop=True, skip_group_check=True),
                          waits=[t2, dqt["thr_free"][0]], sem=sPE)
                b2 = P.op("tensor", lambda e, t0=ln_t0[0]:
                          e.matmul(p_thr[1][:, 0:B1], onesrow_bf[0:1, :],
                                   rows_r[0:1, t0:t0 + B1],
                                   start=True, stop=True, skip_group_check=True),
                          waits=[t1, dqt["thr_free"][1]], sem=sPE)
                c1 = P.op("vector", lambda e: e.tensor_copy(m_sb[:, :],
                                                            p_thr[0][:, 0:B1]),
                          waits=[b1], sem=sV)
                dqt["thr_free"][0] = (sV, sV.n)
                c2t = P.op("vector", lambda e: e.tensor_copy(r_sb[:, :],
                                                             p_thr[1][:, 0:B1]),
                           waits=[b2], sem=sV)
                dqt["thr_free"][1] = (sV, sV.n)
                r_tok[0] = c2t
                last = None
                for ic in range(16):
                    eng = "gpsimd" if ic % 3 == 2 else "vector"
                    sem_ = sPL if ic % 3 == 2 else sV
                    last = P.op(eng, lambda e, ic=ic, sl=xb_sl, so=xh_sl:
                                e.tensor_tensor(xh[so][:, ic, :],
                                                x_bf[sl][:, ic, :],
                                                m_sb[:, :], op=ALU.subtract),
                                waits=[c1, xh_free[xh_sl] if ic == 0 else None],
                                sem=sem_)
                last = [(sV, sV.n), (sPL, sPL.n)]
                return last

            def proj_rope(w_bf, xh_sl, xh_tok, t0, dst_sp, do_rope):
                for hc in range(8):
                    pp = hc % 2
                    mt = None
                    for ic in range(16):
                        w = [xh_tok]
                        if ic == 0:
                            w.append(psw_free[pp])
                        mt = P.op("tensor", lambda e, hc=hc, ic=ic, sl=xh_sl,
                                  pp=pp:
                                  e.matmul(p_swp[pp][:, :],
                                           w_bf[:, ic, hc * 128:(hc + 1) * 128],
                                           xh[sl][:, ic, :], start=(ic == 0),
                                           stop=(ic == 15),
                                           skip_group_check=True),
                                  waits=w, sem=sPE)
                    ev = P.op("vector", lambda e, pp=pp:
                              e.tensor_tensor(prj_bf[pp][:, :], p_swp[pp][:, :],
                                              r_sb[:, :], op=ALU.mult),
                              waits=[mt, prj_free[pp], r_tok[0]], sem=sV)
                    if not do_rope:
                        psw_free[pp] = (sV, sV.n)
                        st = P.op("scalar", lambda e, hc=hc, t0=t0, pp=pp,
                                  dst_sp=dst_sp:
                                  e.dma_start(out=dst_sp[hc, :, t0:t0 + B1],
                                              in_=prj_bf[pp][:, :]),
                                  waits=[ev], sem=dSPL, inc=16)
                        prj_free[pp] = [(sV, sV.n), (dSPL, dSPL.n)]
                        continue
                    sw = P.op("tensor", lambda e, pp=pp:
                              e.matmul(p_swp[pp][:, :],
                                       perm_bf[:, :], prj_bf[pp][:, :],
                                       start=True, stop=True,
                                       skip_group_check=True),
                              waits=[ev, con_tok], sem=sPE)
                    sv = P.op("vector", lambda e, pp=pp:
                              e.tensor_copy(swp_bf[pp][:, :],
                                            p_swp[pp][:, :]),
                              waits=[sw, swb_free[pp]], sem=sV)
                    psw_free[pp] = (sV, sV.n)
                    t1 = P.op("gpsimd", lambda e, pp=pp, t0=t0:
                              e.tensor_tensor(prj_bf[pp][:, :], prj_bf[pp][:, :],
                                              c2_bf[:, t0:t0 + B1], op=ALU.mult),
                              waits=[sw, rope_tok], sem=sPL)
                    t2 = P.op("gpsimd", lambda e, pp=pp, t0=t0:
                              e.tensor_tensor(swp_bf[pp][:, :], swp_bf[pp][:, :],
                                              s2t_bf[:, t0:t0 + B1], op=ALU.mult),
                              waits=[sv], sem=sPL)
                    t3 = P.op("gpsimd", lambda e, pp=pp:
                              e.tensor_tensor(swp_bf[pp][:, :], prj_bf[pp][:, :],
                                              swp_bf[pp][:, :], op=ALU.add),
                              waits=[t1, t2], sem=sPL)
                    prj_free[pp] = (sPL, sPL.n)
                    st = P.op("scalar", lambda e, hc=hc, t0=t0, pp=pp,
                              dst_sp=dst_sp:
                              e.dma_start(out=dst_sp[hc, :, t0:t0 + B1],
                                          in_=swp_bf[pp][:, :]),
                              waits=[t3], sem=dSPL, inc=16)
                    swb_free[pp] = (dSPL, dSPL.n)

            def xload(b):
                sl = b % 4
                t0 = b * B1
                return P.op("gpsimd", lambda e, sl=sl, t0=t0:
                            e.dma_start(out=x_bf[sl][:, :, :],
                                        in_=xT[:, t0:t0 + B1].rearrange(
                                            "(c p) t -> p c t", p=128)),
                            waits=[xb_free[sl]], sem=dXB[sl], inc=16)

            for b in range(NB1 + 1):
                if b < NB1:
                    stats_toks_b = ln_stats(b % 2, b % 4, xld_toks.pop(b))
                if b > 0:
                    pend = b - 1
                    pxl = pend % 4
                    psl = pend % 2
                    pt0 = pend * B1
                    mm_m, mm_s = pend_stats
                    ln_t0[0] = pt0
                    xht = ln_norm(pend % 2, pxl, psl, mm_m, mm_s)
                    xb_free[pxl] = [(sV, sV.n), (sPL, sPL.n), (sPE, sPE.n)]
                    P.op("scalar", lambda e, psl=psl, pt0=pt0:
                         e.dma_start(out=xh1_sp[:, :, pt0:pt0 + B1].rearrange(
                             "c p t -> p c t"), in_=xh[psl][:, :, :]),
                         waits=[xht], sem=dSPL, inc=16)
                    if pend + 4 < NB1:
                        xld_toks[pend + 4] = xload(pend + 4)
                    if pend == 0:
                        xht = [xht, dq_wq]
                    proj_rope(wq_bf, psl, xht, pt0, q_sp, True)
                    xh_free[psl] = [(sPE, sPE.n), (dSPL, dSPL.n)]
                if b < NB1:
                    pend_stats = stats_toks_b

            for b in range(NB1):
                t0 = b * B1
                sl = b % 2
                xrd = P.op("sync", lambda e, sl=sl, t0=t0:
                           e.dma_start(out=xh[sl][:, :, :],
                                       in_=xh1_sp[:, :, t0:t0 + B1].rearrange(
                                           "c p t -> p c t")),
                           waits=[xh_free[sl],
                                  (dSPL, dSPL.n) if b == 0 else None,
                                  dq_kv_tok if b == 0 else None],
                           sem=dXB[sl], inc=16)
                xh_tok = (dXB[sl], dXB[sl].n)
                b2 = P.op("tensor", lambda e, t0=t0:
                          e.matmul(p_thr[1][:, 0:B1], onesrow_bf[0:1, :],
                                   rows_r[0:1, t0:t0 + B1],
                                   start=True, stop=True, skip_group_check=True),
                          waits=[dqt["thr_free"][1],
                                 (dSPL, dSPL.n) if b == 0 else None], sem=sPE)
                c2t = P.op("vector", lambda e: e.tensor_copy(r_sb[:, :],
                                                             p_thr[1][:, 0:B1]),
                           waits=[b2, r_tok[0]], sem=sV)
                dqt["thr_free"][1] = (sV, sV.n)
                r_tok[0] = c2t
                proj_rope(wk_bf, sl, xh_tok, t0, k_sp, True)
                proj_rope(wv_bf, sl, xh_tok, t0, v_sp, False)
                xh_free[sl] = [(sPE, sPE.n)]

            p1 = dict(pe=(sPE, sPE.n), v=(sV, sV.n), sc=(sSC, sSC.n),
                      spl=(dSPL, dSPL.n))
            if upto == "p1":
                nspl = dSPL.n
                P.op("sync", lambda e: e.wait_ge(dSPL.h, nspl),
                     waits=[p1["pe"], p1["v"], p1["sc"]])
            P.flush(blk)
        if upto == "p1":
            return nc

        with contextlib.ExitStack() as sb:
            eb = sb.enter_context
            wo_bf = eb(nc.sbuf_tensor("wo_bf", [128, 8, 2048], BF16))
            masks = eb(nc.sbuf_tensor("masks_s", [128, 4, 512], BF16))
            qblk = [eb(nc.sbuf_tensor(f"qblk{i}", [128, 8, 512], BF16)) for i in range(2)]
            kslab = [eb(nc.sbuf_tensor(f"kslab{i}", [128, T], BF16)) for i in range(3)]
            vslab = [eb(nc.sbuf_tensor(f"vslab{i}", [128, 32, 128], BF16)) for i in range(3)]
            ptb = [eb(nc.sbuf_tensor(f"ptb{i}", [128, 1024], BF16)) for i in range(2)]
            ptsum = [eb(nc.sbuf_tensor(f"ptsum{i}", [128, 512], BF16)) for i in range(2)]
            yblk = eb(nc.sbuf_tensor("yblk", [128, 8, 512], BF16))
            osb = eb(nc.sbuf_tensor("osb", [128, 16, 512], F32))
            rb_sb = eb(nc.sbuf_tensor("rb_sb", [128, 512], F32))
            row2 = eb(nc.sbuf_tensor("row2", [64, 512], F32))
            p_sc = [eb(nc.psum_tensor(f"p_sc{i}", [128, 1024], F32)) for i in range(2)]
            p_av = [eb(nc.psum_tensor(f"p_av{i}", [128, 512], F32)) for i in range(2)]
            p_sum = eb(nc.psum_tensor("p_sum", [1, 512], F32))
            p_rbo = eb(nc.psum_tensor("p_rbo", [128, 512], F32))
            sum_row = row2[0:1, :]
            r_row = row2[32:33, :]

            p1toks = [p1["pe"], p1["v"], p1["sc"], p1["spl"]]
            for eng in ENGINES:
                P.fence(eng, p1toks)

            P.op("sync", lambda e: e.dma_start(
                out=masks[:, :, :], in_=masksI[:, :, :].rearrange("m p t -> p m t")),
                 sem=dCON, inc=16)
            mask_tok = (dCON, dCON.n)

            P.op("sync", lambda e: e.dma_start(
                out=wo_bf[:, :, :],
                in_=cw_out["o"][:, :].rearrange("(c p) n -> p c n", p=128)),
                 waits=[gather_toks["o"]], sem=dST[0], inc=16)
            dq_wo = (dST[0], dST[0].n)

            sc_free = [None, None]
            av_free = [None, None]
            sum_free = None
            rbo_free = None
            qb_free = [None, None]
            kv_free = [None, None, None]
            ptb_free = [None, None]
            ptsum_free = [None, None]
            yblk_free = None
            osb_free = None

            def mk_score(col, ktile, psl, ksl, qsl, h):
                def f(e):
                    return e.matmul(p_sc[psl][:, col * 512:(col + 1) * 512],
                                    kslab[ksl][:, ktile * 128:(ktile + 1) * 128],
                                    qblk[qsl][:, h, :],
                                    start=True, stop=True, skip_group_check=True)
                return f

            for qb in range(NQB):
                q0 = qb * 512
                KT = (qb + 1) * 4
                qsl = qb % 2
                P.op("sync", lambda e, qsl=qsl, q0=q0:
                     e.dma_start(out=qblk[qsl][:, :, :],
                                 in_=q_sp[:, :, q0:q0 + 512].rearrange(
                                     "h p t -> p h t")),
                     waits=[qb_free[qsl], p1["spl"] if qb == 0 else None],
                     sem=dQB[qsl], inc=16)
                qtok = (dQB[qsl], dQB[qsl].n)
                for h in range(8):
                    it = qb * 8 + h
                    ksl = it % 3
                    KH = KT // 2
                    kvtok_h = None
                    for half in range(2):
                        P.op("sync", lambda e, ksl=ksl, h=h, KT=KT, half=half,
                             KH=KH:
                             e.dma_start(out=kslab[ksl][:, half * KH * 128:
                                                        (KT if half else KH)
                                                        * 128],
                                         in_=k_sp[h, :, half * KH * 128:
                                                  (KT if half else KH) * 128]),
                             waits=[kv_free[ksl]
                                    if half == 0 else None,
                                    p1["spl"] if it == 0 else None],
                             sem=dKV[ksl], inc=16)
                        P.op("sync", lambda e, ksl=ksl, h=h, KT=KT, half=half,
                             KH=KH:
                             e.dma_start(out=vslab[ksl][:, half * KH:
                                                        (KT if half else KH),
                                                        :],
                                         in_=v_sp[h, :, half * KH * 128:
                                                  (KT if half else KH) * 128],
                                         transpose=True),
                             waits=[kvtok_h], sem=dKV[ksl], inc=16)
                        if half == 0:
                            kvtok_h = (dKV[ksl], dKV[ksl].n)
                    kvtok = (dKV[ksl], dKV[ksl].n)
                    hpar = h % 2
                    avmm = None
                    summ = None
                    mt_pend = None
                    for kp in range(KT // 2 + 1):
                        if kp < KT // 2:
                            psl = kp % 2
                            ktok = kvtok_h if 2 * kp + 1 < KH else kvtok
                            P.op("tensor", mk_score(0, 2 * kp, psl, ksl, qsl, h),
                                 waits=[ktok, qtok, sc_free[psl]], sem=sPE)
                            mm2 = P.op("tensor", mk_score(1, 2 * kp + 1, psl,
                                                          ksl, qsl, h),
                                       waits=[], sem=sPE)
                            ex = P.op("scalar", lambda e, psl=psl:
                                      e.activation(ptb[psl][:, :],
                                                   p_sc[psl][:, :],
                                                   AF.Exp, scale=ATT_SCALE),
                                      waits=[mm2, ptb_free[psl]], sem=sSC)
                            sc_free[psl] = (sSC, sSC.n)
                            mtoks = [ex, ex]
                            for j, kt in enumerate((2 * kp, 2 * kp + 1)):
                                di = kt - (KT - 4)
                                if 0 <= di < 4:
                                    mtoks[j] = P.op(
                                        "vector",
                                        lambda e, psl=psl, j=j, di=di:
                                        e.tensor_tensor(
                                            ptb[psl][:, j * 512:(j + 1) * 512],
                                            ptb[psl][:, j * 512:(j + 1) * 512],
                                            masks[:, di, :], op=ALU.mult),
                                        waits=[ex, mask_tok], sem=sV)
                            mt_cur = (kp, mtoks)
                        else:
                            mt_cur = None
                        if mt_pend is not None:
                            kp0, mtoks0 = mt_pend
                            psl0 = kp0 % 2
                            padd = P.op("vector",
                                        lambda e, psl=psl0:
                                        e.tensor_tensor(
                                            ptsum[psl][:, :],
                                            ptb[psl][:, 0:512],
                                            ptb[psl][:, 512:1024],
                                            op=ALU.add),
                                        waits=[mtoks0[0], mtoks0[1],
                                               ptsum_free[psl0]], sem=sV)
                            for j, kt in enumerate((2 * kp0, 2 * kp0 + 1)):
                                avmm = P.op(
                                    "tensor",
                                    lambda e, psl=psl0, j=j, kt=kt, ksl=ksl,
                                    KT=KT, hpar=hpar:
                                    e.matmul(p_av[hpar][:, :],
                                             vslab[ksl][:, kt, :],
                                             ptb[psl][:, j * 512:(j + 1) * 512],
                                             start=(kt == 0),
                                             stop=(kt == KT - 1),
                                             skip_group_check=True),
                                    waits=[mtoks0[j],
                                           av_free[hpar] if kt == 0 else None],
                                    sem=sPE)
                            summ = P.op(
                                "tensor",
                                lambda e, psl=psl0, kp0=kp0, KT=KT:
                                e.matmul(p_sum[:, :], onescol_bf[:, :],
                                         ptsum[psl][:, :],
                                         start=(kp0 == 0),
                                         stop=(kp0 == KT // 2 - 1),
                                         skip_group_check=True),
                                waits=[padd,
                                       sum_free if kp0 == 0 else None],
                                sem=sPE)
                            ptb_free[psl0] = (sPE, sPE.n)
                            ptsum_free[psl0] = (sPE, sPE.n)
                        mt_pend = mt_cur
                    kv_free[ksl] = (sPE, sPE.n)
                    t = P.op("vector", lambda e: e.tensor_copy(sum_row,
                                                               p_sum[:, :]),
                             waits=[summ], sem=sV)
                    sum_free = (sV, sV.n)
                    t = P.op("scalar", lambda e: e.activation(r_row, sum_row,
                                                              AF.Ln),
                             waits=[t], sem=sSC)
                    t = P.op("scalar", lambda e: e.activation(r_row, r_row,
                                                              AF.Exp,
                                                              scale=-1.0),
                             waits=[t], sem=sSC)
                    rb = P.op("tensor", lambda e:
                              e.matmul(p_rbo[:, :], onesrow_f[32:33, :], r_row,
                                       start=True, stop=True,
                                       skip_group_check=True),
                              waits=[t, rbo_free], sem=sPE)
                    t = P.op("vector", lambda e: e.tensor_copy(rb_sb[:, :],
                                                               p_rbo[:, :]),
                             waits=[rb], sem=sV)
                    rbo_free = (sV, sV.n)
                    P.op("vector", lambda e, h=h, hpar=hpar:
                         e.tensor_tensor(yblk[:, h, :], p_av[hpar][:, :],
                                         rb_sb[:, :], op=ALU.mult),
                         waits=[avmm, yblk_free if h == 0 else None], sem=sV)
                    av_free[hpar] = (sV, sV.n)
                y_tok = (sV, sV.n)
                ev_last = None
                for oc in range(16):
                    r3 = oc % 3
                    pt = [p_av[0], p_av[1], p_rbo][r3]
                    mt = None
                    for ic in range(8):
                        w = [y_tok] if ic == 0 else []
                        if qb == 0 and oc == 0 and ic == 0:
                            w.append(dq_wo)
                        if ic == 0:
                            w.append(rbo_free if r3 == 2 else av_free[r3])
                        mt = P.op("tensor", lambda e, oc=oc, ic=ic, pt=pt:
                                  e.matmul(pt[:, :],
                                           wo_bf[:, ic, oc * 128:(oc + 1) * 128],
                                           yblk[:, ic, :], start=(ic == 0),
                                           stop=(ic == 7),
                                           skip_group_check=True),
                                  waits=w, sem=sPE)
                    ev = P.op("vector", lambda e, oc=oc, pt=pt:
                              e.tensor_copy(osb[:, oc, :], pt[:, :]),
                              waits=[mt, osb_free if oc == 0 else None],
                              sem=sV)
                    if r3 == 2:
                        rbo_free = (sV, sV.n)
                    else:
                        av_free[r3] = (sV, sV.n)
                    ev_last = ev
                yblk_free = (sPE, sPE.n)
                qb_free[qsl] = (sPE, sPE.n)
                c = qb % 4
                hf = qb // 4
                P.op("gpsimd", lambda e, c=c, hf=hf:
                     e.dma_start(out=o_part[c, hf * 2048:(hf + 1) * 2048, :]
                                 .rearrange("(o p) t -> p o t", p=128),
                                 in_=osb[:, :, :]),
                     waits=[ev_last], sem=dOP[c], inc=16)
                osb_free = (dOP[c], dOP[c].n)
                if hf == 1:
                    P.op("gpsimd", lambda e, c=c:
                         e.collective_compute(
                             "ReduceScatter", ALU.add,
                             replica_groups=[[0, 1], [2, 3], [4, 5], [6, 7]],
                             ins=[o_part[c, :, :]], outs=[or_red[c, :, :]]),
                         waits=[(dOP[c], dOP[c].n)], sem=sCC[c])

            p2 = dict(pe=(sPE, sPE.n), v=(sV, sV.n), sc=(sSC, sSC.n))
            if upto == "p2":
                if or_dbg is not None:
                    for ch in range(4):
                        P.op("sync", lambda e, ch=ch:
                             e.dma_start(out=or_dbg[ch], in_=or_red[ch]),
                             waits=[(sCC[ch], 1)], sem=dOUT, inc=16)
                P.op("sync", lambda e: e.wait_ge(dOUT.h, 64),
                     waits=[p2["pe"], p2["v"], p2["sc"], (sCC[3], 1)])
            P.flush(blk)
        if upto == "p2":
            return nc

        GRPS = [[0, 1], [2, 3], [4, 5], [6, 7], [8]]
        with contextlib.ExitStack() as s3:
            e3 = s3.enter_context
            x1 = e3(nc.sbuf_tensor("x1", [128, 16, 1024], F32))
            x2bf = e3(nc.sbuf_tensor("x2bf", [128, 16, 1024], BF16))
            a3 = [e3(nc.sbuf_tensor(f"a3_{i}", [128, 8, 1024], BF16)) for i in range(2)]
            w13b = [e3(nc.sbuf_tensor(f"w13b{i}", [128, 16, 256], BF16)) for i in range(2)]
            w2b = [e3(nc.sbuf_tensor(f"w2b{i}", [128, 8, 256], BF16)) for i in range(2)]
            comb_cur = e3(nc.sbuf_tensor("combc", [128, 1024], BF16))
            comb_row = e3(nc.sbuf_tensor("combr", [1, 1024], BF16))
            s1f = [e3(nc.sbuf_tensor(f"s1f{i}", [128, 512], F32)) for i in range(2)]
            sq3 = e3(nc.sbuf_tensor("sq3", [128, 1024], BF16))
            xb3 = e3(nc.sbuf_tensor("xb3", [128, 1024], F32))
            ob3 = e3(nc.sbuf_tensor("ob3", [128, 1024], F32))
            m3_sb = e3(nc.sbuf_tensor("m3_sb", [128, 1024], BF16))
            r3_sb = e3(nc.sbuf_tensor("r3_sb", [128, 1024], BF16))
            g_sb = e3(nc.sbuf_tensor("g_sb", [8, 1024], F32))
            gt_sb = e3(nc.sbuf_tensor("gt_sb", [128, 64], F32))
            rtm = e3(nc.sbuf_tensor("rtm", [128, 8], F32))
            comb_tm = e3(nc.sbuf_tensor("comb_tm", [128, 8, 8], F32))
            comb_fm = e3(nc.sbuf_tensor("comb_fm", [8, 1024], BF16))
            gs_scr = [e3(nc.sbuf_tensor(f"gs{i}", [128, 8], F32)) for i in range(4)]
            gm_scr = [e3(nc.sbuf_tensor(f"gm{i}", [128, 1], F32)) for i in range(3)]
            gsum_r = e3(nc.sbuf_tensor("gsum_r", [1, 8], F32))
            gw_f = e3(nc.sbuf_tensor("gw_f", [128, 16, 8], F32))
            rp1 = e3(nc.sbuf_tensor("rp1", [96, 1024], F32))
            rp2 = e3(nc.sbuf_tensor("rp2", [64, 1024], F32))
            rpb = e3(nc.sbuf_tensor("rpb", [64, 1024], BF16))
            ph = e3(nc.psum_tensor("ph", [128, 1024], F32))
            pg = e3(nc.psum_tensor("pg", [128, 1024], F32))
            pd = [e3(nc.psum_tensor(f"pd{i}", [128, 1024], F32)) for i in range(2)]

            mean3 = rp1[0:1, :]
            msq3 = rp1[32:33, :]
            var3 = rp1[64:65, :]
            rstd3 = rp2[0:1, :]
            mr3 = rp2[32:33, :]
            r3bf = rpb[0:1, :]
            mr3bf = rpb[32:33, :]

            p2toks = [p2["pe"], p2["v"], p2["sc"]]
            for eng in ENGINES:
                P.fence(eng, p2toks)

            P.op("sync", lambda e: e.dma_start(
                out=gw_f[:, :, :],
                in_=gwT[:, :].rearrange("(c p) e -> p c e", p=128)),
                 sem=dCON, inc=16)
            gw_tok = (dCON, dCON.n)

            ph_free = None
            pg_free = None
            pd_free = [None, None]
            w13_free = [None, None]
            w2_free = [None, None]
            x3_free = [None, None]
            a3_free = [None, None]
            sq3_free = None
            s1_free = [None, None]
            crow_free = None
            out_cnt = 0

            gmm = None
            for ic in range(16):
                gmm = P.op("tensor", lambda e, ic=ic:
                           e.matmul(ph[0:1, 0:8], onescol_f[:, :],
                                    gw_f[:, ic, :],
                                    start=(ic == 0), stop=(ic == 15),
                                    skip_group_check=True),
                           waits=[gw_tok, con_tok,
                                  ph_free if ic == 0 else None], sem=sPE)
            gr = P.op("vector", lambda e:
                      e.tensor_scalar(gsum_r[:, :], ph[0:1, 0:8], -1.0, None,
                                      op0=ALU.mult),
                      waits=[gmm], sem=sV)
            ph_free = gr

            for sh in range(2):
                tl = sh * 1024
                add_toks = []
                for ic in range(16):
                    s2_ = ic % 2
                    P.op("sync", lambda e, ic=ic, tl=tl:
                         e.dma_start(out=xb3[:, :],
                                     in_=xTo[ic * 128:(ic + 1) * 128,
                                             tl:tl + 1024]),
                         waits=[x3_free[0]], sem=dX3[0], inc=16)
                    P.op("sync", lambda e, ic=ic, sh=sh:
                         e.dma_start(out=ob3[:, 0:512],
                                     in_=or_red[2 * sh,
                                                ic * 128:(ic + 1) * 128, :]),
                         waits=[(sCC[2 * sh], 1), x3_free[1]],
                         sem=dX3[1], inc=16)
                    P.op("sync", lambda e, ic=ic, sh=sh:
                         e.dma_start(out=ob3[:, 512:1024],
                                     in_=or_red[2 * sh + 1,
                                                ic * 128:(ic + 1) * 128, :]),
                         waits=[(sCC[2 * sh + 1], 1)], sem=dX3[1], inc=16)
                    P.op("vector", lambda e, ic=ic:
                         e.tensor_tensor(x1[:, ic, :], xb3[:, :], ob3[:, :],
                                         op=ALU.add),
                         waits=[(dX3[0], dX3[0].n), (dX3[1], dX3[1].n),
                                (dOUT, out_cnt * 16) if ic == 0 else None],
                         sem=sV)
                    x3_free[0] = (sV, sV.n)
                    x3_free[1] = (sV, sV.n)
                    tc = P.op("vector", lambda e, ic=ic:
                              e.tensor_copy(x2bf[:, ic, :], x1[:, ic, :]),
                              waits=[], sem=sV)
                    add_toks.append(tc)
                mean_mm = None
                for ic in range(16):
                    for cl in range(2):
                        mean_mm = P.op("tensor", lambda e, ic=ic, cl=cl:
                                       e.matmul(ph[0:1, cl * 512:(cl + 1) * 512],
                                                onescol_bf[:, :],
                                                x2bf[:, ic,
                                                     cl * 512:(cl + 1) * 512],
                                                start=(ic == 0), stop=(ic == 15),
                                                skip_group_check=True),
                                       waits=[add_toks[ic],
                                              ph_free if ic == 0 and cl == 0
                                              else None],
                                       sem=sPE)
                sq_mm = None
                for ic in range(16):
                    tsq = P.op("vector", lambda e, ic=ic:
                               e.tensor_tensor(sq3[:, :], x2bf[:, ic, :],
                                               x2bf[:, ic, :], op=ALU.mult),
                               waits=[add_toks[ic], sq3_free], sem=sV)
                    for cl in range(2):
                        sq_mm = P.op("tensor", lambda e, ic=ic, cl=cl:
                                     e.matmul(pg[0:1, cl * 512:(cl + 1) * 512],
                                              onescol_bf[:, :],
                                              sq3[:, cl * 512:(cl + 1) * 512],
                                              start=(ic == 0), stop=(ic == 15),
                                              skip_group_check=True),
                                     waits=[tsq,
                                            pg_free if ic == 0 and cl == 0
                                            else None],
                                     sem=sPE)
                    sq3_free = (sPE, sPE.n)
                t = P.op("vector", lambda e:
                         e.tensor_scalar(mean3, ph[0:1, :], 1.0 / 2048.0,
                                         None, op0=ALU.mult),
                         waits=[mean_mm], sem=sV)
                mrow_tok = t
                ph_free = (sV, sV.n)
                t = P.op("vector", lambda e:
                         e.tensor_tensor(msq3, mean3, mean3, op=ALU.mult),
                         waits=[t], sem=sV)
                t = P.op("vector", lambda e:
                         e.scalar_tensor_tensor(var3, pg[0:1, :],
                                                1.0 / 2048.0, msq3,
                                                op0=ALU.mult, op1=ALU.subtract),
                         waits=[sq_mm, t], sem=sV)
                pg_free = (sV, sV.n)
                t = P.op("vector", lambda e:
                         e.tensor_scalar(var3, var3, 1e-5, None, op0=ALU.add),
                         waits=[t], sem=sV)
                t = P.op("scalar", lambda e:
                         e.activation(var3, var3, AF.Ln),
                         waits=[t], sem=sSC)
                rrow_tok = P.op("scalar", lambda e:
                                e.activation(rstd3, var3, AF.Exp, scale=-0.5),
                                waits=[t], sem=sSC)
                t = P.op("vector", lambda e:
                         e.tensor_tensor(mr3, mean3, rstd3, op=ALU.mult),
                         waits=[rrow_tok], sem=sV)
                t1 = P.op("vector", lambda e: e.tensor_copy(r3bf, rstd3),
                          waits=[t], sem=sV)
                t2 = P.op("vector", lambda e: e.tensor_copy(mr3bf, mr3),
                          waits=[t], sem=sV)
                b1 = b2 = None
                for cl in range(2):
                    b1 = P.op("tensor", lambda e, cl=cl:
                              e.matmul(pd[0][:, cl * 512:(cl + 1) * 512],
                                       onesrow_bf[32:33, :],
                                       mr3bf[0:1, cl * 512:(cl + 1) * 512],
                                       start=True, stop=True,
                                       skip_group_check=True),
                              waits=[t2, pd_free[0] if cl == 0 else None],
                              sem=sPE)
                    b2 = P.op("tensor", lambda e, cl=cl:
                              e.matmul(pd[1][:, cl * 512:(cl + 1) * 512],
                                       onesrow_bf[0:1, :],
                                       r3bf[0:1, cl * 512:(cl + 1) * 512],
                                       start=True, stop=True,
                                       skip_group_check=True),
                              waits=[t1, pd_free[1] if cl == 0 else None],
                              sem=sPE)
                c1 = P.op("vector", lambda e: e.tensor_copy(m3_sb[:, :],
                                                            pd[0][:, :]),
                          waits=[b1], sem=sV)
                pd_free[0] = (sV, sV.n)
                c2_ = P.op("vector", lambda e: e.tensor_copy(r3_sb[:, :],
                                                             pd[1][:, :]),
                           waits=[b2], sem=sV)
                pd_free[1] = (sV, sV.n)
                xh_last = None
                for ic in range(16):
                    t = P.op("vector", lambda e, ic=ic:
                             e.tensor_tensor(sq3[:, :], x2bf[:, ic, :],
                                             r3_sb[:, :], op=ALU.mult),
                             waits=[c2_, sq3_free], sem=sV)
                    xh_last = P.op("vector", lambda e, ic=ic:
                                   e.tensor_tensor(x2bf[:, ic, :], sq3[:, :],
                                                   m3_sb[:, :], op=ALU.subtract),
                                   waits=[c1, t], sem=sV)
                    sq3_free = (sV, sV.n)
                xhat_tok = xh_last
                gmm = None
                for ic in range(16):
                    for cl in range(2):
                        gmm = P.op("tensor", lambda e, ic=ic, cl=cl:
                                   e.matmul(ph[0:8, cl * 512:(cl + 1) * 512],
                                            gw_f[:, ic, :],
                                            x1[:, ic, cl * 512:(cl + 1) * 512],
                                            start=(ic == 0), stop=False,
                                            skip_group_check=True),
                                   waits=[add_toks[ic], gr,
                                          ph_free if ic == 0 and cl == 0
                                          else None],
                                   sem=sPE)
                for cl in range(2):
                    gmm = P.op("tensor", lambda e, cl=cl:
                               e.matmul(ph[0:8, cl * 512:(cl + 1) * 512],
                                        gsum_r[:, :],
                                        mean3[0:1, cl * 512:(cl + 1) * 512],
                                        start=False, stop=True,
                                        skip_group_check=True),
                               waits=[mrow_tok], sem=sPE)
                tg = P.op("vector", lambda e: e.tensor_copy(g_sb[:, :],
                                                            ph[0:8, :]),
                          waits=[gmm], sem=sV)
                ph_free = (sV, sV.n)
                rt = None
                for s8 in range(8):
                    rt = P.op("tensor", lambda e, s8=s8:
                              e.transpose(pd[0][:, s8:s8 + 1],
                                          rstd3[0:1, s8 * 128:(s8 + 1) * 128],
                                          ident1[:, :]),
                              waits=[rrow_tok, con_tok,
                                     pd_free[0] if s8 == 0 else None], sem=sPE)
                rt = P.op("vector", lambda e: e.tensor_copy(rtm[:, :],
                                                            pd[0][:, 0:8]),
                          waits=[rt], sem=sV)
                pd_free[0] = (sV, sV.n)
                gt = None
                for s8 in range(8):
                    gt = P.op("tensor", lambda e, s8=s8:
                              e.transpose(pd[1][:, s8 * 8:(s8 + 1) * 8],
                                          g_sb[:, s8 * 128:(s8 + 1) * 128],
                                          ident8[:, :]),
                              waits=[tg, con_tok,
                                     pd_free[1] if s8 == 0 else None], sem=sPE)
                gt = P.op("vector", lambda e: e.tensor_copy(gt_sb[:, :],
                                                            pd[1][:, 0:64]),
                          waits=[gt], sem=sV)
                pd_free[1] = (sV, sV.n)
                cb_toks = []
                prev_ax = None
                for s8 in range(8):
                    g0 = P.op("vector", lambda e, s8=s8:
                              e.tensor_scalar(gs_scr[0][:, :],
                                              gt_sb[:, s8 * 8:(s8 + 1) * 8],
                                              rtm[:, s8:s8 + 1], None,
                                              op0=ALU.mult),
                              waits=[gt, rt, prev_ax], sem=sV)
                    m1 = P.op("vector", lambda e:
                              e.reduce_max(gm_scr[0][:, :], gs_scr[0][:, :],
                                           axis=AX),
                              waits=[g0], sem=sV)
                    nm = P.op("vector", lambda e:
                              e.tensor_scalar(gm_scr[1][:, :], gm_scr[0][:, :],
                                              -1.0, None, op0=ALU.mult),
                              waits=[m1], sem=sV)
                    ax = P.op("scalar", lambda e:
                              e.activation(gs_scr[1][:, :], gs_scr[0][:, :],
                                           AF.Exp, bias=gm_scr[1][:, :]),
                              waits=[nm], sem=sSC)
                    prev_ax = ax
                    sm = P.op("vector", lambda e:
                              e.reduce_sum(gm_scr[2][:, :], gs_scr[1][:, :],
                                           axis=AX),
                              waits=[ax], sem=sV)
                    eq = P.op("vector", lambda e:
                              e.tensor_scalar(gs_scr[2][:, :], gs_scr[0][:, :],
                                              gm_scr[0][:, :], None,
                                              op0=ALU.is_ge),
                              waits=[m1], sem=sV)
                    cx = P.op("vector", lambda e:
                              e.scalar_tensor_tensor(gs_scr[3][:, :],
                                                     gs_scr[2][:, :], -1e30,
                                                     gs_scr[0][:, :],
                                                     op0=ALU.mult, op1=ALU.add),
                              waits=[eq], sem=sV)
                    m2 = P.op("vector", lambda e:
                              e.reduce_max(gm_scr[1][:, :], gs_scr[3][:, :],
                                           axis=AX),
                              waits=[cx], sem=sV)
                    msk = P.op("vector", lambda e:
                               e.tensor_scalar(gs_scr[2][:, :], gs_scr[0][:, :],
                                               gm_scr[1][:, :], None,
                                               op0=ALU.is_ge),
                               waits=[m2], sem=sV)
                    rc = P.op("vector", lambda e:
                              e.reciprocal(gm_scr[2][:, :], gm_scr[2][:, :]),
                              waits=[sm], sem=sV)
                    ws_ = P.op("vector", lambda e:
                               e.tensor_tensor(gs_scr[1][:, :], gs_scr[1][:, :],
                                               gs_scr[2][:, :], op=ALU.mult),
                               waits=[msk], sem=sV)
                    cb = P.op("vector", lambda e, s8=s8:
                              e.tensor_scalar(comb_tm[:, s8, :],
                                              gs_scr[1][:, :],
                                              gm_scr[2][:, :], None,
                                              op0=ALU.mult),
                              waits=[ws_, rc], sem=sV)
                    cb_toks.append(cb)
                ct = None
                for s8 in range(8):
                    ct = P.op("tensor", lambda e, s8=s8:
                              e.transpose(pd[0][0:8, s8 * 128:(s8 + 1) * 128],
                                          comb_tm[:, s8, :], ident128[:, :]),
                              waits=[cb_toks[s8], con_tok,
                                     pd_free[0] if s8 == 0 else None], sem=sPE)
                ct = P.op("vector", lambda e: e.tensor_copy(comb_fm[:, :],
                                                            pd[0][0:8, :]),
                          waits=[ct], sem=sV)
                pd_free[0] = (sV, sV.n)
                cw = P.op("scalar", lambda e:
                          e.dma_start(out=comb_dram[:, :], in_=comb_fm[:, :]),
                          waits=[ct, (dCB, dCB.n)], sem=dCBW, inc=16)
                cw_tok = (dCBW, dCBW.n)

                for grp, exs in enumerate(GRPS):
                    for el, ex in enumerate(exs):
                        if ex < 8:
                            P.op("sync", lambda e, ex=ex:
                                 e.dma_start(out=comb_row[:, :],
                                             in_=comb_dram[ex:ex + 1, :]),
                                 waits=[cw_tok, crow_free], sem=dCB, inc=16)
                            crd_tok = (dCB, dCB.n)
                            cbm = None
                            for cl in range(2):
                                cbm = P.op("tensor", lambda e, cl=cl:
                                           e.matmul(
                                               pg[:, cl * 512:(cl + 1) * 512],
                                               onesrow_bf[0:1, :],
                                               comb_row[0:1,
                                                        cl * 512:(cl + 1) * 512],
                                               start=True, stop=True,
                                               skip_group_check=True),
                                           waits=[crd_tok,
                                                  pg_free if cl == 0 else None],
                                           sem=sPE)
                            crow_free = (sPE, sPE.n)
                            cbv = P.op("vector", lambda e:
                                       e.tensor_copy(comb_cur[:, :], pg[:, :]),
                                       waits=[cbm], sem=sV)
                            pg_free = (sV, sV.n)
                        else:
                            cbv = None
                        for i8 in range(8):
                            wsl = i8 % 2
                            P.op("gpsimd", lambda e, ex=ex, i8=i8, wsl=wsl:
                                 e.dma_start(
                                     out=w13b[wsl][:, :, 0:128],
                                     in_=w1T[ex, :, i8 * 128:(i8 + 1) * 128]
                                     .rearrange("(c p) i -> p c i", p=128)),
                                 waits=[w13_free[wsl]], sem=dW13[wsl], inc=16)
                            P.op("gpsimd", lambda e, ex=ex, i8=i8, wsl=wsl:
                                 e.dma_start(
                                     out=w13b[wsl][:, :, 128:256],
                                     in_=w3T[ex, :, i8 * 128:(i8 + 1) * 128]
                                     .rearrange("(c p) i -> p c i", p=128)),
                                 waits=[], sem=dW13[wsl], inc=16)
                            wtok = (dW13[wsl], dW13[wsl].n)
                            hm = gm = None
                            for ic in range(16):
                                for cl in range(2):
                                    hm = P.op("tensor",
                                              lambda e, ic=ic, cl=cl, wsl=wsl:
                                              e.matmul(
                                                  ph[:, cl * 512:(cl + 1) * 512],
                                                  w13b[wsl][:, ic, 0:128],
                                                  x2bf[:, ic,
                                                       cl * 512:(cl + 1) * 512],
                                                  start=(ic == 0),
                                                  stop=(ic == 15),
                                                  skip_group_check=True),
                                              waits=[wtok, xhat_tok,
                                                     ph_free
                                                     if ic == 0 and cl == 0
                                                     else None],
                                              sem=sPE)
                            for ic in range(16):
                                for cl in range(2):
                                    gm = P.op("tensor",
                                              lambda e, ic=ic, cl=cl, wsl=wsl:
                                              e.matmul(
                                                  pg[:, cl * 512:(cl + 1) * 512],
                                                  w13b[wsl][:, ic, 128:256],
                                                  x2bf[:, ic,
                                                       cl * 512:(cl + 1) * 512],
                                                  start=(ic == 0),
                                                  stop=(ic == 15),
                                                  skip_group_check=True),
                                              waits=[pg_free
                                                     if ic == 0 and cl == 0
                                                     else None],
                                              sem=sPE)
                            w13_free[wsl] = (sPE, sPE.n)
                            am = None
                            for cl in range(2):
                                si = P.op("scalar", lambda e, cl=cl:
                                          e.activation(
                                              s1f[cl][:, :],
                                              ph[:, cl * 512:(cl + 1) * 512],
                                              AF.Silu),
                                          waits=[hm, s1_free[cl]], sem=sSC)
                                am = P.op("vector",
                                          lambda e, el=el, i8=i8, cl=cl:
                                          e.tensor_tensor(
                                              a3[el][:, i8,
                                                     cl * 512:(cl + 1) * 512],
                                              s1f[cl][:, :],
                                              pg[:, cl * 512:(cl + 1) * 512],
                                              op=ALU.mult),
                                          waits=[si, gm,
                                                 a3_free[el]
                                                 if i8 == 0 and cl == 0
                                                 else None],
                                          sem=sV)
                                s1_free[cl] = (sV, sV.n)
                            ph_free = (sSC, sSC.n)
                            pg_free = (sV, sV.n)
                            if ex < 8:
                                P.op("vector", lambda e, el=el, i8=i8:
                                     e.tensor_tensor(a3[el][:, i8, :],
                                                     a3[el][:, i8, :],
                                                     comb_cur[:, :],
                                                     op=ALU.mult),
                                     waits=[cbv], sem=sV)
                    a_tok = (sV, sV.n)
                    nel = len(exs)
                    for dc in range(16):
                        dsl = dc % 2
                        for el, ex in enumerate(exs):
                            P.op("gpsimd", lambda e, ex=ex, el=el, dc=dc,
                                 dsl=dsl:
                                 e.dma_start(
                                     out=w2b[dsl][:, :, el * 128:(el + 1) * 128],
                                     in_=w2T[ex, :, dc * 128:(dc + 1) * 128]
                                     .rearrange("(c p) d -> p c d", p=128)),
                                 waits=[w2_free[dsl] if el == 0 else None],
                                 sem=dW2[dsl], inc=16)
                        w2tok = (dW2[dsl], dW2[dsl].n)
                        bm = None
                        for el in range(nel):
                            for i8 in range(8):
                                for cl in range(2):
                                    first = (el == 0 and i8 == 0 and cl == 0)
                                    bm = P.op("tensor",
                                              lambda e, el=el, i8=i8, cl=cl,
                                              dsl=dsl, nel=nel:
                                              e.matmul(
                                                  pd[dsl][:, cl * 512:
                                                          (cl + 1) * 512],
                                                  w2b[dsl][:, i8,
                                                           el * 128:
                                                           (el + 1) * 128],
                                                  a3[el][:, i8,
                                                         cl * 512:
                                                         (cl + 1) * 512],
                                                  start=(i8 == 0 and el == 0),
                                                  stop=(i8 == 7 and
                                                        el == nel - 1),
                                                  skip_group_check=True),
                                              waits=[w2tok, a_tok,
                                                     pd_free[dsl] if first
                                                     else None],
                                              sem=sPE)
                        w2_free[dsl] = (sPE, sPE.n)
                        P.op("vector", lambda e, dc=dc, dsl=dsl:
                             e.tensor_tensor(x1[:, dc, :], x1[:, dc, :],
                                             pd[dsl][:, :], op=ALU.add),
                             waits=[bm], sem=sV)
                        pd_free[dsl] = (sV, sV.n)
                    for el in range(nel):
                        a3_free[el] = (sPE, sPE.n)
                fin = (sV, sV.n)
                P.op("scalar", lambda e, tl=tl:
                     e.dma_start(out=out[:, tl:tl + 1024].rearrange(
                         "(c p) t -> p c t", p=128), in_=x1[:, :, :]),
                     waits=[fin], sem=dOUT, inc=16)
                out_cnt += 1

            final_cnt = out_cnt * 16
            P.op("sync", lambda e: e.wait_ge(dOUT.h, final_cnt), waits=[])
            P.flush(blk)

    return nc


_NC = None


def _get_nc():
    global _NC
    if _NC is None:
        _NC = build_program()
    return _NC


def _host_consts():
    f32 = np.float32
    pos = np.arange(T, dtype=np.float64)
    inv = 10000.0 ** (-np.arange(64, dtype=np.float64) / 64.0)
    fr = np.outer(pos, inv)
    cos = np.cos(fr).T
    sin = np.sin(fr).T
    c2s2 = np.empty((256, T), f32)
    c2s2[0:64] = cos
    c2s2[64:128] = cos
    c2s2[128:192] = -sin
    c2s2[192:256] = sin
    perm = np.roll(np.eye(128, dtype=f32), 64, axis=0)
    masks = np.zeros((4, 128, 512), dtype=ml_dtypes.bfloat16)
    ki = np.arange(128)[:, None]
    qj = np.arange(512)[None, :]
    for d in range(4):
        masks[d] = (ki + 128 * d <= qj).astype(ml_dtypes.bfloat16)
    return dict(
        c2s2=c2s2,
        permM=perm,
        masksI=masks,
        onescol_bf=np.ones((128, 1), ml_dtypes.bfloat16),
        onescol_f=np.ones((128, 1), f32),
        onesrow_bf=np.ones((96, 128), ml_dtypes.bfloat16),
        onesrow_f=np.ones((96, 128), f32),
        ident128=np.eye(128, dtype=f32),
        ident8=np.eye(8, dtype=f32),
        ident1=np.ones((1, 1), f32),
        ident1b=np.ones((1, 1), ml_dtypes.bfloat16),
    )


def kernel(L0_attn_x, L0_attn_w_q, L0_attn_w_k, L0_attn_w_v, L0_attn_w_o,
           L0_ffn_gate_w, L0_ffn_w1, L0_ffn_w2, L0_ffn_w3,
           L0_ffn_ws1, L0_ffn_ws2, L0_ffn_ws3):
    f32 = np.float32
    x = np.asarray(L0_attn_x, f32)
    wq = np.asarray(L0_attn_w_q, f32)
    wk = np.asarray(L0_attn_w_k, f32)
    wv = np.asarray(L0_attn_w_v, f32)
    wo = np.asarray(L0_attn_w_o, f32)
    gw = np.asarray(L0_ffn_gate_w, f32)
    w1 = np.asarray(L0_ffn_w1, f32)
    w2 = np.asarray(L0_ffn_w2, f32)
    w3 = np.asarray(L0_ffn_w3, f32)
    ws1 = np.asarray(L0_ffn_ws1, f32)
    ws2 = np.asarray(L0_ffn_ws2, f32)
    ws3 = np.asarray(L0_ffn_ws3, f32)

    in_maps = _make_in_maps(x, wq, wk, wv, wo, gw, w1, w2, w3, ws1, ws2, ws3)

    nc = _get_nc()
    res = run_bass_kernel_spmd(nc, in_maps, core_ids=list(range(8)))
    outp = np.empty((4, T, D), f32)
    for c in range(8):
        b, h = c // 2, c % 2
        outp[b, h * 2048:(h + 1) * 2048, :] = np.asarray(res.results[c]["out"]).T
    return outp


def _make_in_maps(x, wq, wk, wv, wo, gw, w1, w2, w3, ws1, ws2, ws3):
    w1T = np.ascontiguousarray(np.concatenate(
        [np.transpose(w1, (0, 2, 1)), ws1.T[None]], axis=0))
    w3T = np.ascontiguousarray(np.concatenate(
        [np.transpose(w3, (0, 2, 1)), ws3.T[None]], axis=0))
    w2T = np.ascontiguousarray(np.concatenate(
        [np.transpose(w2, (0, 2, 1)), ws2.T[None]], axis=0))
    gwT = np.ascontiguousarray(gw.T)
    consts = _host_consts()

    in_maps = []
    for c in range(8):
        b, h = c // 2, c % 2
        m = dict(consts)
        m["xT"] = np.ascontiguousarray(x[b].T)
        m["xTo"] = np.ascontiguousarray(x[b, h * 2048:(h + 1) * 2048, :].T)
        gp = c // 2
        m["wqT"] = np.ascontiguousarray(
            wq[h * 1024:(h + 1) * 1024, :].T[512 * gp:512 * (gp + 1), :])
        m["wkT"] = np.ascontiguousarray(
            wk[h * 1024:(h + 1) * 1024, :].T[512 * gp:512 * (gp + 1), :])
        m["wvT"] = np.ascontiguousarray(
            wv[h * 1024:(h + 1) * 1024, :].T[512 * gp:512 * (gp + 1), :])
        m["woT"] = np.ascontiguousarray(
            wo[:, h * 1024:(h + 1) * 1024].T[256 * gp:256 * (gp + 1), :])
        m["gwT"] = gwT
        m["w1T"] = w1T
        m["w3T"] = w3T
        m["w2T"] = w2T
        in_maps.append(m)
    return in_maps


# version 10
# speedup vs baseline: 1.0087x; 1.0087x over previous
import contextlib
import numpy as np
import ml_dtypes
import concourse.bass as bass
import concourse.mybir as mybir
from concourse.bass_utils import run_bass_kernel_spmd

F32 = mybir.dt.float32
BF16 = mybir.dt.bfloat16
AF = mybir.ActivationFunctionType
ALU = mybir.AluOpType
AX = mybir.AxisListType.X

ENGINES = ("sync", "scalar", "tensor", "vector", "gpsimd")
ATT_SCALE = float(1.0 / np.sqrt(128.0))
D = 2048
T = 4096
NB1 = 16
B1 = 256
NQB = 8


class Sem:
    __slots__ = ("h", "n")

    def __init__(self, h):
        self.h = h
        self.n = 0


def _flat(ws, out):
    for w in ws:
        if w is None:
            continue
        if isinstance(w, list):
            _flat(w, out)
        else:
            out.append(w)


class Prog:
    def __init__(self):
        self.q = {e: [] for e in ENGINES}
        self.wm = {e: {} for e in ENGINES}

    def op(self, eng, fn, waits=(), sem=None, inc=1):
        raw = []
        _flat(list(waits), raw)
        ws = []
        for s, v in raw:
            if v <= 0:
                continue
            k = id(s)
            if self.wm[eng].get(k, 0) >= v:
                continue
            self.wm[eng][k] = v
            ws.append((s, v))
        self.q[eng].append((fn, ws, sem, inc))
        if sem is None:
            return None
        sem.n += inc
        return (sem, sem.n)

    def fence(self, eng, toks):
        self.op(eng, lambda e: None, waits=toks, sem=None)

    def flush(self, blk):
        for e in ENGINES:
            ops = self.q[e]
            if not ops:
                continue
            self.q[e] = []

            def mk(ops):
                def body(eng):
                    for fn, ws, sem, inc in ops:
                        for s, v in ws:
                            eng.wait_ge(s.h, v)
                        inst = fn(eng)
                        if sem is not None:
                            inst.then_inc(sem.h, inc)
                return body
            getattr(blk, e)(mk(ops))


def build_program(dbg=False, upto="full"):
    nc = bass.Bass("TRN2", target_bir_lowering=False)

    xT = nc.dram_tensor("xT", [D, T], F32, kind="ExternalInput")
    xTo = nc.dram_tensor("xTo", [D, D], F32, kind="ExternalInput")
    wqT = nc.dram_tensor("wqT", [512, 1024], F32, kind="ExternalInput")
    wkT = nc.dram_tensor("wkT", [512, 1024], F32, kind="ExternalInput")
    wvT = nc.dram_tensor("wvT", [512, 1024], F32, kind="ExternalInput")
    woT = nc.dram_tensor("woT", [256, D], F32, kind="ExternalInput")
    gwT = nc.dram_tensor("gwT", [D, 8], F32, kind="ExternalInput")
    w1T = nc.dram_tensor("w1T", [9, D, 1024], F32, kind="ExternalInput")
    w3T = nc.dram_tensor("w3T", [9, D, 1024], F32, kind="ExternalInput")
    w2T = nc.dram_tensor("w2T", [9, 1024, D], F32, kind="ExternalInput")
    c2s2 = nc.dram_tensor("c2s2", [256, T], F32, kind="ExternalInput")
    permM = nc.dram_tensor("permM", [128, 128], F32, kind="ExternalInput")
    masksI = nc.dram_tensor("masksI", [4, 128, 512], BF16, kind="ExternalInput")
    onescol_bf_i = nc.dram_tensor("onescol_bf", [128, 1], BF16, kind="ExternalInput")
    onescol_f_i = nc.dram_tensor("onescol_f", [128, 1], F32, kind="ExternalInput")
    onesrow_bf_i = nc.dram_tensor("onesrow_bf", [96, 128], BF16, kind="ExternalInput")
    onesrow_f_i = nc.dram_tensor("onesrow_f", [96, 128], F32, kind="ExternalInput")
    ident128_i = nc.dram_tensor("ident128", [128, 128], F32, kind="ExternalInput")
    ident8_i = nc.dram_tensor("ident8", [8, 8], F32, kind="ExternalInput")
    ident1_i = nc.dram_tensor("ident1", [1, 1], F32, kind="ExternalInput")
    ident1b_i = nc.dram_tensor("ident1b", [1, 1], BF16, kind="ExternalInput")

    out = nc.dram_tensor("out", [D, D], F32, kind="ExternalOutput")

    skind = dict(kind="ExternalOutput") if dbg else {}
    xh1_sp = nc.dram_tensor("xh1_sp", [16, 128, T], BF16, **skind)
    q_sp = nc.dram_tensor("q_sp", [8, 128, T], BF16, **skind)
    k_sp = nc.dram_tensor("k_sp", [8, 128, T], BF16, **skind)
    v_sp = nc.dram_tensor("v_sp", [8, 128, T], BF16, **skind)
    o_part = nc.dram_tensor("o_part", [4, 4096, 512], F32)
    or_red = nc.dram_tensor("or_red", [4, 2048, 512], F32)
    or_dbg = (nc.dram_tensor("or_dbg", [4, 2048, 512], F32,
                             kind="ExternalOutput") if dbg else None)
    comb_dram = nc.dram_tensor("comb_dram", [8, 1024], BF16, **skind)
    cw_in = {"q": nc.dram_tensor("cwq_in", [512, 1024], BF16),
             "k": nc.dram_tensor("cwk_in", [512, 1024], BF16),
             "v": nc.dram_tensor("cwv_in", [512, 1024], BF16),
             "o": nc.dram_tensor("cwo_in", [256, D], BF16)}
    cw_out = {"q": nc.dram_tensor("cwq_out", [D, 1024], BF16),
              "k": nc.dram_tensor("cwk_out", [D, 1024], BF16),
              "v": nc.dram_tensor("cwv_out", [D, 1024], BF16),
              "o": nc.dram_tensor("cwo_out", [1024, D], BF16)}

    P = Prog()

    with contextlib.ExitStack() as glob:
        ec = glob.enter_context
        sPE = Sem(ec(nc.semaphore("sPE")))
        sPL = Sem(ec(nc.semaphore("sPL")))
        sV = Sem(ec(nc.semaphore("sV")))
        sSC = Sem(ec(nc.semaphore("sSC")))
        dCON = Sem(ec(nc.semaphore("dCON")))
        dST = [Sem(ec(nc.semaphore(f"dST{i}"))) for i in range(2)]
        dXB = [Sem(ec(nc.semaphore(f"dXB{i}"))) for i in range(4)]
        dSPL = Sem(ec(nc.semaphore("dSPL")))
        dQB = [Sem(ec(nc.semaphore(f"dQB{i}"))) for i in range(2)]
        dKV = [Sem(ec(nc.semaphore(f"dKV{i}"))) for i in range(3)]
        dOP = [Sem(ec(nc.semaphore(f"dOP{i}"))) for i in range(4)]
        sCC = [Sem(ec(nc.semaphore(f"sCC{i}"))) for i in range(4)]
        dW13 = [Sem(ec(nc.semaphore(f"dW13{i}"))) for i in range(2)]
        dW2 = [Sem(ec(nc.semaphore(f"dW2{i}"))) for i in range(2)]
        dX3 = [Sem(ec(nc.semaphore(f"dX3{i}"))) for i in range(2)]
        dCB = Sem(ec(nc.semaphore("dCB")))
        sCW = [Sem(ec(nc.semaphore(f"sCW{i}"))) for i in range(4)]
        dWG = [Sem(ec(nc.semaphore(f"dWG{i}"))) for i in range(3)]
        dCBW = Sem(ec(nc.semaphore("dCBW")))
        dOUT = Sem(ec(nc.semaphore("dOUT")))

        onescol_bf = ec(nc.sbuf_tensor("onescol_bf_s", [128, 1], BF16))
        onescol_f = ec(nc.sbuf_tensor("onescol_f_s", [128, 1], F32))
        onesrow_bf = ec(nc.sbuf_tensor("onesrow_bf_s", [96, 128], BF16))
        onesrow_f = ec(nc.sbuf_tensor("onesrow_f_s", [96, 128], F32))
        ident128 = ec(nc.sbuf_tensor("ident128_s", [128, 128], F32))
        ident8 = ec(nc.sbuf_tensor("ident8_s", [8, 8], F32))
        ident1 = ec(nc.sbuf_tensor("ident1_s", [1, 1], F32))
        ident1b = ec(nc.sbuf_tensor("ident1b_s", [1, 1], BF16))
        perm_bf = ec(nc.sbuf_tensor("perm_bf", [128, 128], BF16))

        blk = glob.enter_context(nc.Block())

        for dst, src in [
            (onescol_bf[:, :], onescol_bf_i[:, :]),
            (onescol_f[:, :], onescol_f_i[:, :]),
            (onesrow_bf[:, :], onesrow_bf_i[:, :]),
            (onesrow_f[:, :], onesrow_f_i[:, :]),
            (ident128[:, :], ident128_i[:, :]),
            (ident8[:, :], ident8_i[:, :]),
            (ident1[:, :], ident1_i[:, :]),
            (ident1b[:, :], ident1b_i[:, :]),
        ]:
            P.op("sync", lambda e, d=dst, s=src: e.dma_start(out=d, in_=s),
                 sem=dCON, inc=16)
        P.op("gpsimd", lambda e: e.dma_start(out=perm_bf[:, :], in_=permM[:, :]),
             sem=dCON, inc=16)
        con_tok = (dCON, dCON.n)

        def self_tile(i, ic, oc, sl, ld, t_abs, ctx, w_bf, spill=None):
            stage, a_bf, s_scr, g1_scr, g2_scr, thr_rows, p_dqs, p_thr = ctx["bufs"]
            t_sum = P.op("tensor", lambda e, sl=sl:
                         e.matmul(p_dqs[0:1, 0:512],
                                  onescol_bf[:, :], a_bf[sl][:, :],
                                  start=True, stop=True, skip_group_check=True),
                         waits=[t_abs, con_tok, ctx["dqs_free"]], sem=sPE)
            ctx["abs_free"][sl] = (sPE, sPE.n)
            t_thr = P.op("vector", lambda e, sl=sl:
                         e.tensor_scalar(thr_rows[sl][:, :],
                                         p_dqs[0:1, 0:512],
                                         2.0 ** -8, 5e-9,
                                         op0=ALU.mult, op1=ALU.max),
                         waits=[t_sum], sem=sV)
            ctx["dqs_free"] = (sV, sV.n)
            t_bc = P.op("tensor", lambda e, sl=sl:
                        e.matmul(p_thr[sl][:, :], onesrow_f[0:1, :],
                                 thr_rows[sl][:, :],
                                 start=True, stop=True, skip_group_check=True),
                        waits=[t_thr, ctx["thr_free"][sl]], sem=sPE)
            t_cp = P.op("vector", lambda e, sl=sl:
                        e.tensor_copy(thr_sb[sl][:, :], p_thr[sl][:, :]),
                        waits=[t_bc, ctx["thrsb_free"][sl]], sem=sV)
            ctx["thr_free"][sl] = (sV, sV.n)
            t_s1 = P.op("gpsimd", lambda e, sl=sl:
                        e.tensor_tensor(s_scr[0][:, :], stage[sl][:, :],
                                        thr_sb[sl][:, :], op=ALU.subtract),
                        waits=[t_cp, ctx["sscr_free"][0]], sem=sPL)
            t_g1 = P.op("scalar", lambda e:
                        e.activation(g1_scr[:, :], s_scr[0][:, :], AF.Sign),
                        waits=[t_s1], sem=sSC)
            t_s2 = P.op("gpsimd", lambda e, sl=sl:
                        e.tensor_tensor(s_scr[1][:, :], stage[sl][:, :],
                                        thr_sb[sl][:, :], op=ALU.add),
                        waits=[t_g1, ctx["sscr_free"][1]], sem=sPL)
            ctx["stage_free"][sl] = (sPL, sPL.n)
            ctx["thrsb_free"][sl] = (sPL, sPL.n)
            t_g2 = P.op("scalar", lambda e:
                        e.activation(g2_scr[:, :], s_scr[1][:, :], AF.Sign),
                        waits=[t_s2], sem=sSC)
            ctx["sscr_free"][1] = (sSC, sSC.n)
            t_h = P.op("vector", lambda e:
                       e.tensor_tensor(s_scr[0][:, :], g1_scr[:, :],
                                       g2_scr[:, :], op=ALU.add),
                       waits=[t_g2], sem=sV)
            ctx["sscr_free"][0] = (sV, sV.n)
            if spill is None:
                t_wq = P.op("vector", lambda e, sl=sl, ic=ic, oc=oc:
                            e.tensor_tensor(w_bf[:, ic, oc * 512:(oc + 1) * 512],
                                            s_scr[0][:, :], thr_sb[sl][:, :],
                                            op=ALU.mult),
                            waits=[t_h], sem=sV)
                ctx["thrsb_free"][sl] = [ctx["thrsb_free"][sl], (sV, sV.n)]
                ctx["sscr_free"][0] = (sV, sV.n)
                return t_wq
            slots, spill_dram = spill
            t_wq = P.op("vector", lambda e, sl=sl:
                        e.tensor_tensor(slots[sl][:, :],
                                        s_scr[0][:, :], thr_sb[sl][:, :],
                                        op=ALU.mult),
                        waits=[t_h, ctx["wqs_free"][sl]], sem=sV)
            ctx["thrsb_free"][sl] = [ctx["thrsb_free"][sl], (sV, sV.n)]
            ctx["sscr_free"][0] = (sV, sV.n)
            sp = P.op("scalar", lambda e, sl=sl, ic=ic, oc=oc:
                      e.dma_start(out=spill_dram[ic * 128:(ic + 1) * 128,
                                                 oc * 512:(oc + 1) * 512],
                                  in_=slots[sl][:, :]),
                      waits=[t_wq], sem=dSPL, inc=16)
            ctx["wqs_free"][sl] = sp
            return sp

        def dequant_gen(w_in, w_bf, IC, OC, ctx, spill=None):
            tiles = [(ic, oc) for oc in range(OC // 512) for ic in range(IC)]
            stage = ctx["bufs"][0]
            a_bf = ctx["bufs"][1]
            for i, (ic, oc) in enumerate(tiles):
                sl = i % 2
                ld = P.op("sync", lambda e, sl=sl, ic=ic, oc=oc, w_in=w_in:
                          e.dma_start(out=stage[sl][:, :],
                                      in_=w_in[ic * 128:(ic + 1) * 128,
                                               oc * 512:(oc + 1) * 512]),
                          waits=[ctx["stage_free"][sl]], sem=dST[sl], inc=16)
                t_abs = P.op("scalar", lambda e, sl=sl:
                             e.activation(a_bf[sl][:, :], stage[sl][:, :],
                                          AF.Abs),
                             waits=[ld, ctx["abs_free"][sl]], sem=sSC)
                yield self_tile(i, ic, oc, sl, ld, t_abs, ctx, w_bf, spill)

        def dequant2(w_in, w_bf, IC, OC, ctx, spill=None):
            last = None
            for last in dequant_gen(w_in, w_bf, IC, OC, ctx, spill):
                pass
            return last

        with contextlib.ExitStack() as sa:
            ea = sa.enter_context
            wq_bf = ea(nc.sbuf_tensor("wq_bf", [128, 16, 1024], BF16))
            wk_bf = ea(nc.sbuf_tensor("wk_bf", [128, 16, 1024], BF16))
            wv_bf = ea(nc.sbuf_tensor("wv_bf", [128, 16, 1024], BF16))
            stage = [ea(nc.sbuf_tensor(f"stage{i}", [128, 512], F32)) for i in range(2)]
            a_bf = [ea(nc.sbuf_tensor(f"a_bf{i}", [128, 512], BF16)) for i in range(2)]
            s_scr = [ea(nc.sbuf_tensor(f"s_scr{i}", [128, 512], F32)) for i in range(2)]
            g1_scr = ea(nc.sbuf_tensor("g1_scr", [128, 512], F32))
            g2_scr = ea(nc.sbuf_tensor("g2_scr", [128, 512], F32))
            thr_rows = [ea(nc.sbuf_tensor(f"thr_r{i}", [1, 512], F32)) for i in range(2)]
            x_bf = [ea(nc.sbuf_tensor(f"x_bf{i}", [128, 16, B1], BF16)) for i in range(4)]
            xh = [ea(nc.sbuf_tensor(f"xh{i}", [128, 16, B1], BF16)) for i in range(2)]
            sq_scr = [ea(nc.sbuf_tensor(f"sq_scr{i}", [128, B1], BF16)) for i in range(2)]
            prj_bf = [ea(nc.sbuf_tensor(f"prj_bf{i}", [128, B1], BF16)) for i in range(2)]
            swp_bf = [ea(nc.sbuf_tensor(f"swp_bf{i}", [128, B1], BF16)) for i in range(2)]
            wqs = [ea(nc.sbuf_tensor(f"wqs{i}", [128, 512], BF16)) for i in range(2)]
            thr_sb = [ea(nc.sbuf_tensor(f"thr_sb{i}", [128, 512], F32)) for i in range(2)]
            c2_bf = ea(nc.sbuf_tensor("c2_bf", [128, T], BF16))
            s2t_bf = ea(nc.sbuf_tensor("s2t_bf", [128, T], BF16))
            rows_r = ea(nc.sbuf_tensor("rows_r", [1, T], BF16))
            rowf1 = ea(nc.sbuf_tensor("rowf1", [96, B1], F32))
            rowf2 = ea(nc.sbuf_tensor("rowf2", [64, B1], F32))
            rowbf = ea(nc.sbuf_tensor("rowbf", [64, B1], BF16))
            m_sb = ea(nc.sbuf_tensor("m_sb", [128, B1], BF16))
            r_sb = ea(nc.sbuf_tensor("r_sb", [128, B1], F32))

            p_swp = [ea(nc.psum_tensor(f"p_swp{i}", [128, B1], F32)) for i in range(3)]
            p_stats = [ea(nc.psum_tensor(f"p_stats{i}", [1, 512], F32)) for i in range(2)]
            p_dq = ea(nc.psum_tensor("p_dq", [1, 512], F32))
            p_thr = [ea(nc.psum_tensor(f"p_thr{i}", [128, 512], F32)) for i in range(2)]

            ln_t0 = [0]
            r_tok = [None]
            rows_r_free = None
            mrow = rowf1[0:1, :]
            msqr = rowf1[32:33, :]
            varr = rowf1[64:65, :]
            rstdr = rowf2[0:1, :]
            mrr = rowf2[32:33, :]
            rstdr_bf = rowbf[0:1, :]
            mrr_bf = rowbf[32:33, :]

            P.op("gpsimd", lambda e: e.dma_start(out=c2_bf[:, :],
                                                 in_=c2s2[0:128, :]),
                 sem=dCON, inc=16)
            P.op("gpsimd", lambda e: e.dma_start(out=s2t_bf[:, :],
                                                 in_=c2s2[128:256, :]),
                 sem=dCON, inc=16)
            rope_tok = (dCON, dCON.n)

            dq_ctx = {"bufs": (stage, a_bf, s_scr, g1_scr, g2_scr, thr_rows,
                               p_dq, p_thr),
                      "stage_free": [None, None], "abs_free": [None, None],
                      "dqs_free": None, "thr_free": [None, None],
                      "wqs_free": [None, None],
                      "thrsb_free": [None, None], "sscr_free": [None, None]}
            xb_free = [None, None, None, None]
            xld_toks = {}
            for i in range(4):
                xld_toks[i] = P.op("gpsimd", lambda e, sl=i, t0=i * B1:
                                   e.dma_start(out=x_bf[sl][:, :, :],
                                               in_=xT[:, t0:t0 + B1].rearrange(
                                                   "(c p) t -> p c t", p=128)),
                                   waits=[], sem=dXB[i], inc=16)

            wg_groups = [[0, 2, 4, 6], [1, 3, 5, 7]]
            gather_toks = {}
            for wi, (nm, w_in, ICg) in enumerate(
                    [("q", wqT, 4), ("k", wkT, 4), ("v", wvT, 4),
                     ("o", woT, 2)]):
                OCg = 1024 if nm != "o" else 2048
                dequant2(w_in[:, :], None, ICg, OCg, dq_ctx,
                         spill=(wqs, cw_in[nm]))
                gather_toks[nm] = P.op(
                    "gpsimd", lambda e, nm=nm:
                    e.collective_compute("AllGather", ALU.bypass,
                                         replica_groups=wg_groups,
                                         ins=[cw_in[nm][:, :]],
                                         outs=[cw_out[nm][:, :]]),
                    waits=[(dSPL, dSPL.n)], sem=sCW[wi])
            ld_q = P.op("sync", lambda e: e.dma_start(
                out=wq_bf[:, :, :],
                in_=cw_out["q"][:, :].rearrange("(c p) n -> p c n", p=128)),
                waits=[gather_toks["q"]], sem=dWG[0], inc=16)
            ld_k = P.op("sync", lambda e: e.dma_start(
                out=wk_bf[:, :, :],
                in_=cw_out["k"][:, :].rearrange("(c p) n -> p c n", p=128)),
                waits=[gather_toks["k"]], sem=dWG[1], inc=16)
            ld_v = P.op("sync", lambda e: e.dma_start(
                out=wv_bf[:, :, :],
                in_=cw_out["v"][:, :].rearrange("(c p) n -> p c n", p=128)),
                waits=[gather_toks["v"]], sem=dWG[2], inc=16)
            dq_wq = ld_q
            dq_kv_tok = [ld_k, ld_v]

            xh_free = [None, None]
            psw_free = [None, None, None]
            prj_free = [None, None]
            swb_free = [None, None]
            stats_free = [None, None]
            pv_free = None
            sq_free = [None, None]
            dqt = dq_ctx

            def ln_stats(par, xb_sl, xld):
                t = None
                for ic in range(16):
                    t = P.op("tensor", lambda e, ic=ic, sl=xb_sl, par=par:
                             e.matmul(p_stats[par][0:1, 0:B1], onescol_bf[:, :],
                                      x_bf[sl][:, ic, :], start=(ic == 0),
                                      stop=(ic == 15), skip_group_check=True),
                             waits=[xld, con_tok,
                                    stats_free[par] if ic == 0 else None],
                             sem=sPE)
                mean_mm = t
                for ic in range(16):
                    sl2 = ic % 2
                    tsq = P.op("vector", lambda e, ic=ic, sl=xb_sl, sl2=sl2:
                               e.tensor_tensor(sq_scr[sl2][:, :],
                                               x_bf[sl][:, ic, :],
                                               x_bf[sl][:, ic, :], op=ALU.mult),
                               waits=[xld, sq_free[sl2]], sem=sV)
                    t = P.op("tensor", lambda e, ic=ic, sl2=sl2, par=par:
                             e.matmul(p_stats[par][0:1, B1:2 * B1],
                                      onescol_bf[:, :],
                                      sq_scr[sl2][:, :], start=(ic == 0),
                                      stop=(ic == 15), skip_group_check=True),
                             waits=[tsq], sem=sPE)
                    sq_free[sl2] = (sPE, sPE.n)
                return (mean_mm, t)

            def ln_norm(par, xb_sl, xh_sl, mean_mm, sq_mm):
                t = P.op("vector", lambda e, par=par:
                         e.tensor_scalar(mrow, p_stats[par][0:1, 0:B1],
                                         1.0 / 2048.0, None, op0=ALU.mult),
                         waits=[mean_mm, sq_mm], sem=sV)
                t = P.op("vector", lambda e:
                         e.tensor_tensor(msqr, mrow, mrow, op=ALU.mult),
                         waits=[t], sem=sV)
                t = P.op("vector", lambda e, par=par:
                         e.scalar_tensor_tensor(varr, p_stats[par][0:1, B1:2 * B1],
                                                1.0 / 2048.0, msqr,
                                                op0=ALU.mult, op1=ALU.subtract),
                         waits=[t], sem=sV)
                stats_free[par] = (sV, sV.n)
                t = P.op("vector", lambda e:
                         e.tensor_scalar(varr, varr, 1e-5, None, op0=ALU.add),
                         waits=[t], sem=sV)
                t = P.op("scalar", lambda e:
                         e.activation(varr, varr, AF.Ln),
                         waits=[t], sem=sSC)
                t = P.op("scalar", lambda e:
                         e.activation(rstdr, varr, AF.Exp, scale=-0.5),
                         waits=[t], sem=sSC)
                t1 = P.op("vector", lambda e, t0=ln_t0[0]:
                          e.tensor_copy(rows_r[0:1, t0:t0 + B1], rstdr),
                          waits=[t, rows_r_free], sem=sV)

                t2 = P.op("vector", lambda e: e.tensor_copy(mrr_bf, mrow),
                          waits=[t], sem=sV)
                b1 = P.op("tensor", lambda e:
                          e.matmul(p_thr[0][:, 0:B1], onesrow_bf[32:33, :],
                                   mrr_bf,
                                   start=True, stop=True, skip_group_check=True),
                          waits=[t2, dqt["thr_free"][0]], sem=sPE)
                b2 = P.op("tensor", lambda e, t0=ln_t0[0]:
                          e.matmul(p_thr[1][:, 0:B1], onesrow_bf[0:1, :],
                                   rows_r[0:1, t0:t0 + B1],
                                   start=True, stop=True, skip_group_check=True),
                          waits=[t1, dqt["thr_free"][1]], sem=sPE)
                c1 = P.op("vector", lambda e: e.tensor_copy(m_sb[:, :],
                                                            p_thr[0][:, 0:B1]),
                          waits=[b1], sem=sV)
                dqt["thr_free"][0] = (sV, sV.n)
                c2t = P.op("vector", lambda e: e.tensor_copy(r_sb[:, :],
                                                             p_thr[1][:, 0:B1]),
                           waits=[b2], sem=sV)
                dqt["thr_free"][1] = (sV, sV.n)
                r_tok[0] = c2t
                last = None
                for ic in range(16):
                    eng = "gpsimd" if ic % 3 == 2 else "vector"
                    sem_ = sPL if ic % 3 == 2 else sV
                    last = P.op(eng, lambda e, ic=ic, sl=xb_sl, so=xh_sl:
                                e.tensor_tensor(xh[so][:, ic, :],
                                                x_bf[sl][:, ic, :],
                                                m_sb[:, :], op=ALU.subtract),
                                waits=[c1, xh_free[xh_sl] if ic == 0 else None],
                                sem=sem_)
                last = [(sV, sV.n), (sPL, sPL.n)]
                return last

            def proj_rope(w_bf, xh_sl, xh_tok, t0, dst_sp, do_rope):
                for hc in range(8):
                    pp = hc % 2
                    pb = hc % 3
                    mt = None
                    for ic in range(16):
                        w = [xh_tok]
                        if ic == 0:
                            w.append(psw_free[pb])
                        mt = P.op("tensor", lambda e, hc=hc, ic=ic, sl=xh_sl,
                                  pp=pb:
                                  e.matmul(p_swp[pp][:, :],
                                           w_bf[:, ic, hc * 128:(hc + 1) * 128],
                                           xh[sl][:, ic, :], start=(ic == 0),
                                           stop=(ic == 15),
                                           skip_group_check=True),
                                  waits=w, sem=sPE)
                    ev = P.op("vector", lambda e, pp=pp, pb=pb:
                              e.tensor_tensor(prj_bf[pp][:, :], p_swp[pb][:, :],
                                              r_sb[:, :], op=ALU.mult),
                              waits=[mt, prj_free[pp], r_tok[0]], sem=sV)
                    if not do_rope:
                        psw_free[pb] = (sV, sV.n)
                        st = P.op("scalar", lambda e, hc=hc, t0=t0, pp=pp,
                                  dst_sp=dst_sp:
                                  e.dma_start(out=dst_sp[hc, :, t0:t0 + B1],
                                              in_=prj_bf[pp][:, :]),
                                  waits=[ev], sem=dSPL, inc=16)
                        prj_free[pp] = [(sV, sV.n), (dSPL, dSPL.n)]
                        continue
                    sw = P.op("tensor", lambda e, pb=pb, pp=pp:
                              e.matmul(p_swp[pb][:, :],
                                       perm_bf[:, :], prj_bf[pp][:, :],
                                       start=True, stop=True,
                                       skip_group_check=True),
                              waits=[ev, con_tok], sem=sPE)
                    sv = P.op("vector", lambda e, pp=pp, pb=pb:
                              e.tensor_copy(swp_bf[pp][:, :],
                                            p_swp[pb][:, :]),
                              waits=[sw, swb_free[pp]], sem=sV)
                    psw_free[pb] = (sV, sV.n)
                    t1 = P.op("gpsimd", lambda e, pp=pp, t0=t0:
                              e.tensor_tensor(prj_bf[pp][:, :], prj_bf[pp][:, :],
                                              c2_bf[:, t0:t0 + B1], op=ALU.mult),
                              waits=[sw, rope_tok], sem=sPL)
                    t2 = P.op("gpsimd", lambda e, pp=pp, t0=t0:
                              e.tensor_tensor(swp_bf[pp][:, :], swp_bf[pp][:, :],
                                              s2t_bf[:, t0:t0 + B1], op=ALU.mult),
                              waits=[sv], sem=sPL)
                    t3 = P.op("gpsimd", lambda e, pp=pp:
                              e.tensor_tensor(swp_bf[pp][:, :], prj_bf[pp][:, :],
                                              swp_bf[pp][:, :], op=ALU.add),
                              waits=[t1, t2], sem=sPL)
                    prj_free[pp] = (sPL, sPL.n)
                    st = P.op("scalar", lambda e, hc=hc, t0=t0, pp=pp,
                              dst_sp=dst_sp:
                              e.dma_start(out=dst_sp[hc, :, t0:t0 + B1],
                                          in_=swp_bf[pp][:, :]),
                              waits=[t3], sem=dSPL, inc=16)
                    swb_free[pp] = (dSPL, dSPL.n)

            def xload(b):
                sl = b % 4
                t0 = b * B1
                return P.op("gpsimd", lambda e, sl=sl, t0=t0:
                            e.dma_start(out=x_bf[sl][:, :, :],
                                        in_=xT[:, t0:t0 + B1].rearrange(
                                            "(c p) t -> p c t", p=128)),
                            waits=[xb_free[sl]], sem=dXB[sl], inc=16)

            for b in range(NB1 + 1):
                if b < NB1:
                    stats_toks_b = ln_stats(b % 2, b % 4, xld_toks.pop(b))
                if b > 0:
                    pend = b - 1
                    pxl = pend % 4
                    psl = pend % 2
                    pt0 = pend * B1
                    mm_m, mm_s = pend_stats
                    ln_t0[0] = pt0
                    xht = ln_norm(pend % 2, pxl, psl, mm_m, mm_s)
                    xb_free[pxl] = [(sV, sV.n), (sPL, sPL.n), (sPE, sPE.n)]
                    P.op("scalar", lambda e, psl=psl, pt0=pt0:
                         e.dma_start(out=xh1_sp[:, :, pt0:pt0 + B1].rearrange(
                             "c p t -> p c t"), in_=xh[psl][:, :, :]),
                         waits=[xht], sem=dSPL, inc=16)
                    if pend + 4 < NB1:
                        xld_toks[pend + 4] = xload(pend + 4)
                    if pend == 0:
                        xht = [xht, dq_wq]
                    proj_rope(wq_bf, psl, xht, pt0, q_sp, True)
                    xh_free[psl] = [(sPE, sPE.n), (dSPL, dSPL.n)]
                if b < NB1:
                    pend_stats = stats_toks_b

            for b in range(NB1):
                t0 = b * B1
                sl = b % 2
                xrd = P.op("sync", lambda e, sl=sl, t0=t0:
                           e.dma_start(out=xh[sl][:, :, :],
                                       in_=xh1_sp[:, :, t0:t0 + B1].rearrange(
                                           "c p t -> p c t")),
                           waits=[xh_free[sl],
                                  (dSPL, dSPL.n) if b == 0 else None,
                                  dq_kv_tok if b == 0 else None],
                           sem=dXB[sl], inc=16)
                xh_tok = (dXB[sl], dXB[sl].n)
                b2 = P.op("tensor", lambda e, t0=t0:
                          e.matmul(p_thr[1][:, 0:B1], onesrow_bf[0:1, :],
                                   rows_r[0:1, t0:t0 + B1],
                                   start=True, stop=True, skip_group_check=True),
                          waits=[dqt["thr_free"][1],
                                 (dSPL, dSPL.n) if b == 0 else None], sem=sPE)
                c2t = P.op("vector", lambda e: e.tensor_copy(r_sb[:, :],
                                                             p_thr[1][:, 0:B1]),
                           waits=[b2, r_tok[0]], sem=sV)
                dqt["thr_free"][1] = (sV, sV.n)
                r_tok[0] = c2t
                proj_rope(wk_bf, sl, xh_tok, t0, k_sp, True)
                proj_rope(wv_bf, sl, xh_tok, t0, v_sp, False)
                xh_free[sl] = [(sPE, sPE.n)]

            p1 = dict(pe=(sPE, sPE.n), v=(sV, sV.n), sc=(sSC, sSC.n),
                      spl=(dSPL, dSPL.n))
            if upto == "p1":
                nspl = dSPL.n
                P.op("sync", lambda e: e.wait_ge(dSPL.h, nspl),
                     waits=[p1["pe"], p1["v"], p1["sc"]])
            P.flush(blk)
        if upto == "p1":
            return nc

        with contextlib.ExitStack() as sb:
            eb = sb.enter_context
            wo_bf = eb(nc.sbuf_tensor("wo_bf", [128, 8, 2048], BF16))
            masks = eb(nc.sbuf_tensor("masks_s", [128, 4, 512], BF16))
            qblk = [eb(nc.sbuf_tensor(f"qblk{i}", [128, 8, 512], BF16)) for i in range(2)]
            kslab = [eb(nc.sbuf_tensor(f"kslab{i}", [128, T], BF16)) for i in range(3)]
            vslab = [eb(nc.sbuf_tensor(f"vslab{i}", [128, 32, 128], BF16)) for i in range(3)]
            ptb = [eb(nc.sbuf_tensor(f"ptb{i}", [128, 1024], BF16)) for i in range(2)]
            ptsum = [eb(nc.sbuf_tensor(f"ptsum{i}", [128, 512], BF16)) for i in range(2)]
            yblk = eb(nc.sbuf_tensor("yblk", [128, 8, 512], BF16))
            osb = eb(nc.sbuf_tensor("osb", [128, 16, 512], F32))
            rb_sb = eb(nc.sbuf_tensor("rb_sb", [128, 512], F32))
            row2 = eb(nc.sbuf_tensor("row2", [64, 512], F32))
            p_sc = [eb(nc.psum_tensor(f"p_sc{i}", [128, 1024], F32)) for i in range(2)]
            p_av = [eb(nc.psum_tensor(f"p_av{i}", [128, 512], F32)) for i in range(2)]
            p_sum = eb(nc.psum_tensor("p_sum", [1, 512], F32))
            p_rbo = eb(nc.psum_tensor("p_rbo", [128, 512], F32))
            sum_row = row2[0:1, :]
            r_row = row2[32:33, :]

            p1toks = [p1["pe"], p1["v"], p1["sc"], p1["spl"]]
            for eng in ENGINES:
                P.fence(eng, p1toks)

            P.op("sync", lambda e: e.dma_start(
                out=masks[:, :, :], in_=masksI[:, :, :].rearrange("m p t -> p m t")),
                 sem=dCON, inc=16)
            mask_tok = (dCON, dCON.n)

            P.op("sync", lambda e: e.dma_start(
                out=wo_bf[:, :, :],
                in_=cw_out["o"][:, :].rearrange("(c p) n -> p c n", p=128)),
                 waits=[gather_toks["o"]], sem=dST[0], inc=16)
            dq_wo = (dST[0], dST[0].n)

            sc_free = [None, None]
            av_free = [None, None]
            sum_free = None
            rbo_free = None
            qb_free = [None, None]
            kv_free = [None, None, None]
            ptb_free = [None, None]
            ptsum_free = [None, None]
            yblk_free = None
            osb_free = None

            def mk_score(col, ktile, psl, ksl, qsl, h):
                def f(e):
                    return e.matmul(p_sc[psl][:, col * 512:(col + 1) * 512],
                                    kslab[ksl][:, ktile * 128:(ktile + 1) * 128],
                                    qblk[qsl][:, h, :],
                                    start=True, stop=True, skip_group_check=True)
                return f

            for qb in range(NQB):
                q0 = qb * 512
                KT = (qb + 1) * 4
                qsl = qb % 2
                P.op("sync", lambda e, qsl=qsl, q0=q0:
                     e.dma_start(out=qblk[qsl][:, :, :],
                                 in_=q_sp[:, :, q0:q0 + 512].rearrange(
                                     "h p t -> p h t")),
                     waits=[qb_free[qsl], p1["spl"] if qb == 0 else None],
                     sem=dQB[qsl], inc=16)
                qtok = (dQB[qsl], dQB[qsl].n)
                for h in range(8):
                    it = qb * 8 + h
                    ksl = it % 3
                    KH = KT // 2
                    kvtok_h = None
                    for half in range(2):
                        P.op("sync", lambda e, ksl=ksl, h=h, KT=KT, half=half,
                             KH=KH:
                             e.dma_start(out=kslab[ksl][:, half * KH * 128:
                                                        (KT if half else KH)
                                                        * 128],
                                         in_=k_sp[h, :, half * KH * 128:
                                                  (KT if half else KH) * 128]),
                             waits=[kv_free[ksl]
                                    if half == 0 else None,
                                    p1["spl"] if it == 0 else None],
                             sem=dKV[ksl], inc=16)
                        P.op("sync", lambda e, ksl=ksl, h=h, KT=KT, half=half,
                             KH=KH:
                             e.dma_start(out=vslab[ksl][:, half * KH:
                                                        (KT if half else KH),
                                                        :],
                                         in_=v_sp[h, :, half * KH * 128:
                                                  (KT if half else KH) * 128],
                                         transpose=True),
                             waits=[kvtok_h], sem=dKV[ksl], inc=16)
                        if half == 0:
                            kvtok_h = (dKV[ksl], dKV[ksl].n)
                    kvtok = (dKV[ksl], dKV[ksl].n)
                    hpar = h % 2
                    avmm = None
                    summ = None
                    mt_pend = None
                    for kp in range(KT // 2 + 1):
                        if kp < KT // 2:
                            psl = kp % 2
                            ktok = kvtok_h if 2 * kp + 1 < KH else kvtok
                            P.op("tensor", mk_score(0, 2 * kp, psl, ksl, qsl, h),
                                 waits=[ktok, qtok, sc_free[psl]], sem=sPE)
                            mm2 = P.op("tensor", mk_score(1, 2 * kp + 1, psl,
                                                          ksl, qsl, h),
                                       waits=[], sem=sPE)
                            ex = P.op("scalar", lambda e, psl=psl:
                                      e.activation(ptb[psl][:, :],
                                                   p_sc[psl][:, :],
                                                   AF.Exp, scale=ATT_SCALE),
                                      waits=[mm2, ptb_free[psl]], sem=sSC)
                            sc_free[psl] = (sSC, sSC.n)
                            mtoks = [ex, ex]
                            for j, kt in enumerate((2 * kp, 2 * kp + 1)):
                                di = kt - (KT - 4)
                                if 0 <= di < 4:
                                    mtoks[j] = P.op(
                                        "vector",
                                        lambda e, psl=psl, j=j, di=di:
                                        e.tensor_tensor(
                                            ptb[psl][:, j * 512:(j + 1) * 512],
                                            ptb[psl][:, j * 512:(j + 1) * 512],
                                            masks[:, di, :], op=ALU.mult),
                                        waits=[ex, mask_tok], sem=sV)
                            mt_cur = (kp, mtoks)
                        else:
                            mt_cur = None
                        if mt_pend is not None:
                            kp0, mtoks0 = mt_pend
                            psl0 = kp0 % 2
                            padd = P.op("vector",
                                        lambda e, psl=psl0:
                                        e.tensor_tensor(
                                            ptsum[psl][:, :],
                                            ptb[psl][:, 0:512],
                                            ptb[psl][:, 512:1024],
                                            op=ALU.add),
                                        waits=[mtoks0[0], mtoks0[1],
                                               ptsum_free[psl0]], sem=sV)
                            for j, kt in enumerate((2 * kp0, 2 * kp0 + 1)):
                                avmm = P.op(
                                    "tensor",
                                    lambda e, psl=psl0, j=j, kt=kt, ksl=ksl,
                                    KT=KT, hpar=hpar:
                                    e.matmul(p_av[hpar][:, :],
                                             vslab[ksl][:, kt, :],
                                             ptb[psl][:, j * 512:(j + 1) * 512],
                                             start=(kt == 0),
                                             stop=(kt == KT - 1),
                                             skip_group_check=True),
                                    waits=[mtoks0[j],
                                           av_free[hpar] if kt == 0 else None],
                                    sem=sPE)
                            summ = P.op(
                                "tensor",
                                lambda e, psl=psl0, kp0=kp0, KT=KT:
                                e.matmul(p_sum[:, :], onescol_bf[:, :],
                                         ptsum[psl][:, :],
                                         start=(kp0 == 0),
                                         stop=(kp0 == KT // 2 - 1),
                                         skip_group_check=True),
                                waits=[padd,
                                       sum_free if kp0 == 0 else None],
                                sem=sPE)
                            ptb_free[psl0] = (sPE, sPE.n)
                            ptsum_free[psl0] = (sPE, sPE.n)
                        mt_pend = mt_cur
                    kv_free[ksl] = (sPE, sPE.n)
                    t = P.op("scalar", lambda e: e.activation(r_row,
                                                              p_sum[0:1, :],
                                                              AF.Ln),
                             waits=[summ], sem=sSC)
                    sum_free = (sSC, sSC.n)
                    t = P.op("scalar", lambda e: e.activation(r_row, r_row,
                                                              AF.Exp,
                                                              scale=-1.0),
                             waits=[t], sem=sSC)
                    rb = P.op("tensor", lambda e:
                              e.matmul(p_rbo[:, :], onesrow_f[32:33, :], r_row,
                                       start=True, stop=True,
                                       skip_group_check=True),
                              waits=[t, rbo_free], sem=sPE)
                    t = P.op("vector", lambda e: e.tensor_copy(rb_sb[:, :],
                                                               p_rbo[:, :]),
                             waits=[rb], sem=sV)
                    rbo_free = (sV, sV.n)
                    P.op("vector", lambda e, h=h, hpar=hpar:
                         e.tensor_tensor(yblk[:, h, :], p_av[hpar][:, :],
                                         rb_sb[:, :], op=ALU.mult),
                         waits=[avmm, yblk_free if h == 0 else None], sem=sV)
                    av_free[hpar] = (sV, sV.n)
                y_tok = (sV, sV.n)
                ev_last = None
                for oc in range(16):
                    r3 = oc % 3
                    pt = [p_av[0], p_av[1], p_rbo][r3]
                    mt = None
                    for ic in range(8):
                        w = [y_tok] if ic == 0 else []
                        if qb == 0 and oc == 0 and ic == 0:
                            w.append(dq_wo)
                        if ic == 0:
                            w.append(rbo_free if r3 == 2 else av_free[r3])
                        mt = P.op("tensor", lambda e, oc=oc, ic=ic, pt=pt:
                                  e.matmul(pt[:, :],
                                           wo_bf[:, ic, oc * 128:(oc + 1) * 128],
                                           yblk[:, ic, :], start=(ic == 0),
                                           stop=(ic == 7),
                                           skip_group_check=True),
                                  waits=w, sem=sPE)
                    ev = P.op("vector", lambda e, oc=oc, pt=pt:
                              e.tensor_copy(osb[:, oc, :], pt[:, :]),
                              waits=[mt, osb_free if oc == 0 else None],
                              sem=sV)
                    if r3 == 2:
                        rbo_free = (sV, sV.n)
                    else:
                        av_free[r3] = (sV, sV.n)
                    ev_last = ev
                yblk_free = (sPE, sPE.n)
                qb_free[qsl] = (sPE, sPE.n)
                c = qb % 4
                hf = qb // 4
                P.op("gpsimd", lambda e, c=c, hf=hf:
                     e.dma_start(out=o_part[c, hf * 2048:(hf + 1) * 2048, :]
                                 .rearrange("(o p) t -> p o t", p=128),
                                 in_=osb[:, :, :]),
                     waits=[ev_last], sem=dOP[c], inc=16)
                osb_free = (dOP[c], dOP[c].n)
                if hf == 1:
                    P.op("gpsimd", lambda e, c=c:
                         e.collective_compute(
                             "ReduceScatter", ALU.add,
                             replica_groups=[[0, 1], [2, 3], [4, 5], [6, 7]],
                             ins=[o_part[c, :, :]], outs=[or_red[c, :, :]]),
                         waits=[(dOP[c], dOP[c].n)], sem=sCC[c])

            p2 = dict(pe=(sPE, sPE.n), v=(sV, sV.n), sc=(sSC, sSC.n))
            if upto == "p2":
                if or_dbg is not None:
                    for ch in range(4):
                        P.op("sync", lambda e, ch=ch:
                             e.dma_start(out=or_dbg[ch], in_=or_red[ch]),
                             waits=[(sCC[ch], 1)], sem=dOUT, inc=16)
                P.op("sync", lambda e: e.wait_ge(dOUT.h, 64),
                     waits=[p2["pe"], p2["v"], p2["sc"], (sCC[3], 1)])
            P.flush(blk)
        if upto == "p2":
            return nc

        GRPS = [[0, 1], [2, 3], [4, 5], [6, 7], [8]]
        with contextlib.ExitStack() as s3:
            e3 = s3.enter_context
            x1 = e3(nc.sbuf_tensor("x1", [128, 16, 1024], F32))
            x2bf = e3(nc.sbuf_tensor("x2bf", [128, 16, 1024], BF16))
            a3 = [e3(nc.sbuf_tensor(f"a3_{i}", [128, 8, 1024], BF16)) for i in range(2)]
            w13b = [e3(nc.sbuf_tensor(f"w13b{i}", [128, 16, 256], BF16)) for i in range(2)]
            w2b = [e3(nc.sbuf_tensor(f"w2b{i}", [128, 8, 256], BF16)) for i in range(2)]
            comb_cur = e3(nc.sbuf_tensor("combc", [128, 1024], BF16))
            comb_row = e3(nc.sbuf_tensor("combr", [1, 1024], BF16))
            s1f = [e3(nc.sbuf_tensor(f"s1f{i}", [128, 512], F32)) for i in range(2)]
            sq3 = e3(nc.sbuf_tensor("sq3", [128, 1024], BF16))
            xb3 = e3(nc.sbuf_tensor("xb3", [128, 1024], F32))
            ob3 = e3(nc.sbuf_tensor("ob3", [128, 1024], F32))
            m3_sb = e3(nc.sbuf_tensor("m3_sb", [128, 1024], BF16))
            r3_sb = e3(nc.sbuf_tensor("r3_sb", [128, 1024], BF16))
            g_sb = e3(nc.sbuf_tensor("g_sb", [8, 1024], F32))
            gt_sb = e3(nc.sbuf_tensor("gt_sb", [128, 64], F32))
            rtm = e3(nc.sbuf_tensor("rtm", [128, 8], F32))
            comb_tm = e3(nc.sbuf_tensor("comb_tm", [128, 8, 8], F32))
            comb_fm = e3(nc.sbuf_tensor("comb_fm", [8, 1024], BF16))
            gs_scr = [e3(nc.sbuf_tensor(f"gs{i}", [128, 8], F32)) for i in range(4)]
            gm_scr = [e3(nc.sbuf_tensor(f"gm{i}", [128, 1], F32)) for i in range(3)]
            gsum_r = e3(nc.sbuf_tensor("gsum_r", [1, 8], F32))
            gw_f = e3(nc.sbuf_tensor("gw_f", [128, 16, 8], F32))
            rp1 = e3(nc.sbuf_tensor("rp1", [96, 1024], F32))
            rp2 = e3(nc.sbuf_tensor("rp2", [64, 1024], F32))
            rpb = e3(nc.sbuf_tensor("rpb", [64, 1024], BF16))
            ph = e3(nc.psum_tensor("ph", [128, 1024], F32))
            pg = e3(nc.psum_tensor("pg", [128, 1024], F32))
            pd = [e3(nc.psum_tensor(f"pd{i}", [128, 1024], F32)) for i in range(2)]

            mean3 = rp1[0:1, :]
            msq3 = rp1[32:33, :]
            var3 = rp1[64:65, :]
            rstd3 = rp2[0:1, :]
            mr3 = rp2[32:33, :]
            r3bf = rpb[0:1, :]
            mr3bf = rpb[32:33, :]

            p2toks = [p2["pe"], p2["v"], p2["sc"]]
            for eng in ENGINES:
                P.fence(eng, p2toks)

            P.op("sync", lambda e: e.dma_start(
                out=gw_f[:, :, :],
                in_=gwT[:, :].rearrange("(c p) e -> p c e", p=128)),
                 sem=dCON, inc=16)
            gw_tok = (dCON, dCON.n)

            ph_free = None
            pg_free = None
            pd_free = [None, None]
            w13_free = [None, None]
            w2_free = [None, None]
            x3_free = [None, None]
            a3_free = [None, None]
            sq3_free = None
            s1_free = [None, None]
            crow_free = None
            out_cnt = 0

            gmm = None
            for ic in range(16):
                gmm = P.op("tensor", lambda e, ic=ic:
                           e.matmul(ph[0:1, 0:8], onescol_f[:, :],
                                    gw_f[:, ic, :],
                                    start=(ic == 0), stop=(ic == 15),
                                    skip_group_check=True),
                           waits=[gw_tok, con_tok,
                                  ph_free if ic == 0 else None], sem=sPE)
            gr = P.op("vector", lambda e:
                      e.tensor_scalar(gsum_r[:, :], ph[0:1, 0:8], -1.0, None,
                                      op0=ALU.mult),
                      waits=[gmm], sem=sV)
            ph_free = gr

            for sh in range(2):
                tl = sh * 1024
                add_toks = []
                for ic in range(16):
                    s2_ = ic % 2
                    P.op("sync", lambda e, ic=ic, tl=tl:
                         e.dma_start(out=xb3[:, :],
                                     in_=xTo[ic * 128:(ic + 1) * 128,
                                             tl:tl + 1024]),
                         waits=[x3_free[0]], sem=dX3[0], inc=16)
                    P.op("sync", lambda e, ic=ic, sh=sh:
                         e.dma_start(out=ob3[:, 0:512],
                                     in_=or_red[2 * sh,
                                                ic * 128:(ic + 1) * 128, :]),
                         waits=[(sCC[2 * sh], 1), x3_free[1]],
                         sem=dX3[1], inc=16)
                    P.op("sync", lambda e, ic=ic, sh=sh:
                         e.dma_start(out=ob3[:, 512:1024],
                                     in_=or_red[2 * sh + 1,
                                                ic * 128:(ic + 1) * 128, :]),
                         waits=[(sCC[2 * sh + 1], 1)], sem=dX3[1], inc=16)
                    P.op("vector", lambda e, ic=ic:
                         e.tensor_tensor(x1[:, ic, :], xb3[:, :], ob3[:, :],
                                         op=ALU.add),
                         waits=[(dX3[0], dX3[0].n), (dX3[1], dX3[1].n),
                                (dOUT, out_cnt * 16) if ic == 0 else None],
                         sem=sV)
                    x3_free[0] = (sV, sV.n)
                    x3_free[1] = (sV, sV.n)
                    tc = P.op("vector", lambda e, ic=ic:
                              e.tensor_copy(x2bf[:, ic, :], x1[:, ic, :]),
                              waits=[], sem=sV)
                    add_toks.append(tc)
                mean_mm = None
                for ic in range(16):
                    for cl in range(2):
                        mean_mm = P.op("tensor", lambda e, ic=ic, cl=cl:
                                       e.matmul(ph[0:1, cl * 512:(cl + 1) * 512],
                                                onescol_bf[:, :],
                                                x2bf[:, ic,
                                                     cl * 512:(cl + 1) * 512],
                                                start=(ic == 0), stop=(ic == 15),
                                                skip_group_check=True),
                                       waits=[add_toks[ic],
                                              ph_free if ic == 0 and cl == 0
                                              else None],
                                       sem=sPE)
                sq_mm = None
                for ic in range(16):
                    tsq = P.op("vector", lambda e, ic=ic:
                               e.tensor_tensor(sq3[:, :], x2bf[:, ic, :],
                                               x2bf[:, ic, :], op=ALU.mult),
                               waits=[add_toks[ic], sq3_free], sem=sV)
                    for cl in range(2):
                        sq_mm = P.op("tensor", lambda e, ic=ic, cl=cl:
                                     e.matmul(pg[0:1, cl * 512:(cl + 1) * 512],
                                              onescol_bf[:, :],
                                              sq3[:, cl * 512:(cl + 1) * 512],
                                              start=(ic == 0), stop=(ic == 15),
                                              skip_group_check=True),
                                     waits=[tsq,
                                            pg_free if ic == 0 and cl == 0
                                            else None],
                                     sem=sPE)
                    sq3_free = (sPE, sPE.n)
                t = P.op("vector", lambda e:
                         e.tensor_scalar(mean3, ph[0:1, :], 1.0 / 2048.0,
                                         None, op0=ALU.mult),
                         waits=[mean_mm], sem=sV)
                mrow_tok = t
                ph_free = (sV, sV.n)
                t = P.op("vector", lambda e:
                         e.tensor_tensor(msq3, mean3, mean3, op=ALU.mult),
                         waits=[t], sem=sV)
                t = P.op("vector", lambda e:
                         e.scalar_tensor_tensor(var3, pg[0:1, :],
                                                1.0 / 2048.0, msq3,
                                                op0=ALU.mult, op1=ALU.subtract),
                         waits=[sq_mm, t], sem=sV)
                pg_free = (sV, sV.n)
                t = P.op("vector", lambda e:
                         e.tensor_scalar(var3, var3, 1e-5, None, op0=ALU.add),
                         waits=[t], sem=sV)
                t = P.op("scalar", lambda e:
                         e.activation(var3, var3, AF.Ln),
                         waits=[t], sem=sSC)
                rrow_tok = P.op("scalar", lambda e:
                                e.activation(rstd3, var3, AF.Exp, scale=-0.5),
                                waits=[t], sem=sSC)
                t = P.op("vector", lambda e:
                         e.tensor_tensor(mr3, mean3, rstd3, op=ALU.mult),
                         waits=[rrow_tok], sem=sV)
                t1 = P.op("vector", lambda e: e.tensor_copy(r3bf, rstd3),
                          waits=[t], sem=sV)
                t2 = P.op("vector", lambda e: e.tensor_copy(mr3bf, mr3),
                          waits=[t], sem=sV)
                b1 = b2 = None
                for cl in range(2):
                    b1 = P.op("tensor", lambda e, cl=cl:
                              e.matmul(pd[0][:, cl * 512:(cl + 1) * 512],
                                       onesrow_bf[32:33, :],
                                       mr3bf[0:1, cl * 512:(cl + 1) * 512],
                                       start=True, stop=True,
                                       skip_group_check=True),
                              waits=[t2, pd_free[0] if cl == 0 else None],
                              sem=sPE)
                    b2 = P.op("tensor", lambda e, cl=cl:
                              e.matmul(pd[1][:, cl * 512:(cl + 1) * 512],
                                       onesrow_bf[0:1, :],
                                       r3bf[0:1, cl * 512:(cl + 1) * 512],
                                       start=True, stop=True,
                                       skip_group_check=True),
                              waits=[t1, pd_free[1] if cl == 0 else None],
                              sem=sPE)
                c1 = P.op("vector", lambda e: e.tensor_copy(m3_sb[:, :],
                                                            pd[0][:, :]),
                          waits=[b1], sem=sV)
                pd_free[0] = (sV, sV.n)
                c2_ = P.op("vector", lambda e: e.tensor_copy(r3_sb[:, :],
                                                             pd[1][:, :]),
                           waits=[b2], sem=sV)
                pd_free[1] = (sV, sV.n)
                xh_last = None
                for ic in range(16):
                    t = P.op("vector", lambda e, ic=ic:
                             e.tensor_tensor(sq3[:, :], x2bf[:, ic, :],
                                             r3_sb[:, :], op=ALU.mult),
                             waits=[c2_, sq3_free], sem=sV)
                    xh_last = P.op("vector", lambda e, ic=ic:
                                   e.tensor_tensor(x2bf[:, ic, :], sq3[:, :],
                                                   m3_sb[:, :], op=ALU.subtract),
                                   waits=[c1, t], sem=sV)
                    sq3_free = (sV, sV.n)
                xhat_tok = xh_last
                gmm = None
                for ic in range(16):
                    for cl in range(2):
                        gmm = P.op("tensor", lambda e, ic=ic, cl=cl:
                                   e.matmul(ph[0:8, cl * 512:(cl + 1) * 512],
                                            gw_f[:, ic, :],
                                            x1[:, ic, cl * 512:(cl + 1) * 512],
                                            start=(ic == 0), stop=False,
                                            skip_group_check=True),
                                   waits=[add_toks[ic], gr,
                                          ph_free if ic == 0 and cl == 0
                                          else None],
                                   sem=sPE)
                for cl in range(2):
                    gmm = P.op("tensor", lambda e, cl=cl:
                               e.matmul(ph[0:8, cl * 512:(cl + 1) * 512],
                                        gsum_r[:, :],
                                        mean3[0:1, cl * 512:(cl + 1) * 512],
                                        start=False, stop=True,
                                        skip_group_check=True),
                               waits=[mrow_tok], sem=sPE)
                tg = P.op("vector", lambda e: e.tensor_copy(g_sb[:, :],
                                                            ph[0:8, :]),
                          waits=[gmm], sem=sV)
                ph_free = (sV, sV.n)
                rt = None
                for s8 in range(8):
                    rt = P.op("tensor", lambda e, s8=s8:
                              e.transpose(pd[0][:, s8:s8 + 1],
                                          rstd3[0:1, s8 * 128:(s8 + 1) * 128],
                                          ident1[:, :]),
                              waits=[rrow_tok, con_tok,
                                     pd_free[0] if s8 == 0 else None], sem=sPE)
                rt = P.op("vector", lambda e: e.tensor_copy(rtm[:, :],
                                                            pd[0][:, 0:8]),
                          waits=[rt], sem=sV)
                pd_free[0] = (sV, sV.n)
                gt = None
                for s8 in range(8):
                    gt = P.op("tensor", lambda e, s8=s8:
                              e.transpose(pd[1][:, s8 * 8:(s8 + 1) * 8],
                                          g_sb[:, s8 * 128:(s8 + 1) * 128],
                                          ident8[:, :]),
                              waits=[tg, con_tok,
                                     pd_free[1] if s8 == 0 else None], sem=sPE)
                gt = P.op("vector", lambda e: e.tensor_copy(gt_sb[:, :],
                                                            pd[1][:, 0:64]),
                          waits=[gt], sem=sV)
                pd_free[1] = (sV, sV.n)
                cb_toks = []
                prev_ax = None
                for s8 in range(8):
                    g0 = P.op("vector", lambda e, s8=s8:
                              e.tensor_scalar(gs_scr[0][:, :],
                                              gt_sb[:, s8 * 8:(s8 + 1) * 8],
                                              rtm[:, s8:s8 + 1], None,
                                              op0=ALU.mult),
                              waits=[gt, rt, prev_ax], sem=sV)
                    m1 = P.op("vector", lambda e:
                              e.reduce_max(gm_scr[0][:, :], gs_scr[0][:, :],
                                           axis=AX),
                              waits=[g0], sem=sV)
                    nm = P.op("vector", lambda e:
                              e.tensor_scalar(gm_scr[1][:, :], gm_scr[0][:, :],
                                              -1.0, None, op0=ALU.mult),
                              waits=[m1], sem=sV)
                    ax = P.op("scalar", lambda e:
                              e.activation(gs_scr[1][:, :], gs_scr[0][:, :],
                                           AF.Exp, bias=gm_scr[1][:, :]),
                              waits=[nm], sem=sSC)
                    prev_ax = ax
                    sm = P.op("vector", lambda e:
                              e.reduce_sum(gm_scr[2][:, :], gs_scr[1][:, :],
                                           axis=AX),
                              waits=[ax], sem=sV)
                    eq = P.op("vector", lambda e:
                              e.tensor_scalar(gs_scr[2][:, :], gs_scr[0][:, :],
                                              gm_scr[0][:, :], None,
                                              op0=ALU.is_ge),
                              waits=[m1], sem=sV)
                    cx = P.op("vector", lambda e:
                              e.scalar_tensor_tensor(gs_scr[3][:, :],
                                                     gs_scr[2][:, :], -1e30,
                                                     gs_scr[0][:, :],
                                                     op0=ALU.mult, op1=ALU.add),
                              waits=[eq], sem=sV)
                    m2 = P.op("vector", lambda e:
                              e.reduce_max(gm_scr[1][:, :], gs_scr[3][:, :],
                                           axis=AX),
                              waits=[cx], sem=sV)
                    msk = P.op("vector", lambda e:
                               e.tensor_scalar(gs_scr[2][:, :], gs_scr[0][:, :],
                                               gm_scr[1][:, :], None,
                                               op0=ALU.is_ge),
                               waits=[m2], sem=sV)
                    rc = P.op("vector", lambda e:
                              e.reciprocal(gm_scr[2][:, :], gm_scr[2][:, :]),
                              waits=[sm], sem=sV)
                    ws_ = P.op("vector", lambda e:
                               e.tensor_tensor(gs_scr[1][:, :], gs_scr[1][:, :],
                                               gs_scr[2][:, :], op=ALU.mult),
                               waits=[msk], sem=sV)
                    cb = P.op("vector", lambda e, s8=s8:
                              e.tensor_scalar(comb_tm[:, s8, :],
                                              gs_scr[1][:, :],
                                              gm_scr[2][:, :], None,
                                              op0=ALU.mult),
                              waits=[ws_, rc], sem=sV)
                    cb_toks.append(cb)
                ct = None
                for s8 in range(8):
                    ct = P.op("tensor", lambda e, s8=s8:
                              e.transpose(pd[0][0:8, s8 * 128:(s8 + 1) * 128],
                                          comb_tm[:, s8, :], ident128[:, :]),
                              waits=[cb_toks[s8], con_tok,
                                     pd_free[0] if s8 == 0 else None], sem=sPE)
                ct = P.op("vector", lambda e: e.tensor_copy(comb_fm[:, :],
                                                            pd[0][0:8, :]),
                          waits=[ct], sem=sV)
                pd_free[0] = (sV, sV.n)
                cw = P.op("scalar", lambda e:
                          e.dma_start(out=comb_dram[:, :], in_=comb_fm[:, :]),
                          waits=[ct, (dCB, dCB.n)], sem=dCBW, inc=16)
                cw_tok = (dCBW, dCBW.n)

                for grp, exs in enumerate(GRPS):
                    for el, ex in enumerate(exs):
                        if ex < 8:
                            P.op("sync", lambda e, ex=ex:
                                 e.dma_start(out=comb_row[:, :],
                                             in_=comb_dram[ex:ex + 1, :]),
                                 waits=[cw_tok, crow_free], sem=dCB, inc=16)
                            crd_tok = (dCB, dCB.n)
                            cbm = None
                            for cl in range(2):
                                cbm = P.op("tensor", lambda e, cl=cl:
                                           e.matmul(
                                               pg[:, cl * 512:(cl + 1) * 512],
                                               onesrow_bf[0:1, :],
                                               comb_row[0:1,
                                                        cl * 512:(cl + 1) * 512],
                                               start=True, stop=True,
                                               skip_group_check=True),
                                           waits=[crd_tok,
                                                  pg_free if cl == 0 else None],
                                           sem=sPE)
                            crow_free = (sPE, sPE.n)
                            cbv = P.op("vector", lambda e:
                                       e.tensor_copy(comb_cur[:, :], pg[:, :]),
                                       waits=[cbm], sem=sV)
                            pg_free = (sV, sV.n)
                        else:
                            cbv = None
                        for i8 in range(8):
                            wsl = i8 % 2
                            P.op("gpsimd", lambda e, ex=ex, i8=i8, wsl=wsl:
                                 e.dma_start(
                                     out=w13b[wsl][:, :, 0:128],
                                     in_=w1T[ex, :, i8 * 128:(i8 + 1) * 128]
                                     .rearrange("(c p) i -> p c i", p=128)),
                                 waits=[w13_free[wsl]], sem=dW13[wsl], inc=16)
                            P.op("gpsimd", lambda e, ex=ex, i8=i8, wsl=wsl:
                                 e.dma_start(
                                     out=w13b[wsl][:, :, 128:256],
                                     in_=w3T[ex, :, i8 * 128:(i8 + 1) * 128]
                                     .rearrange("(c p) i -> p c i", p=128)),
                                 waits=[], sem=dW13[wsl], inc=16)
                            wtok = (dW13[wsl], dW13[wsl].n)
                            hm = gm = None
                            for ic in range(16):
                                for cl in range(2):
                                    hm = P.op("tensor",
                                              lambda e, ic=ic, cl=cl, wsl=wsl:
                                              e.matmul(
                                                  ph[:, cl * 512:(cl + 1) * 512],
                                                  w13b[wsl][:, ic, 0:128],
                                                  x2bf[:, ic,
                                                       cl * 512:(cl + 1) * 512],
                                                  start=(ic == 0),
                                                  stop=(ic == 15),
                                                  skip_group_check=True),
                                              waits=[wtok, xhat_tok,
                                                     ph_free
                                                     if ic == 0 and cl == 0
                                                     else None],
                                              sem=sPE)
                            for ic in range(16):
                                for cl in range(2):
                                    gm = P.op("tensor",
                                              lambda e, ic=ic, cl=cl, wsl=wsl:
                                              e.matmul(
                                                  pg[:, cl * 512:(cl + 1) * 512],
                                                  w13b[wsl][:, ic, 128:256],
                                                  x2bf[:, ic,
                                                       cl * 512:(cl + 1) * 512],
                                                  start=(ic == 0),
                                                  stop=(ic == 15),
                                                  skip_group_check=True),
                                              waits=[pg_free
                                                     if ic == 0 and cl == 0
                                                     else None],
                                              sem=sPE)
                            w13_free[wsl] = (sPE, sPE.n)
                            am = None
                            for cl in range(2):
                                si = P.op("scalar", lambda e, cl=cl:
                                          e.activation(
                                              s1f[cl][:, :],
                                              ph[:, cl * 512:(cl + 1) * 512],
                                              AF.Silu),
                                          waits=[hm, s1_free[cl]], sem=sSC)
                                am = P.op("vector",
                                          lambda e, el=el, i8=i8, cl=cl:
                                          e.tensor_tensor(
                                              a3[el][:, i8,
                                                     cl * 512:(cl + 1) * 512],
                                              s1f[cl][:, :],
                                              pg[:, cl * 512:(cl + 1) * 512],
                                              op=ALU.mult),
                                          waits=[si, gm,
                                                 a3_free[el]
                                                 if i8 == 0 and cl == 0
                                                 else None],
                                          sem=sV)
                                s1_free[cl] = (sV, sV.n)
                            ph_free = (sSC, sSC.n)
                            pg_free = (sV, sV.n)
                            if ex < 8:
                                P.op("vector", lambda e, el=el, i8=i8:
                                     e.tensor_tensor(a3[el][:, i8, :],
                                                     a3[el][:, i8, :],
                                                     comb_cur[:, :],
                                                     op=ALU.mult),
                                     waits=[cbv], sem=sV)
                    a_tok = (sV, sV.n)
                    nel = len(exs)
                    for dc in range(16):
                        dsl = dc % 2
                        for el, ex in enumerate(exs):
                            P.op("gpsimd", lambda e, ex=ex, el=el, dc=dc,
                                 dsl=dsl:
                                 e.dma_start(
                                     out=w2b[dsl][:, :, el * 128:(el + 1) * 128],
                                     in_=w2T[ex, :, dc * 128:(dc + 1) * 128]
                                     .rearrange("(c p) d -> p c d", p=128)),
                                 waits=[w2_free[dsl] if el == 0 else None],
                                 sem=dW2[dsl], inc=16)
                        w2tok = (dW2[dsl], dW2[dsl].n)
                        bm = None
                        for el in range(nel):
                            for i8 in range(8):
                                for cl in range(2):
                                    first = (el == 0 and i8 == 0 and cl == 0)
                                    bm = P.op("tensor",
                                              lambda e, el=el, i8=i8, cl=cl,
                                              dsl=dsl, nel=nel:
                                              e.matmul(
                                                  pd[dsl][:, cl * 512:
                                                          (cl + 1) * 512],
                                                  w2b[dsl][:, i8,
                                                           el * 128:
                                                           (el + 1) * 128],
                                                  a3[el][:, i8,
                                                         cl * 512:
                                                         (cl + 1) * 512],
                                                  start=(i8 == 0 and el == 0),
                                                  stop=(i8 == 7 and
                                                        el == nel - 1),
                                                  skip_group_check=True),
                                              waits=[w2tok, a_tok,
                                                     pd_free[dsl] if first
                                                     else None],
                                              sem=sPE)
                        w2_free[dsl] = (sPE, sPE.n)
                        P.op("vector", lambda e, dc=dc, dsl=dsl:
                             e.tensor_tensor(x1[:, dc, :], x1[:, dc, :],
                                             pd[dsl][:, :], op=ALU.add),
                             waits=[bm], sem=sV)
                        pd_free[dsl] = (sV, sV.n)
                    for el in range(nel):
                        a3_free[el] = (sPE, sPE.n)
                fin = (sV, sV.n)
                P.op("scalar", lambda e, tl=tl:
                     e.dma_start(out=out[:, tl:tl + 1024].rearrange(
                         "(c p) t -> p c t", p=128), in_=x1[:, :, :]),
                     waits=[fin], sem=dOUT, inc=16)
                out_cnt += 1

            final_cnt = out_cnt * 16
            P.op("sync", lambda e: e.wait_ge(dOUT.h, final_cnt), waits=[])
            P.flush(blk)

    return nc


_NC = None


def _get_nc():
    global _NC
    if _NC is None:
        _NC = build_program()
    return _NC


def _host_consts():
    f32 = np.float32
    pos = np.arange(T, dtype=np.float64)
    inv = 10000.0 ** (-np.arange(64, dtype=np.float64) / 64.0)
    fr = np.outer(pos, inv)
    cos = np.cos(fr).T
    sin = np.sin(fr).T
    c2s2 = np.empty((256, T), f32)
    c2s2[0:64] = cos
    c2s2[64:128] = cos
    c2s2[128:192] = -sin
    c2s2[192:256] = sin
    perm = np.roll(np.eye(128, dtype=f32), 64, axis=0)
    masks = np.zeros((4, 128, 512), dtype=ml_dtypes.bfloat16)
    ki = np.arange(128)[:, None]
    qj = np.arange(512)[None, :]
    for d in range(4):
        masks[d] = (ki + 128 * d <= qj).astype(ml_dtypes.bfloat16)
    return dict(
        c2s2=c2s2,
        permM=perm,
        masksI=masks,
        onescol_bf=np.ones((128, 1), ml_dtypes.bfloat16),
        onescol_f=np.ones((128, 1), f32),
        onesrow_bf=np.ones((96, 128), ml_dtypes.bfloat16),
        onesrow_f=np.ones((96, 128), f32),
        ident128=np.eye(128, dtype=f32),
        ident8=np.eye(8, dtype=f32),
        ident1=np.ones((1, 1), f32),
        ident1b=np.ones((1, 1), ml_dtypes.bfloat16),
    )


def kernel(L0_attn_x, L0_attn_w_q, L0_attn_w_k, L0_attn_w_v, L0_attn_w_o,
           L0_ffn_gate_w, L0_ffn_w1, L0_ffn_w2, L0_ffn_w3,
           L0_ffn_ws1, L0_ffn_ws2, L0_ffn_ws3):
    f32 = np.float32
    x = np.asarray(L0_attn_x, f32)
    wq = np.asarray(L0_attn_w_q, f32)
    wk = np.asarray(L0_attn_w_k, f32)
    wv = np.asarray(L0_attn_w_v, f32)
    wo = np.asarray(L0_attn_w_o, f32)
    gw = np.asarray(L0_ffn_gate_w, f32)
    w1 = np.asarray(L0_ffn_w1, f32)
    w2 = np.asarray(L0_ffn_w2, f32)
    w3 = np.asarray(L0_ffn_w3, f32)
    ws1 = np.asarray(L0_ffn_ws1, f32)
    ws2 = np.asarray(L0_ffn_ws2, f32)
    ws3 = np.asarray(L0_ffn_ws3, f32)

    in_maps = _make_in_maps(x, wq, wk, wv, wo, gw, w1, w2, w3, ws1, ws2, ws3)

    nc = _get_nc()
    res = run_bass_kernel_spmd(nc, in_maps, core_ids=list(range(8)))
    outp = np.empty((4, T, D), f32)
    for c in range(8):
        b, h = c // 2, c % 2
        outp[b, h * 2048:(h + 1) * 2048, :] = np.asarray(res.results[c]["out"]).T
    return outp


def _make_in_maps(x, wq, wk, wv, wo, gw, w1, w2, w3, ws1, ws2, ws3):
    w1T = np.ascontiguousarray(np.concatenate(
        [np.transpose(w1, (0, 2, 1)), ws1.T[None]], axis=0))
    w3T = np.ascontiguousarray(np.concatenate(
        [np.transpose(w3, (0, 2, 1)), ws3.T[None]], axis=0))
    w2T = np.ascontiguousarray(np.concatenate(
        [np.transpose(w2, (0, 2, 1)), ws2.T[None]], axis=0))
    gwT = np.ascontiguousarray(gw.T)
    consts = _host_consts()

    in_maps = []
    for c in range(8):
        b, h = c // 2, c % 2
        m = dict(consts)
        m["xT"] = np.ascontiguousarray(x[b].T)
        m["xTo"] = np.ascontiguousarray(x[b, h * 2048:(h + 1) * 2048, :].T)
        gp = c // 2
        m["wqT"] = np.ascontiguousarray(
            wq[h * 1024:(h + 1) * 1024, :].T[512 * gp:512 * (gp + 1), :])
        m["wkT"] = np.ascontiguousarray(
            wk[h * 1024:(h + 1) * 1024, :].T[512 * gp:512 * (gp + 1), :])
        m["wvT"] = np.ascontiguousarray(
            wv[h * 1024:(h + 1) * 1024, :].T[512 * gp:512 * (gp + 1), :])
        m["woT"] = np.ascontiguousarray(
            wo[:, h * 1024:(h + 1) * 1024].T[256 * gp:256 * (gp + 1), :])
        m["gwT"] = gwT
        m["w1T"] = w1T
        m["w3T"] = w3T
        m["w2T"] = w2T
        in_maps.append(m)
    return in_maps


# version 11
# speedup vs baseline: 1.0368x; 1.0278x over previous
import contextlib
import numpy as np
import ml_dtypes
import concourse.bass as bass
import concourse.mybir as mybir
from concourse.bass_utils import run_bass_kernel_spmd

F32 = mybir.dt.float32
BF16 = mybir.dt.bfloat16
AF = mybir.ActivationFunctionType
ALU = mybir.AluOpType
AX = mybir.AxisListType.X

ENGINES = ("sync", "scalar", "tensor", "vector", "gpsimd")
ATT_SCALE = float(1.0 / np.sqrt(128.0))
D = 2048
T = 4096
NB1 = 16
B1 = 256
NQB = 8


class Sem:
    __slots__ = ("h", "n")

    def __init__(self, h):
        self.h = h
        self.n = 0


def _flat(ws, out):
    for w in ws:
        if w is None:
            continue
        if isinstance(w, list):
            _flat(w, out)
        else:
            out.append(w)


class Prog:
    def __init__(self):
        self.q = {e: [] for e in ENGINES}
        self.wm = {e: {} for e in ENGINES}

    def op(self, eng, fn, waits=(), sem=None, inc=1):
        raw = []
        _flat(list(waits), raw)
        ws = []
        for s, v in raw:
            if v <= 0:
                continue
            k = id(s)
            if self.wm[eng].get(k, 0) >= v:
                continue
            self.wm[eng][k] = v
            ws.append((s, v))
        self.q[eng].append((fn, ws, sem, inc))
        if sem is None:
            return None
        sem.n += inc
        return (sem, sem.n)

    def fence(self, eng, toks):
        self.op(eng, lambda e: None, waits=toks, sem=None)

    def flush(self, blk):
        for e in ENGINES:
            ops = self.q[e]
            if not ops:
                continue
            self.q[e] = []

            def mk(ops):
                def body(eng):
                    for fn, ws, sem, inc in ops:
                        for s, v in ws:
                            eng.wait_ge(s.h, v)
                        inst = fn(eng)
                        if sem is not None:
                            inst.then_inc(sem.h, inc)
                return body
            getattr(blk, e)(mk(ops))


def build_program(dbg=False, upto="full"):
    nc = bass.Bass("TRN2", target_bir_lowering=False)

    xT = nc.dram_tensor("xT", [D, T], F32, kind="ExternalInput")
    xTo = nc.dram_tensor("xTo", [D, D], F32, kind="ExternalInput")
    wqT = nc.dram_tensor("wqT", [512, 1024], F32, kind="ExternalInput")
    wkT = nc.dram_tensor("wkT", [512, 1024], F32, kind="ExternalInput")
    wvT = nc.dram_tensor("wvT", [512, 1024], F32, kind="ExternalInput")
    woT = nc.dram_tensor("woT", [256, D], F32, kind="ExternalInput")
    gwT = nc.dram_tensor("gwT", [D, 8], F32, kind="ExternalInput")
    w1T = nc.dram_tensor("w1T", [9, D, 1024], F32, kind="ExternalInput")
    w3T = nc.dram_tensor("w3T", [9, D, 1024], F32, kind="ExternalInput")
    w2T = nc.dram_tensor("w2T", [9, 1024, D], F32, kind="ExternalInput")
    c2s2 = nc.dram_tensor("c2s2", [256, T], F32, kind="ExternalInput")
    permM = nc.dram_tensor("permM", [128, 128], F32, kind="ExternalInput")
    masksI = nc.dram_tensor("masksI", [4, 128, 512], BF16, kind="ExternalInput")
    onescol_bf_i = nc.dram_tensor("onescol_bf", [128, 1], BF16, kind="ExternalInput")
    onescol_f_i = nc.dram_tensor("onescol_f", [128, 1], F32, kind="ExternalInput")
    onesrow_bf_i = nc.dram_tensor("onesrow_bf", [96, 128], BF16, kind="ExternalInput")
    onesrow_f_i = nc.dram_tensor("onesrow_f", [96, 128], F32, kind="ExternalInput")
    ident128_i = nc.dram_tensor("ident128", [128, 128], F32, kind="ExternalInput")
    ident8_i = nc.dram_tensor("ident8", [8, 8], F32, kind="ExternalInput")
    ident1_i = nc.dram_tensor("ident1", [1, 1], F32, kind="ExternalInput")
    ident1b_i = nc.dram_tensor("ident1b", [1, 1], BF16, kind="ExternalInput")

    out = nc.dram_tensor("out", [D, D], F32, kind="ExternalOutput")

    skind = dict(kind="ExternalOutput") if dbg else {}
    xh1_sp = nc.dram_tensor("xh1_sp", [16, 128, T], BF16, **skind)
    q_sp = nc.dram_tensor("q_sp", [8, 128, T], BF16, **skind)
    k_sp = nc.dram_tensor("k_sp", [8, 128, T], BF16, **skind)
    v_sp = nc.dram_tensor("v_sp", [8, 128, T], BF16, **skind)
    o_part = nc.dram_tensor("o_part", [4, 4096, 512], F32)
    or_red = nc.dram_tensor("or_red", [4, 2048, 512], F32)
    or_dbg = (nc.dram_tensor("or_dbg", [4, 2048, 512], F32,
                             kind="ExternalOutput") if dbg else None)
    comb_dram = nc.dram_tensor("comb_dram", [8, 1024], BF16, **skind)
    cw_in = {"q": nc.dram_tensor("cwq_in", [512, 1024], BF16),
             "k": nc.dram_tensor("cwk_in", [512, 1024], BF16),
             "v": nc.dram_tensor("cwv_in", [512, 1024], BF16),
             "o": nc.dram_tensor("cwo_in", [256, D], BF16)}
    cw_out = {"q": nc.dram_tensor("cwq_out", [D, 1024], BF16),
              "k": nc.dram_tensor("cwk_out", [D, 1024], BF16),
              "v": nc.dram_tensor("cwv_out", [D, 1024], BF16),
              "o": nc.dram_tensor("cwo_out", [1024, D], BF16)}

    P = Prog()

    with contextlib.ExitStack() as glob:
        ec = glob.enter_context
        sPE = Sem(ec(nc.semaphore("sPE")))
        sPL = Sem(ec(nc.semaphore("sPL")))
        sV = Sem(ec(nc.semaphore("sV")))
        sSC = Sem(ec(nc.semaphore("sSC")))
        dCON = Sem(ec(nc.semaphore("dCON")))
        dST = [Sem(ec(nc.semaphore(f"dST{i}"))) for i in range(2)]
        dXB = [Sem(ec(nc.semaphore(f"dXB{i}"))) for i in range(4)]
        dSPL = Sem(ec(nc.semaphore("dSPL")))
        dQB = [Sem(ec(nc.semaphore(f"dQB{i}"))) for i in range(2)]
        dKV = [Sem(ec(nc.semaphore(f"dKV{i}"))) for i in range(3)]
        dOP = [Sem(ec(nc.semaphore(f"dOP{i}"))) for i in range(4)]
        sCC = [Sem(ec(nc.semaphore(f"sCC{i}"))) for i in range(4)]
        dW13 = [Sem(ec(nc.semaphore(f"dW13{i}"))) for i in range(2)]
        dW2 = [Sem(ec(nc.semaphore(f"dW2{i}"))) for i in range(2)]
        dX3 = [Sem(ec(nc.semaphore(f"dX3{i}"))) for i in range(2)]
        dCB = Sem(ec(nc.semaphore("dCB")))
        sCW = [Sem(ec(nc.semaphore(f"sCW{i}"))) for i in range(4)]
        dWG = [Sem(ec(nc.semaphore(f"dWG{i}"))) for i in range(3)]
        dCBW = Sem(ec(nc.semaphore("dCBW")))
        dOUT = Sem(ec(nc.semaphore("dOUT")))

        onescol_bf = ec(nc.sbuf_tensor("onescol_bf_s", [128, 1], BF16))
        onescol_f = ec(nc.sbuf_tensor("onescol_f_s", [128, 1], F32))
        onesrow_bf = ec(nc.sbuf_tensor("onesrow_bf_s", [96, 128], BF16))
        onesrow_f = ec(nc.sbuf_tensor("onesrow_f_s", [96, 128], F32))
        ident128 = ec(nc.sbuf_tensor("ident128_s", [128, 128], F32))
        ident8 = ec(nc.sbuf_tensor("ident8_s", [8, 8], F32))
        ident1 = ec(nc.sbuf_tensor("ident1_s", [1, 1], F32))
        ident1b = ec(nc.sbuf_tensor("ident1b_s", [1, 1], BF16))
        perm_bf = ec(nc.sbuf_tensor("perm_bf", [128, 128], BF16))

        blk = glob.enter_context(nc.Block())

        for dst, src in [
            (onescol_bf[:, :], onescol_bf_i[:, :]),
            (onescol_f[:, :], onescol_f_i[:, :]),
            (onesrow_bf[:, :], onesrow_bf_i[:, :]),
            (onesrow_f[:, :], onesrow_f_i[:, :]),
            (ident128[:, :], ident128_i[:, :]),
            (ident8[:, :], ident8_i[:, :]),
            (ident1[:, :], ident1_i[:, :]),
            (ident1b[:, :], ident1b_i[:, :]),
        ]:
            P.op("sync", lambda e, d=dst, s=src: e.dma_start(out=d, in_=s),
                 sem=dCON, inc=16)
        P.op("gpsimd", lambda e: e.dma_start(out=perm_bf[:, :], in_=permM[:, :]),
             sem=dCON, inc=16)
        con_tok = (dCON, dCON.n)

        def self_tile(i, ic, oc, sl, ld, t_abs, ctx, w_bf, spill=None):
            stage, a_bf, s_scr, g1_scr, g2_scr, thr_rows, p_dqs, p_thr = ctx["bufs"]
            t_sum = P.op("tensor", lambda e, sl=sl:
                         e.matmul(p_dqs[0:1, 0:512],
                                  onescol_bf[:, :], a_bf[sl][:, :],
                                  start=True, stop=True, skip_group_check=True),
                         waits=[t_abs, con_tok, ctx["dqs_free"]], sem=sPE)
            ctx["abs_free"][sl] = (sPE, sPE.n)
            t_thr = P.op("vector", lambda e, sl=sl:
                         e.tensor_scalar(thr_rows[sl][:, :],
                                         p_dqs[0:1, 0:512],
                                         2.0 ** -8, 5e-9,
                                         op0=ALU.mult, op1=ALU.max),
                         waits=[t_sum], sem=sV)
            ctx["dqs_free"] = (sV, sV.n)
            t_bc = P.op("tensor", lambda e, sl=sl:
                        e.matmul(p_thr[sl][:, :], onesrow_f[0:1, :],
                                 thr_rows[sl][:, :],
                                 start=True, stop=True, skip_group_check=True),
                        waits=[t_thr, ctx["thr_free"][sl]], sem=sPE)
            t_cp = P.op("vector", lambda e, sl=sl:
                        e.tensor_copy(thr_sb[sl][:, :], p_thr[sl][:, :]),
                        waits=[t_bc, ctx["thrsb_free"][sl]], sem=sV)
            ctx["thr_free"][sl] = (sV, sV.n)
            t_s1 = P.op("gpsimd", lambda e, sl=sl:
                        e.tensor_tensor(s_scr[0][:, :], stage[sl][:, :],
                                        thr_sb[sl][:, :], op=ALU.subtract),
                        waits=[t_cp, ctx["sscr_free"][0]], sem=sPL)
            t_g1 = P.op("scalar", lambda e:
                        e.activation(g1_scr[:, :], s_scr[0][:, :], AF.Sign),
                        waits=[t_s1], sem=sSC)
            t_s2 = P.op("gpsimd", lambda e, sl=sl:
                        e.tensor_tensor(s_scr[1][:, :], stage[sl][:, :],
                                        thr_sb[sl][:, :], op=ALU.add),
                        waits=[t_g1, ctx["sscr_free"][1]], sem=sPL)
            ctx["stage_free"][sl] = (sPL, sPL.n)
            ctx["thrsb_free"][sl] = (sPL, sPL.n)
            t_g2 = P.op("scalar", lambda e:
                        e.activation(g2_scr[:, :], s_scr[1][:, :], AF.Sign),
                        waits=[t_s2], sem=sSC)
            ctx["sscr_free"][1] = (sSC, sSC.n)
            t_h = P.op("vector", lambda e:
                       e.tensor_tensor(s_scr[0][:, :], g1_scr[:, :],
                                       g2_scr[:, :], op=ALU.add),
                       waits=[t_g2], sem=sV)
            ctx["sscr_free"][0] = (sV, sV.n)
            if spill is None:
                t_wq = P.op("vector", lambda e, sl=sl, ic=ic, oc=oc:
                            e.tensor_tensor(w_bf[:, ic, oc * 512:(oc + 1) * 512],
                                            s_scr[0][:, :], thr_sb[sl][:, :],
                                            op=ALU.mult),
                            waits=[t_h], sem=sV)
                ctx["thrsb_free"][sl] = [ctx["thrsb_free"][sl], (sV, sV.n)]
                ctx["sscr_free"][0] = (sV, sV.n)
                return t_wq
            slots, spill_dram = spill
            t_wq = P.op("vector", lambda e, sl=sl:
                        e.tensor_tensor(slots[sl][:, :],
                                        s_scr[0][:, :], thr_sb[sl][:, :],
                                        op=ALU.mult),
                        waits=[t_h, ctx["wqs_free"][sl]], sem=sV)
            ctx["thrsb_free"][sl] = [ctx["thrsb_free"][sl], (sV, sV.n)]
            ctx["sscr_free"][0] = (sV, sV.n)
            sp = P.op("scalar", lambda e, sl=sl, ic=ic, oc=oc:
                      e.dma_start(out=spill_dram[ic * 128:(ic + 1) * 128,
                                                 oc * 512:(oc + 1) * 512],
                                  in_=slots[sl][:, :]),
                      waits=[t_wq], sem=dSPL, inc=16)
            ctx["wqs_free"][sl] = sp
            return sp

        def dequant_gen(w_in, w_bf, IC, OC, ctx, spill=None):
            tiles = [(ic, oc) for oc in range(OC // 512) for ic in range(IC)]
            stage = ctx["bufs"][0]
            a_bf = ctx["bufs"][1]
            for i, (ic, oc) in enumerate(tiles):
                sl = i % 2
                ld = P.op("sync", lambda e, sl=sl, ic=ic, oc=oc, w_in=w_in:
                          e.dma_start(out=stage[sl][:, :],
                                      in_=w_in[ic * 128:(ic + 1) * 128,
                                               oc * 512:(oc + 1) * 512]),
                          waits=[ctx["stage_free"][sl]], sem=dST[sl], inc=16)
                t_abs = P.op("scalar", lambda e, sl=sl:
                             e.activation(a_bf[sl][:, :], stage[sl][:, :],
                                          AF.Abs),
                             waits=[ld, ctx["abs_free"][sl]], sem=sSC)
                yield self_tile(i, ic, oc, sl, ld, t_abs, ctx, w_bf, spill)

        def dequant2(w_in, w_bf, IC, OC, ctx, spill=None):
            last = None
            for last in dequant_gen(w_in, w_bf, IC, OC, ctx, spill):
                pass
            return last

        with contextlib.ExitStack() as sa:
            ea = sa.enter_context
            wq_bf = ea(nc.sbuf_tensor("wq_bf", [128, 16, 1024], BF16))
            wk_bf = ea(nc.sbuf_tensor("wk_bf", [128, 16, 1024], BF16))
            wv_bf = ea(nc.sbuf_tensor("wv_bf", [128, 16, 1024], BF16))
            stage = [ea(nc.sbuf_tensor(f"stage{i}", [128, 512], F32)) for i in range(2)]
            a_bf = [ea(nc.sbuf_tensor(f"a_bf{i}", [128, 512], BF16)) for i in range(2)]
            s_scr = [ea(nc.sbuf_tensor(f"s_scr{i}", [128, 512], F32)) for i in range(2)]
            g1_scr = ea(nc.sbuf_tensor("g1_scr", [128, 512], F32))
            g2_scr = ea(nc.sbuf_tensor("g2_scr", [128, 512], F32))
            thr_rows = [ea(nc.sbuf_tensor(f"thr_r{i}", [1, 512], F32)) for i in range(2)]
            x_bf = [ea(nc.sbuf_tensor(f"x_bf{i}", [128, 16, B1], BF16)) for i in range(4)]
            xh = [ea(nc.sbuf_tensor(f"xh{i}", [128, 16, B1], BF16)) for i in range(2)]
            sq_scr = [ea(nc.sbuf_tensor(f"sq_scr{i}", [128, B1], BF16)) for i in range(2)]
            prj_bf = [ea(nc.sbuf_tensor(f"prj_bf{i}", [128, B1], BF16)) for i in range(2)]
            swp_bf = [ea(nc.sbuf_tensor(f"swp_bf{i}", [128, B1], BF16)) for i in range(2)]
            wqs = [ea(nc.sbuf_tensor(f"wqs{i}", [128, 512], BF16)) for i in range(2)]
            thr_sb = [ea(nc.sbuf_tensor(f"thr_sb{i}", [128, 512], F32)) for i in range(2)]
            c2_bf = ea(nc.sbuf_tensor("c2_bf", [128, T], BF16))
            s2t_bf = ea(nc.sbuf_tensor("s2t_bf", [128, T], BF16))
            rows_r = ea(nc.sbuf_tensor("rows_r", [1, T], BF16))
            rowf1 = ea(nc.sbuf_tensor("rowf1", [96, B1], F32))
            rowf2 = ea(nc.sbuf_tensor("rowf2", [64, B1], F32))
            rowbf = ea(nc.sbuf_tensor("rowbf", [64, B1], BF16))
            m_sb = ea(nc.sbuf_tensor("m_sb", [128, B1], BF16))
            r_sb = ea(nc.sbuf_tensor("r_sb", [128, B1], F32))

            p_swp = [ea(nc.psum_tensor(f"p_swp{i}", [128, B1], F32)) for i in range(3)]
            p_stats = [ea(nc.psum_tensor(f"p_stats{i}", [1, 512], F32)) for i in range(2)]
            p_dq = ea(nc.psum_tensor("p_dq", [1, 512], F32))
            p_thr = [ea(nc.psum_tensor(f"p_thr{i}", [128, 512], F32)) for i in range(2)]

            ln_t0 = [0]
            r_tok = [None]
            rows_r_free = None
            mrow = rowf1[0:1, :]
            msqr = rowf1[32:33, :]
            varr = rowf1[64:65, :]
            rstdr = rowf2[0:1, :]
            mrr = rowf2[32:33, :]
            rstdr_bf = rowbf[0:1, :]
            mrr_bf = rowbf[32:33, :]

            P.op("gpsimd", lambda e: e.dma_start(out=c2_bf[:, :],
                                                 in_=c2s2[0:128, :]),
                 sem=dCON, inc=16)
            P.op("gpsimd", lambda e: e.dma_start(out=s2t_bf[:, :],
                                                 in_=c2s2[128:256, :]),
                 sem=dCON, inc=16)
            rope_tok = (dCON, dCON.n)

            dq_ctx = {"bufs": (stage, a_bf, s_scr, g1_scr, g2_scr, thr_rows,
                               p_dq, p_thr),
                      "stage_free": [None, None], "abs_free": [None, None],
                      "dqs_free": None, "thr_free": [None, None],
                      "wqs_free": [None, None],
                      "thrsb_free": [None, None], "sscr_free": [None, None]}
            xb_free = [None, None, None, None]
            xld_toks = {}
            for i in range(4):
                xld_toks[i] = P.op("gpsimd", lambda e, sl=i, t0=i * B1:
                                   e.dma_start(out=x_bf[sl][:, :, :],
                                               in_=xT[:, t0:t0 + B1].rearrange(
                                                   "(c p) t -> p c t", p=128)),
                                   waits=[], sem=dXB[i], inc=16)

            wg_groups = [[0, 2, 4, 6], [1, 3, 5, 7]]
            gather_toks = {}
            for wi, (nm, w_in, ICg) in enumerate(
                    [("q", wqT, 4), ("k", wkT, 4), ("v", wvT, 4),
                     ("o", woT, 2)]):
                OCg = 1024 if nm != "o" else 2048
                dequant2(w_in[:, :], None, ICg, OCg, dq_ctx,
                         spill=(wqs, cw_in[nm]))
                gather_toks[nm] = P.op(
                    "gpsimd", lambda e, nm=nm:
                    e.collective_compute("AllGather", ALU.bypass,
                                         replica_groups=wg_groups,
                                         ins=[cw_in[nm][:, :]],
                                         outs=[cw_out[nm][:, :]]),
                    waits=[(dSPL, dSPL.n)], sem=sCW[wi])
            ld_q = P.op("sync", lambda e: e.dma_start(
                out=wq_bf[:, :, :],
                in_=cw_out["q"][:, :].rearrange("(c p) n -> p c n", p=128)),
                waits=[gather_toks["q"]], sem=dWG[0], inc=16)
            ld_k = P.op("sync", lambda e: e.dma_start(
                out=wk_bf[:, :, :],
                in_=cw_out["k"][:, :].rearrange("(c p) n -> p c n", p=128)),
                waits=[gather_toks["k"]], sem=dWG[1], inc=16)
            ld_v = P.op("sync", lambda e: e.dma_start(
                out=wv_bf[:, :, :],
                in_=cw_out["v"][:, :].rearrange("(c p) n -> p c n", p=128)),
                waits=[gather_toks["v"]], sem=dWG[2], inc=16)
            dq_wq = ld_q
            dq_kv_tok = [ld_k, ld_v]

            xh_free = [None, None]
            psw_free = [None, None, None]
            prj_free = [None, None]
            swb_free = [None, None]
            stats_free = [None, None]
            pv_free = None
            sq_free = [None, None]
            dqt = dq_ctx

            def ln_stats(par, xb_sl, xld):
                t = None
                for ic in range(16):
                    t = P.op("tensor", lambda e, ic=ic, sl=xb_sl, par=par:
                             e.matmul(p_stats[par][0:1, 0:B1], onescol_bf[:, :],
                                      x_bf[sl][:, ic, :], start=(ic == 0),
                                      stop=(ic == 15), skip_group_check=True),
                             waits=[xld, con_tok,
                                    stats_free[par] if ic == 0 else None],
                             sem=sPE)
                mean_mm = t
                for ic in range(16):
                    sl2 = ic % 2
                    tsq = P.op("vector", lambda e, ic=ic, sl=xb_sl, sl2=sl2:
                               e.tensor_tensor(sq_scr[sl2][:, :],
                                               x_bf[sl][:, ic, :],
                                               x_bf[sl][:, ic, :], op=ALU.mult),
                               waits=[xld, sq_free[sl2]], sem=sV)
                    t = P.op("tensor", lambda e, ic=ic, sl2=sl2, par=par:
                             e.matmul(p_stats[par][0:1, B1:2 * B1],
                                      onescol_bf[:, :],
                                      sq_scr[sl2][:, :], start=(ic == 0),
                                      stop=(ic == 15), skip_group_check=True),
                             waits=[tsq], sem=sPE)
                    sq_free[sl2] = (sPE, sPE.n)
                return (mean_mm, t)

            def ln_norm(par, xb_sl, xh_sl, mean_mm, sq_mm):
                t = P.op("vector", lambda e, par=par:
                         e.tensor_scalar(mrow, p_stats[par][0:1, 0:B1],
                                         1.0 / 2048.0, None, op0=ALU.mult),
                         waits=[mean_mm, sq_mm], sem=sV)
                t = P.op("vector", lambda e:
                         e.tensor_tensor(msqr, mrow, mrow, op=ALU.mult),
                         waits=[t], sem=sV)
                t = P.op("vector", lambda e, par=par:
                         e.scalar_tensor_tensor(varr, p_stats[par][0:1, B1:2 * B1],
                                                1.0 / 2048.0, msqr,
                                                op0=ALU.mult, op1=ALU.subtract),
                         waits=[t], sem=sV)
                stats_free[par] = (sV, sV.n)
                t = P.op("vector", lambda e:
                         e.tensor_scalar(varr, varr, 1e-5, None, op0=ALU.add),
                         waits=[t], sem=sV)
                t = P.op("scalar", lambda e:
                         e.activation(varr, varr, AF.Ln),
                         waits=[t], sem=sSC)
                t = P.op("scalar", lambda e:
                         e.activation(rstdr, varr, AF.Exp, scale=-0.5),
                         waits=[t], sem=sSC)
                t1 = P.op("vector", lambda e, t0=ln_t0[0]:
                          e.tensor_copy(rows_r[0:1, t0:t0 + B1], rstdr),
                          waits=[t, rows_r_free], sem=sV)

                t2 = P.op("vector", lambda e: e.tensor_copy(mrr_bf, mrow),
                          waits=[t], sem=sV)
                b1 = P.op("tensor", lambda e:
                          e.matmul(p_thr[0][:, 0:B1], onesrow_bf[32:33, :],
                                   mrr_bf,
                                   start=True, stop=True, skip_group_check=True),
                          waits=[t2, dqt["thr_free"][0]], sem=sPE)
                b2 = P.op("tensor", lambda e, t0=ln_t0[0]:
                          e.matmul(p_thr[1][:, 0:B1], onesrow_bf[0:1, :],
                                   rows_r[0:1, t0:t0 + B1],
                                   start=True, stop=True, skip_group_check=True),
                          waits=[t1, dqt["thr_free"][1]], sem=sPE)
                c1 = P.op("vector", lambda e: e.tensor_copy(m_sb[:, :],
                                                            p_thr[0][:, 0:B1]),
                          waits=[b1], sem=sV)
                dqt["thr_free"][0] = (sV, sV.n)
                c2t = P.op("vector", lambda e: e.tensor_copy(r_sb[:, :],
                                                             p_thr[1][:, 0:B1]),
                           waits=[b2], sem=sV)
                dqt["thr_free"][1] = (sV, sV.n)
                r_tok[0] = c2t
                last = None
                for ic in range(16):
                    eng = "gpsimd" if ic % 3 == 2 else "vector"
                    sem_ = sPL if ic % 3 == 2 else sV
                    last = P.op(eng, lambda e, ic=ic, sl=xb_sl, so=xh_sl:
                                e.tensor_tensor(xh[so][:, ic, :],
                                                x_bf[sl][:, ic, :],
                                                m_sb[:, :], op=ALU.subtract),
                                waits=[c1, xh_free[xh_sl] if ic == 0 else None],
                                sem=sem_)
                last = [(sV, sV.n), (sPL, sPL.n)]
                return last

            def proj_rope(w_bf, xh_sl, xh_tok, t0, dst_sp, do_rope):
                def emit_sw(hc, pp, pb, ev):
                    sw = P.op("tensor", lambda e, pb=pb, pp=pp:
                              e.matmul(p_swp[pb][:, :],
                                       perm_bf[:, :], prj_bf[pp][:, :],
                                       start=True, stop=True,
                                       skip_group_check=True),
                              waits=[ev, con_tok], sem=sPE)
                    sv = P.op("vector", lambda e, pp=pp, pb=pb:
                              e.tensor_copy(swp_bf[pp][:, :],
                                            p_swp[pb][:, :]),
                              waits=[sw, swb_free[pp]], sem=sV)
                    psw_free[pb] = (sV, sV.n)
                    t1 = P.op("gpsimd", lambda e, pp=pp, t0=t0:
                              e.tensor_tensor(prj_bf[pp][:, :], prj_bf[pp][:, :],
                                              c2_bf[:, t0:t0 + B1], op=ALU.mult),
                              waits=[sw, rope_tok], sem=sPL)
                    t2 = P.op("gpsimd", lambda e, pp=pp, t0=t0:
                              e.tensor_tensor(swp_bf[pp][:, :], swp_bf[pp][:, :],
                                              s2t_bf[:, t0:t0 + B1], op=ALU.mult),
                              waits=[sv], sem=sPL)
                    t3 = P.op("gpsimd", lambda e, pp=pp:
                              e.tensor_tensor(swp_bf[pp][:, :], prj_bf[pp][:, :],
                                              swp_bf[pp][:, :], op=ALU.add),
                              waits=[t1, t2], sem=sPL)
                    prj_free[pp] = (sPL, sPL.n)
                    st = P.op("scalar", lambda e, hc=hc, t0=t0, pp=pp,
                              dst_sp=dst_sp:
                              e.dma_start(out=dst_sp[hc, :, t0:t0 + B1],
                                          in_=swp_bf[pp][:, :]),
                              waits=[t3], sem=dSPL, inc=16)
                    swb_free[pp] = (dSPL, dSPL.n)

                pend_sw = None
                for hc in range(8):
                    pp = hc % 2
                    pb = hc % 3
                    mt = None
                    for ic in range(16):
                        w = [xh_tok]
                        if ic == 0:
                            w.append(psw_free[pb])
                        mt = P.op("tensor", lambda e, hc=hc, ic=ic, sl=xh_sl,
                                  pp=pb:
                                  e.matmul(p_swp[pp][:, :],
                                           w_bf[:, ic, hc * 128:(hc + 1) * 128],
                                           xh[sl][:, ic, :], start=(ic == 0),
                                           stop=(ic == 15),
                                           skip_group_check=True),
                                  waits=w, sem=sPE)
                    ev = P.op("vector", lambda e, pp=pp, pb=pb:
                              e.tensor_tensor(prj_bf[pp][:, :], p_swp[pb][:, :],
                                              r_sb[:, :], op=ALU.mult),
                              waits=[mt, prj_free[pp], r_tok[0]], sem=sV)
                    if not do_rope:
                        psw_free[pb] = (sV, sV.n)
                        st = P.op("scalar", lambda e, hc=hc, t0=t0, pp=pp,
                                  dst_sp=dst_sp:
                                  e.dma_start(out=dst_sp[hc, :, t0:t0 + B1],
                                              in_=prj_bf[pp][:, :]),
                                  waits=[ev], sem=dSPL, inc=16)
                        prj_free[pp] = [(sV, sV.n), (dSPL, dSPL.n)]
                        continue
                    if pend_sw is not None:
                        emit_sw(*pend_sw)
                    pend_sw = (hc, pp, pb, ev)
                if pend_sw is not None:
                    emit_sw(*pend_sw)

            def xload(b):
                sl = b % 4
                t0 = b * B1
                return P.op("gpsimd", lambda e, sl=sl, t0=t0:
                            e.dma_start(out=x_bf[sl][:, :, :],
                                        in_=xT[:, t0:t0 + B1].rearrange(
                                            "(c p) t -> p c t", p=128)),
                            waits=[xb_free[sl]], sem=dXB[sl], inc=16)

            for b in range(NB1 + 1):
                if b < NB1:
                    stats_toks_b = ln_stats(b % 2, b % 4, xld_toks.pop(b))
                if b > 0:
                    pend = b - 1
                    pxl = pend % 4
                    psl = pend % 2
                    pt0 = pend * B1
                    mm_m, mm_s = pend_stats
                    ln_t0[0] = pt0
                    xht = ln_norm(pend % 2, pxl, psl, mm_m, mm_s)
                    xb_free[pxl] = [(sV, sV.n), (sPL, sPL.n), (sPE, sPE.n)]
                    P.op("scalar", lambda e, psl=psl, pt0=pt0:
                         e.dma_start(out=xh1_sp[:, :, pt0:pt0 + B1].rearrange(
                             "c p t -> p c t"), in_=xh[psl][:, :, :]),
                         waits=[xht], sem=dSPL, inc=16)
                    if pend + 4 < NB1:
                        xld_toks[pend + 4] = xload(pend + 4)
                    if pend == 0:
                        xht = [xht, dq_wq]
                    proj_rope(wq_bf, psl, xht, pt0, q_sp, True)
                    xh_free[psl] = [(sPE, sPE.n), (dSPL, dSPL.n)]
                if b < NB1:
                    pend_stats = stats_toks_b

            for b in range(NB1):
                t0 = b * B1
                sl = b % 2
                xrd = P.op("sync", lambda e, sl=sl, t0=t0:
                           e.dma_start(out=xh[sl][:, :, :],
                                       in_=xh1_sp[:, :, t0:t0 + B1].rearrange(
                                           "c p t -> p c t")),
                           waits=[xh_free[sl],
                                  (dSPL, dSPL.n) if b == 0 else None,
                                  dq_kv_tok if b == 0 else None],
                           sem=dXB[sl], inc=16)
                xh_tok = (dXB[sl], dXB[sl].n)
                b2 = P.op("tensor", lambda e, t0=t0:
                          e.matmul(p_thr[1][:, 0:B1], onesrow_bf[0:1, :],
                                   rows_r[0:1, t0:t0 + B1],
                                   start=True, stop=True, skip_group_check=True),
                          waits=[dqt["thr_free"][1],
                                 (dSPL, dSPL.n) if b == 0 else None], sem=sPE)
                c2t = P.op("vector", lambda e: e.tensor_copy(r_sb[:, :],
                                                             p_thr[1][:, 0:B1]),
                           waits=[b2, r_tok[0]], sem=sV)
                dqt["thr_free"][1] = (sV, sV.n)
                r_tok[0] = c2t
                proj_rope(wk_bf, sl, xh_tok, t0, k_sp, True)
                proj_rope(wv_bf, sl, xh_tok, t0, v_sp, False)
                xh_free[sl] = [(sPE, sPE.n)]

            p1 = dict(pe=(sPE, sPE.n), v=(sV, sV.n), sc=(sSC, sSC.n),
                      spl=(dSPL, dSPL.n))
            if upto == "p1":
                nspl = dSPL.n
                P.op("sync", lambda e: e.wait_ge(dSPL.h, nspl),
                     waits=[p1["pe"], p1["v"], p1["sc"]])
            P.flush(blk)
        if upto == "p1":
            return nc

        with contextlib.ExitStack() as sb:
            eb = sb.enter_context
            wo_bf = eb(nc.sbuf_tensor("wo_bf", [128, 8, 2048], BF16))
            masks = eb(nc.sbuf_tensor("masks_s", [128, 4, 512], BF16))
            qblk = [eb(nc.sbuf_tensor(f"qblk{i}", [128, 8, 512], BF16)) for i in range(2)]
            kslab = [eb(nc.sbuf_tensor(f"kslab{i}", [128, T], BF16)) for i in range(3)]
            vslab = [eb(nc.sbuf_tensor(f"vslab{i}", [128, 32, 128], BF16)) for i in range(3)]
            ptb = [eb(nc.sbuf_tensor(f"ptb{i}", [128, 1024], BF16)) for i in range(2)]
            ptsum = [eb(nc.sbuf_tensor(f"ptsum{i}", [128, 512], BF16)) for i in range(2)]
            yblk = eb(nc.sbuf_tensor("yblk", [128, 8, 512], BF16))
            osb = eb(nc.sbuf_tensor("osb", [128, 16, 512], F32))
            rb_sb = eb(nc.sbuf_tensor("rb_sb", [128, 512], F32))
            row2 = eb(nc.sbuf_tensor("row2", [64, 512], F32))
            p_sc = [eb(nc.psum_tensor(f"p_sc{i}", [128, 1024], F32)) for i in range(2)]
            p_av = [eb(nc.psum_tensor(f"p_av{i}", [128, 512], F32)) for i in range(2)]
            p_sum = eb(nc.psum_tensor("p_sum", [1, 512], F32))
            p_rbo = eb(nc.psum_tensor("p_rbo", [128, 512], F32))
            sum_row = row2[0:1, :]
            r_row = row2[32:33, :]

            p1toks = [p1["pe"], p1["v"], p1["sc"], p1["spl"]]
            for eng in ENGINES:
                P.fence(eng, p1toks)

            P.op("sync", lambda e: e.dma_start(
                out=masks[:, :, :], in_=masksI[:, :, :].rearrange("m p t -> p m t")),
                 sem=dCON, inc=16)
            mask_tok = (dCON, dCON.n)

            P.op("sync", lambda e: e.dma_start(
                out=wo_bf[:, :, :],
                in_=cw_out["o"][:, :].rearrange("(c p) n -> p c n", p=128)),
                 waits=[gather_toks["o"]], sem=dST[0], inc=16)
            dq_wo = (dST[0], dST[0].n)

            sc_free = [None, None]
            av_free = [None, None]
            sum_free = None
            rbo_free = None
            qb_free = [None, None]
            kv_free = [None, None, None]
            ptb_free = [None, None]
            ptsum_free = [None, None]
            yblk_free = None
            osb_free = None

            def mk_score(col, ktile, psl, ksl, qsl, h):
                def f(e):
                    return e.matmul(p_sc[psl][:, col * 512:(col + 1) * 512],
                                    kslab[ksl][:, ktile * 128:(ktile + 1) * 128],
                                    qblk[qsl][:, h, :],
                                    start=True, stop=True, skip_group_check=True)
                return f

            for qb in range(NQB):
                q0 = qb * 512
                KT = (qb + 1) * 4
                qsl = qb % 2
                P.op("sync", lambda e, qsl=qsl, q0=q0:
                     e.dma_start(out=qblk[qsl][:, :, :],
                                 in_=q_sp[:, :, q0:q0 + 512].rearrange(
                                     "h p t -> p h t")),
                     waits=[qb_free[qsl], p1["spl"] if qb == 0 else None],
                     sem=dQB[qsl], inc=16)
                qtok = (dQB[qsl], dQB[qsl].n)
                for h in range(8):
                    it = qb * 8 + h
                    ksl = it % 3
                    KH = KT // 2
                    kvtok_h = None
                    for half in range(2):
                        P.op("sync", lambda e, ksl=ksl, h=h, KT=KT, half=half,
                             KH=KH:
                             e.dma_start(out=kslab[ksl][:, half * KH * 128:
                                                        (KT if half else KH)
                                                        * 128],
                                         in_=k_sp[h, :, half * KH * 128:
                                                  (KT if half else KH) * 128]),
                             waits=[kv_free[ksl]
                                    if half == 0 else None,
                                    p1["spl"] if it == 0 else None],
                             sem=dKV[ksl], inc=16)
                        P.op("sync", lambda e, ksl=ksl, h=h, KT=KT, half=half,
                             KH=KH:
                             e.dma_start(out=vslab[ksl][:, half * KH:
                                                        (KT if half else KH),
                                                        :],
                                         in_=v_sp[h, :, half * KH * 128:
                                                  (KT if half else KH) * 128],
                                         transpose=True),
                             waits=[kvtok_h], sem=dKV[ksl], inc=16)
                        if half == 0:
                            kvtok_h = (dKV[ksl], dKV[ksl].n)
                    kvtok = (dKV[ksl], dKV[ksl].n)
                    hpar = h % 2
                    avmm = None
                    summ = None
                    mt_pend = None
                    for kp in range(KT // 2 + 1):
                        if kp < KT // 2:
                            psl = kp % 2
                            ktok = kvtok_h if 2 * kp + 1 < KH else kvtok
                            P.op("tensor", mk_score(0, 2 * kp, psl, ksl, qsl, h),
                                 waits=[ktok, qtok, sc_free[psl]], sem=sPE)
                            mm2 = P.op("tensor", mk_score(1, 2 * kp + 1, psl,
                                                          ksl, qsl, h),
                                       waits=[], sem=sPE)
                            ex = P.op("scalar", lambda e, psl=psl:
                                      e.activation(ptb[psl][:, :],
                                                   p_sc[psl][:, :],
                                                   AF.Exp, scale=ATT_SCALE),
                                      waits=[mm2, ptb_free[psl]], sem=sSC)
                            sc_free[psl] = (sSC, sSC.n)
                            mtoks = [ex, ex]
                            for j, kt in enumerate((2 * kp, 2 * kp + 1)):
                                di = kt - (KT - 4)
                                if 0 <= di < 4:
                                    mtoks[j] = P.op(
                                        "vector",
                                        lambda e, psl=psl, j=j, di=di:
                                        e.tensor_tensor(
                                            ptb[psl][:, j * 512:(j + 1) * 512],
                                            ptb[psl][:, j * 512:(j + 1) * 512],
                                            masks[:, di, :], op=ALU.mult),
                                        waits=[ex, mask_tok], sem=sV)
                            mt_cur = (kp, mtoks)
                        else:
                            mt_cur = None
                        if mt_pend is not None:
                            kp0, mtoks0 = mt_pend
                            psl0 = kp0 % 2
                            padd = P.op("vector",
                                        lambda e, psl=psl0:
                                        e.tensor_tensor(
                                            ptsum[psl][:, :],
                                            ptb[psl][:, 0:512],
                                            ptb[psl][:, 512:1024],
                                            op=ALU.add),
                                        waits=[mtoks0[0], mtoks0[1],
                                               ptsum_free[psl0]], sem=sV)
                            for j, kt in enumerate((2 * kp0, 2 * kp0 + 1)):
                                avmm = P.op(
                                    "tensor",
                                    lambda e, psl=psl0, j=j, kt=kt, ksl=ksl,
                                    KT=KT, hpar=hpar:
                                    e.matmul(p_av[hpar][:, :],
                                             vslab[ksl][:, kt, :],
                                             ptb[psl][:, j * 512:(j + 1) * 512],
                                             start=(kt == 0),
                                             stop=(kt == KT - 1),
                                             skip_group_check=True),
                                    waits=[mtoks0[j],
                                           av_free[hpar] if kt == 0 else None],
                                    sem=sPE)
                            summ = P.op(
                                "tensor",
                                lambda e, psl=psl0, kp0=kp0, KT=KT:
                                e.matmul(p_sum[:, :], onescol_bf[:, :],
                                         ptsum[psl][:, :],
                                         start=(kp0 == 0),
                                         stop=(kp0 == KT // 2 - 1),
                                         skip_group_check=True),
                                waits=[padd,
                                       sum_free if kp0 == 0 else None],
                                sem=sPE)
                            ptb_free[psl0] = (sPE, sPE.n)
                            ptsum_free[psl0] = (sPE, sPE.n)
                        mt_pend = mt_cur
                    kv_free[ksl] = (sPE, sPE.n)
                    t = P.op("scalar", lambda e: e.activation(r_row,
                                                              p_sum[0:1, :],
                                                              AF.Ln),
                             waits=[summ], sem=sSC)
                    sum_free = (sSC, sSC.n)
                    t = P.op("scalar", lambda e: e.activation(r_row, r_row,
                                                              AF.Exp,
                                                              scale=-1.0),
                             waits=[t], sem=sSC)
                    rb = P.op("tensor", lambda e:
                              e.matmul(p_rbo[:, :], onesrow_f[32:33, :], r_row,
                                       start=True, stop=True,
                                       skip_group_check=True),
                              waits=[t, rbo_free], sem=sPE)
                    t = P.op("vector", lambda e: e.tensor_copy(rb_sb[:, :],
                                                               p_rbo[:, :]),
                             waits=[rb], sem=sV)
                    rbo_free = (sV, sV.n)
                    P.op("vector", lambda e, h=h, hpar=hpar:
                         e.tensor_tensor(yblk[:, h, :], p_av[hpar][:, :],
                                         rb_sb[:, :], op=ALU.mult),
                         waits=[avmm, yblk_free if h == 0 else None], sem=sV)
                    av_free[hpar] = (sV, sV.n)
                y_tok = (sV, sV.n)
                ev_last = None
                for oc in range(16):
                    r3 = oc % 3
                    pt = [p_av[0], p_av[1], p_rbo][r3]
                    mt = None
                    for ic in range(8):
                        w = [y_tok] if ic == 0 else []
                        if qb == 0 and oc == 0 and ic == 0:
                            w.append(dq_wo)
                        if ic == 0:
                            w.append(rbo_free if r3 == 2 else av_free[r3])
                        mt = P.op("tensor", lambda e, oc=oc, ic=ic, pt=pt:
                                  e.matmul(pt[:, :],
                                           wo_bf[:, ic, oc * 128:(oc + 1) * 128],
                                           yblk[:, ic, :], start=(ic == 0),
                                           stop=(ic == 7),
                                           skip_group_check=True),
                                  waits=w, sem=sPE)
                    ev = P.op("vector", lambda e, oc=oc, pt=pt:
                              e.tensor_copy(osb[:, oc, :], pt[:, :]),
                              waits=[mt, osb_free if oc == 0 else None],
                              sem=sV)
                    if r3 == 2:
                        rbo_free = (sV, sV.n)
                    else:
                        av_free[r3] = (sV, sV.n)
                    ev_last = ev
                yblk_free = (sPE, sPE.n)
                qb_free[qsl] = (sPE, sPE.n)
                c = qb % 4
                hf = qb // 4
                P.op("gpsimd", lambda e, c=c, hf=hf:
                     e.dma_start(out=o_part[c, hf * 2048:(hf + 1) * 2048, :]
                                 .rearrange("(o p) t -> p o t", p=128),
                                 in_=osb[:, :, :]),
                     waits=[ev_last], sem=dOP[c], inc=16)
                osb_free = (dOP[c], dOP[c].n)
                if hf == 1:
                    P.op("gpsimd", lambda e, c=c:
                         e.collective_compute(
                             "ReduceScatter", ALU.add,
                             replica_groups=[[0, 1], [2, 3], [4, 5], [6, 7]],
                             ins=[o_part[c, :, :]], outs=[or_red[c, :, :]]),
                         waits=[(dOP[c], dOP[c].n)], sem=sCC[c])

            p2 = dict(pe=(sPE, sPE.n), v=(sV, sV.n), sc=(sSC, sSC.n))
            if upto == "p2":
                if or_dbg is not None:
                    for ch in range(4):
                        P.op("sync", lambda e, ch=ch:
                             e.dma_start(out=or_dbg[ch], in_=or_red[ch]),
                             waits=[(sCC[ch], 1)], sem=dOUT, inc=16)
                P.op("sync", lambda e: e.wait_ge(dOUT.h, 64),
                     waits=[p2["pe"], p2["v"], p2["sc"], (sCC[3], 1)])
            P.flush(blk)
        if upto == "p2":
            return nc

        GRPS = [[0, 1], [2, 3], [4, 5], [6, 7], [8]]
        with contextlib.ExitStack() as s3:
            e3 = s3.enter_context
            x1 = e3(nc.sbuf_tensor("x1", [128, 16, 1024], F32))
            x2bf = e3(nc.sbuf_tensor("x2bf", [128, 16, 1024], BF16))
            a3 = [e3(nc.sbuf_tensor(f"a3_{i}", [128, 8, 1024], BF16)) for i in range(2)]
            w13b = [e3(nc.sbuf_tensor(f"w13b{i}", [128, 16, 256], BF16)) for i in range(2)]
            w2b = [e3(nc.sbuf_tensor(f"w2b{i}", [128, 8, 256], BF16)) for i in range(2)]
            comb_cur = e3(nc.sbuf_tensor("combc", [128, 1024], BF16))
            comb_row = e3(nc.sbuf_tensor("combr", [1, 1024], BF16))
            s1f = [e3(nc.sbuf_tensor(f"s1f{i}", [128, 512], F32)) for i in range(2)]
            sq3 = e3(nc.sbuf_tensor("sq3", [128, 1024], BF16))
            xb3 = e3(nc.sbuf_tensor("xb3", [128, 1024], F32))
            ob3 = e3(nc.sbuf_tensor("ob3", [128, 1024], F32))
            m3_sb = e3(nc.sbuf_tensor("m3_sb", [128, 1024], BF16))
            r3_sb = e3(nc.sbuf_tensor("r3_sb", [128, 1024], BF16))
            g_sb = e3(nc.sbuf_tensor("g_sb", [8, 1024], F32))
            gt_sb = e3(nc.sbuf_tensor("gt_sb", [128, 64], F32))
            rtm = e3(nc.sbuf_tensor("rtm", [128, 8], F32))
            comb_tm = e3(nc.sbuf_tensor("comb_tm", [128, 8, 8], F32))
            comb_fm = e3(nc.sbuf_tensor("comb_fm", [8, 1024], BF16))
            gs_scr = [e3(nc.sbuf_tensor(f"gs{i}", [128, 8], F32)) for i in range(4)]
            gm_scr = [e3(nc.sbuf_tensor(f"gm{i}", [128, 1], F32)) for i in range(3)]
            gsum_r = e3(nc.sbuf_tensor("gsum_r", [1, 8], F32))
            gw_f = e3(nc.sbuf_tensor("gw_f", [128, 16, 8], F32))
            rp1 = e3(nc.sbuf_tensor("rp1", [96, 1024], F32))
            rp2 = e3(nc.sbuf_tensor("rp2", [64, 1024], F32))
            rpb = e3(nc.sbuf_tensor("rpb", [64, 1024], BF16))
            ph = e3(nc.psum_tensor("ph", [128, 1024], F32))
            pg = e3(nc.psum_tensor("pg", [128, 1024], F32))
            pd = [e3(nc.psum_tensor(f"pd{i}", [128, 1024], F32)) for i in range(2)]

            mean3 = rp1[0:1, :]
            msq3 = rp1[32:33, :]
            var3 = rp1[64:65, :]
            rstd3 = rp2[0:1, :]
            mr3 = rp2[32:33, :]
            r3bf = rpb[0:1, :]
            mr3bf = rpb[32:33, :]

            p2toks = [p2["pe"], p2["v"], p2["sc"]]
            for eng in ENGINES:
                P.fence(eng, p2toks)

            P.op("sync", lambda e: e.dma_start(
                out=gw_f[:, :, :],
                in_=gwT[:, :].rearrange("(c p) e -> p c e", p=128)),
                 sem=dCON, inc=16)
            gw_tok = (dCON, dCON.n)

            ph_free = None
            pg_free = None
            pd_free = [None, None]
            w13_free = [None, None]
            w2_free = [None, None]
            x3_free = [None, None]
            a3_free = [None, None]
            sq3_free = None
            s1_free = [None, None]
            crow_free = None
            out_cnt = 0

            gmm = None
            for ic in range(16):
                gmm = P.op("tensor", lambda e, ic=ic:
                           e.matmul(ph[0:1, 0:8], onescol_f[:, :],
                                    gw_f[:, ic, :],
                                    start=(ic == 0), stop=(ic == 15),
                                    skip_group_check=True),
                           waits=[gw_tok, con_tok,
                                  ph_free if ic == 0 else None], sem=sPE)
            gr = P.op("vector", lambda e:
                      e.tensor_scalar(gsum_r[:, :], ph[0:1, 0:8], -1.0, None,
                                      op0=ALU.mult),
                      waits=[gmm], sem=sV)
            ph_free = gr

            for sh in range(2):
                tl = sh * 1024
                add_toks = []
                for ic in range(16):
                    s2_ = ic % 2
                    P.op("sync", lambda e, ic=ic, tl=tl:
                         e.dma_start(out=xb3[:, :],
                                     in_=xTo[ic * 128:(ic + 1) * 128,
                                             tl:tl + 1024]),
                         waits=[x3_free[0]], sem=dX3[0], inc=16)
                    P.op("sync", lambda e, ic=ic, sh=sh:
                         e.dma_start(out=ob3[:, 0:512],
                                     in_=or_red[2 * sh,
                                                ic * 128:(ic + 1) * 128, :]),
                         waits=[(sCC[2 * sh], 1), x3_free[1]],
                         sem=dX3[1], inc=16)
                    P.op("sync", lambda e, ic=ic, sh=sh:
                         e.dma_start(out=ob3[:, 512:1024],
                                     in_=or_red[2 * sh + 1,
                                                ic * 128:(ic + 1) * 128, :]),
                         waits=[(sCC[2 * sh + 1], 1)], sem=dX3[1], inc=16)
                    P.op("vector", lambda e, ic=ic:
                         e.tensor_tensor(x1[:, ic, :], xb3[:, :], ob3[:, :],
                                         op=ALU.add),
                         waits=[(dX3[0], dX3[0].n), (dX3[1], dX3[1].n),
                                (dOUT, out_cnt * 16) if ic == 0 else None],
                         sem=sV)
                    x3_free[0] = (sV, sV.n)
                    x3_free[1] = (sV, sV.n)
                    tc = P.op("vector", lambda e, ic=ic:
                              e.tensor_copy(x2bf[:, ic, :], x1[:, ic, :]),
                              waits=[], sem=sV)
                    add_toks.append(tc)
                mean_mm = None
                for ic in range(16):
                    for cl in range(2):
                        mean_mm = P.op("tensor", lambda e, ic=ic, cl=cl:
                                       e.matmul(ph[0:1, cl * 512:(cl + 1) * 512],
                                                onescol_bf[:, :],
                                                x2bf[:, ic,
                                                     cl * 512:(cl + 1) * 512],
                                                start=(ic == 0), stop=(ic == 15),
                                                skip_group_check=True),
                                       waits=[add_toks[ic],
                                              ph_free if ic == 0 and cl == 0
                                              else None],
                                       sem=sPE)
                sq_mm = None
                for ic in range(16):
                    tsq = P.op("vector", lambda e, ic=ic:
                               e.tensor_tensor(sq3[:, :], x2bf[:, ic, :],
                                               x2bf[:, ic, :], op=ALU.mult),
                               waits=[add_toks[ic], sq3_free], sem=sV)
                    for cl in range(2):
                        sq_mm = P.op("tensor", lambda e, ic=ic, cl=cl:
                                     e.matmul(pg[0:1, cl * 512:(cl + 1) * 512],
                                              onescol_bf[:, :],
                                              sq3[:, cl * 512:(cl + 1) * 512],
                                              start=(ic == 0), stop=(ic == 15),
                                              skip_group_check=True),
                                     waits=[tsq,
                                            pg_free if ic == 0 and cl == 0
                                            else None],
                                     sem=sPE)
                    sq3_free = (sPE, sPE.n)
                t = P.op("vector", lambda e:
                         e.tensor_scalar(mean3, ph[0:1, :], 1.0 / 2048.0,
                                         None, op0=ALU.mult),
                         waits=[mean_mm], sem=sV)
                mrow_tok = t
                ph_free = (sV, sV.n)
                t = P.op("vector", lambda e:
                         e.tensor_tensor(msq3, mean3, mean3, op=ALU.mult),
                         waits=[t], sem=sV)
                t = P.op("vector", lambda e:
                         e.scalar_tensor_tensor(var3, pg[0:1, :],
                                                1.0 / 2048.0, msq3,
                                                op0=ALU.mult, op1=ALU.subtract),
                         waits=[sq_mm, t], sem=sV)
                pg_free = (sV, sV.n)
                t = P.op("vector", lambda e:
                         e.tensor_scalar(var3, var3, 1e-5, None, op0=ALU.add),
                         waits=[t], sem=sV)
                t = P.op("scalar", lambda e:
                         e.activation(var3, var3, AF.Ln),
                         waits=[t], sem=sSC)
                rrow_tok = P.op("scalar", lambda e:
                                e.activation(rstd3, var3, AF.Exp, scale=-0.5),
                                waits=[t], sem=sSC)
                t = P.op("vector", lambda e:
                         e.tensor_tensor(mr3, mean3, rstd3, op=ALU.mult),
                         waits=[rrow_tok], sem=sV)
                t1 = P.op("vector", lambda e: e.tensor_copy(r3bf, rstd3),
                          waits=[t], sem=sV)
                t2 = P.op("vector", lambda e: e.tensor_copy(mr3bf, mr3),
                          waits=[t], sem=sV)
                b1 = b2 = None
                for cl in range(2):
                    b1 = P.op("tensor", lambda e, cl=cl:
                              e.matmul(pd[0][:, cl * 512:(cl + 1) * 512],
                                       onesrow_bf[32:33, :],
                                       mr3bf[0:1, cl * 512:(cl + 1) * 512],
                                       start=True, stop=True,
                                       skip_group_check=True),
                              waits=[t2, pd_free[0] if cl == 0 else None],
                              sem=sPE)
                    b2 = P.op("tensor", lambda e, cl=cl:
                              e.matmul(pd[1][:, cl * 512:(cl + 1) * 512],
                                       onesrow_bf[0:1, :],
                                       r3bf[0:1, cl * 512:(cl + 1) * 512],
                                       start=True, stop=True,
                                       skip_group_check=True),
                              waits=[t1, pd_free[1] if cl == 0 else None],
                              sem=sPE)
                c1 = P.op("vector", lambda e: e.tensor_copy(m3_sb[:, :],
                                                            pd[0][:, :]),
                          waits=[b1], sem=sV)
                pd_free[0] = (sV, sV.n)
                c2_ = P.op("vector", lambda e: e.tensor_copy(r3_sb[:, :],
                                                             pd[1][:, :]),
                           waits=[b2], sem=sV)
                pd_free[1] = (sV, sV.n)
                xh_last = None
                for ic in range(16):
                    t = P.op("vector", lambda e, ic=ic:
                             e.tensor_tensor(sq3[:, :], x2bf[:, ic, :],
                                             r3_sb[:, :], op=ALU.mult),
                             waits=[c2_, sq3_free], sem=sV)
                    xh_last = P.op("vector", lambda e, ic=ic:
                                   e.tensor_tensor(x2bf[:, ic, :], sq3[:, :],
                                                   m3_sb[:, :], op=ALU.subtract),
                                   waits=[c1, t], sem=sV)
                    sq3_free = (sV, sV.n)
                xhat_tok = xh_last
                gmm = None
                for ic in range(16):
                    for cl in range(2):
                        gmm = P.op("tensor", lambda e, ic=ic, cl=cl:
                                   e.matmul(ph[0:8, cl * 512:(cl + 1) * 512],
                                            gw_f[:, ic, :],
                                            x1[:, ic, cl * 512:(cl + 1) * 512],
                                            start=(ic == 0), stop=False,
                                            skip_group_check=True),
                                   waits=[add_toks[ic], gr,
                                          ph_free if ic == 0 and cl == 0
                                          else None],
                                   sem=sPE)
                for cl in range(2):
                    gmm = P.op("tensor", lambda e, cl=cl:
                               e.matmul(ph[0:8, cl * 512:(cl + 1) * 512],
                                        gsum_r[:, :],
                                        mean3[0:1, cl * 512:(cl + 1) * 512],
                                        start=False, stop=True,
                                        skip_group_check=True),
                               waits=[mrow_tok], sem=sPE)
                tg = P.op("vector", lambda e: e.tensor_copy(g_sb[:, :],
                                                            ph[0:8, :]),
                          waits=[gmm], sem=sV)
                ph_free = (sV, sV.n)
                rt = None
                for s8 in range(8):
                    rt = P.op("tensor", lambda e, s8=s8:
                              e.transpose(pd[0][:, s8:s8 + 1],
                                          rstd3[0:1, s8 * 128:(s8 + 1) * 128],
                                          ident1[:, :]),
                              waits=[rrow_tok, con_tok,
                                     pd_free[0] if s8 == 0 else None], sem=sPE)
                rt = P.op("vector", lambda e: e.tensor_copy(rtm[:, :],
                                                            pd[0][:, 0:8]),
                          waits=[rt], sem=sV)
                pd_free[0] = (sV, sV.n)
                gt = None
                for s8 in range(8):
                    gt = P.op("tensor", lambda e, s8=s8:
                              e.transpose(pd[1][:, s8 * 8:(s8 + 1) * 8],
                                          g_sb[:, s8 * 128:(s8 + 1) * 128],
                                          ident8[:, :]),
                              waits=[tg, con_tok,
                                     pd_free[1] if s8 == 0 else None], sem=sPE)
                gt = P.op("vector", lambda e: e.tensor_copy(gt_sb[:, :],
                                                            pd[1][:, 0:64]),
                          waits=[gt], sem=sV)
                pd_free[1] = (sV, sV.n)
                cb_toks = []
                prev_ax = None
                for s8 in range(8):
                    g0 = P.op("vector", lambda e, s8=s8:
                              e.tensor_scalar(gs_scr[0][:, :],
                                              gt_sb[:, s8 * 8:(s8 + 1) * 8],
                                              rtm[:, s8:s8 + 1], None,
                                              op0=ALU.mult),
                              waits=[gt, rt, prev_ax], sem=sV)
                    m1 = P.op("vector", lambda e:
                              e.reduce_max(gm_scr[0][:, :], gs_scr[0][:, :],
                                           axis=AX),
                              waits=[g0], sem=sV)
                    nm = P.op("vector", lambda e:
                              e.tensor_scalar(gm_scr[1][:, :], gm_scr[0][:, :],
                                              -1.0, None, op0=ALU.mult),
                              waits=[m1], sem=sV)
                    ax = P.op("scalar", lambda e:
                              e.activation(gs_scr[1][:, :], gs_scr[0][:, :],
                                           AF.Exp, bias=gm_scr[1][:, :]),
                              waits=[nm], sem=sSC)
                    prev_ax = ax
                    sm = P.op("vector", lambda e:
                              e.reduce_sum(gm_scr[2][:, :], gs_scr[1][:, :],
                                           axis=AX),
                              waits=[ax], sem=sV)
                    eq = P.op("vector", lambda e:
                              e.tensor_scalar(gs_scr[2][:, :], gs_scr[0][:, :],
                                              gm_scr[0][:, :], None,
                                              op0=ALU.is_ge),
                              waits=[m1], sem=sV)
                    cx = P.op("vector", lambda e:
                              e.scalar_tensor_tensor(gs_scr[3][:, :],
                                                     gs_scr[2][:, :], -1e30,
                                                     gs_scr[0][:, :],
                                                     op0=ALU.mult, op1=ALU.add),
                              waits=[eq], sem=sV)
                    m2 = P.op("vector", lambda e:
                              e.reduce_max(gm_scr[1][:, :], gs_scr[3][:, :],
                                           axis=AX),
                              waits=[cx], sem=sV)
                    msk = P.op("vector", lambda e:
                               e.tensor_scalar(gs_scr[2][:, :], gs_scr[0][:, :],
                                               gm_scr[1][:, :], None,
                                               op0=ALU.is_ge),
                               waits=[m2], sem=sV)
                    rc = P.op("vector", lambda e:
                              e.reciprocal(gm_scr[2][:, :], gm_scr[2][:, :]),
                              waits=[sm], sem=sV)
                    ws_ = P.op("vector", lambda e:
                               e.tensor_tensor(gs_scr[1][:, :], gs_scr[1][:, :],
                                               gs_scr[2][:, :], op=ALU.mult),
                               waits=[msk], sem=sV)
                    cb = P.op("vector", lambda e, s8=s8:
                              e.tensor_scalar(comb_tm[:, s8, :],
                                              gs_scr[1][:, :],
                                              gm_scr[2][:, :], None,
                                              op0=ALU.mult),
                              waits=[ws_, rc], sem=sV)
                    cb_toks.append(cb)
                ct = None
                for s8 in range(8):
                    ct = P.op("tensor", lambda e, s8=s8:
                              e.transpose(pd[0][0:8, s8 * 128:(s8 + 1) * 128],
                                          comb_tm[:, s8, :], ident128[:, :]),
                              waits=[cb_toks[s8], con_tok,
                                     pd_free[0] if s8 == 0 else None], sem=sPE)
                ct = P.op("vector", lambda e: e.tensor_copy(comb_fm[:, :],
                                                            pd[0][0:8, :]),
                          waits=[ct], sem=sV)
                pd_free[0] = (sV, sV.n)
                cw = P.op("scalar", lambda e:
                          e.dma_start(out=comb_dram[:, :], in_=comb_fm[:, :]),
                          waits=[ct, (dCB, dCB.n)], sem=dCBW, inc=16)
                cw_tok = (dCBW, dCBW.n)

                for grp, exs in enumerate(GRPS):
                    for el, ex in enumerate(exs):
                        if ex < 8:
                            P.op("sync", lambda e, ex=ex:
                                 e.dma_start(out=comb_row[:, :],
                                             in_=comb_dram[ex:ex + 1, :]),
                                 waits=[cw_tok, crow_free], sem=dCB, inc=16)
                            crd_tok = (dCB, dCB.n)
                            cbm = None
                            for cl in range(2):
                                cbm = P.op("tensor", lambda e, cl=cl:
                                           e.matmul(
                                               pg[:, cl * 512:(cl + 1) * 512],
                                               onesrow_bf[0:1, :],
                                               comb_row[0:1,
                                                        cl * 512:(cl + 1) * 512],
                                               start=True, stop=True,
                                               skip_group_check=True),
                                           waits=[crd_tok,
                                                  pg_free if cl == 0 else None],
                                           sem=sPE)
                            crow_free = (sPE, sPE.n)
                            cbv = P.op("vector", lambda e:
                                       e.tensor_copy(comb_cur[:, :], pg[:, :]),
                                       waits=[cbm], sem=sV)
                            pg_free = (sV, sV.n)
                        else:
                            cbv = None
                        for i8 in range(8):
                            wsl = i8 % 2
                            P.op("gpsimd", lambda e, ex=ex, i8=i8, wsl=wsl:
                                 e.dma_start(
                                     out=w13b[wsl][:, :, 0:128],
                                     in_=w1T[ex, :, i8 * 128:(i8 + 1) * 128]
                                     .rearrange("(c p) i -> p c i", p=128)),
                                 waits=[w13_free[wsl]], sem=dW13[wsl], inc=16)
                            P.op("gpsimd", lambda e, ex=ex, i8=i8, wsl=wsl:
                                 e.dma_start(
                                     out=w13b[wsl][:, :, 128:256],
                                     in_=w3T[ex, :, i8 * 128:(i8 + 1) * 128]
                                     .rearrange("(c p) i -> p c i", p=128)),
                                 waits=[], sem=dW13[wsl], inc=16)
                            wtok = (dW13[wsl], dW13[wsl].n)
                            hm = gm = None
                            for ic in range(16):
                                for cl in range(2):
                                    hm = P.op("tensor",
                                              lambda e, ic=ic, cl=cl, wsl=wsl:
                                              e.matmul(
                                                  ph[:, cl * 512:(cl + 1) * 512],
                                                  w13b[wsl][:, ic, 0:128],
                                                  x2bf[:, ic,
                                                       cl * 512:(cl + 1) * 512],
                                                  start=(ic == 0),
                                                  stop=(ic == 15),
                                                  skip_group_check=True),
                                              waits=[wtok, xhat_tok,
                                                     ph_free
                                                     if ic == 0 and cl == 0
                                                     else None],
                                              sem=sPE)
                            for ic in range(16):
                                for cl in range(2):
                                    gm = P.op("tensor",
                                              lambda e, ic=ic, cl=cl, wsl=wsl:
                                              e.matmul(
                                                  pg[:, cl * 512:(cl + 1) * 512],
                                                  w13b[wsl][:, ic, 128:256],
                                                  x2bf[:, ic,
                                                       cl * 512:(cl + 1) * 512],
                                                  start=(ic == 0),
                                                  stop=(ic == 15),
                                                  skip_group_check=True),
                                              waits=[pg_free
                                                     if ic == 0 and cl == 0
                                                     else None],
                                              sem=sPE)
                            w13_free[wsl] = (sPE, sPE.n)
                            am = None
                            for cl in range(2):
                                si = P.op("scalar", lambda e, cl=cl:
                                          e.activation(
                                              s1f[cl][:, :],
                                              ph[:, cl * 512:(cl + 1) * 512],
                                              AF.Silu),
                                          waits=[hm, s1_free[cl]], sem=sSC)
                                am = P.op("vector",
                                          lambda e, el=el, i8=i8, cl=cl:
                                          e.tensor_tensor(
                                              a3[el][:, i8,
                                                     cl * 512:(cl + 1) * 512],
                                              s1f[cl][:, :],
                                              pg[:, cl * 512:(cl + 1) * 512],
                                              op=ALU.mult),
                                          waits=[si, gm,
                                                 a3_free[el]
                                                 if i8 == 0 and cl == 0
                                                 else None],
                                          sem=sV)
                                s1_free[cl] = (sV, sV.n)
                            ph_free = (sSC, sSC.n)
                            pg_free = (sV, sV.n)
                            if ex < 8:
                                P.op("vector", lambda e, el=el, i8=i8:
                                     e.tensor_tensor(a3[el][:, i8, :],
                                                     a3[el][:, i8, :],
                                                     comb_cur[:, :],
                                                     op=ALU.mult),
                                     waits=[cbv], sem=sV)
                    a_tok = (sV, sV.n)
                    nel = len(exs)
                    for dc in range(16):
                        dsl = dc % 2
                        for el, ex in enumerate(exs):
                            P.op("gpsimd", lambda e, ex=ex, el=el, dc=dc,
                                 dsl=dsl:
                                 e.dma_start(
                                     out=w2b[dsl][:, :, el * 128:(el + 1) * 128],
                                     in_=w2T[ex, :, dc * 128:(dc + 1) * 128]
                                     .rearrange("(c p) d -> p c d", p=128)),
                                 waits=[w2_free[dsl] if el == 0 else None],
                                 sem=dW2[dsl], inc=16)
                        w2tok = (dW2[dsl], dW2[dsl].n)
                        bm = None
                        for el in range(nel):
                            for i8 in range(8):
                                for cl in range(2):
                                    first = (el == 0 and i8 == 0 and cl == 0)
                                    bm = P.op("tensor",
                                              lambda e, el=el, i8=i8, cl=cl,
                                              dsl=dsl, nel=nel:
                                              e.matmul(
                                                  pd[dsl][:, cl * 512:
                                                          (cl + 1) * 512],
                                                  w2b[dsl][:, i8,
                                                           el * 128:
                                                           (el + 1) * 128],
                                                  a3[el][:, i8,
                                                         cl * 512:
                                                         (cl + 1) * 512],
                                                  start=(i8 == 0 and el == 0),
                                                  stop=(i8 == 7 and
                                                        el == nel - 1),
                                                  skip_group_check=True),
                                              waits=[w2tok, a_tok,
                                                     pd_free[dsl] if first
                                                     else None],
                                              sem=sPE)
                        w2_free[dsl] = (sPE, sPE.n)
                        P.op("vector", lambda e, dc=dc, dsl=dsl:
                             e.tensor_tensor(x1[:, dc, :], x1[:, dc, :],
                                             pd[dsl][:, :], op=ALU.add),
                             waits=[bm], sem=sV)
                        pd_free[dsl] = (sV, sV.n)
                    for el in range(nel):
                        a3_free[el] = (sPE, sPE.n)
                fin = (sV, sV.n)
                P.op("scalar", lambda e, tl=tl:
                     e.dma_start(out=out[:, tl:tl + 1024].rearrange(
                         "(c p) t -> p c t", p=128), in_=x1[:, :, :]),
                     waits=[fin], sem=dOUT, inc=16)
                out_cnt += 1

            final_cnt = out_cnt * 16
            P.op("sync", lambda e: e.wait_ge(dOUT.h, final_cnt), waits=[])
            P.flush(blk)

    return nc


_NC = None


def _get_nc():
    global _NC
    if _NC is None:
        _NC = build_program()
    return _NC


def _host_consts():
    f32 = np.float32
    pos = np.arange(T, dtype=np.float64)
    inv = 10000.0 ** (-np.arange(64, dtype=np.float64) / 64.0)
    fr = np.outer(pos, inv)
    cos = np.cos(fr).T
    sin = np.sin(fr).T
    c2s2 = np.empty((256, T), f32)
    c2s2[0:64] = cos
    c2s2[64:128] = cos
    c2s2[128:192] = -sin
    c2s2[192:256] = sin
    perm = np.roll(np.eye(128, dtype=f32), 64, axis=0)
    masks = np.zeros((4, 128, 512), dtype=ml_dtypes.bfloat16)
    ki = np.arange(128)[:, None]
    qj = np.arange(512)[None, :]
    for d in range(4):
        masks[d] = (ki + 128 * d <= qj).astype(ml_dtypes.bfloat16)
    return dict(
        c2s2=c2s2,
        permM=perm,
        masksI=masks,
        onescol_bf=np.ones((128, 1), ml_dtypes.bfloat16),
        onescol_f=np.ones((128, 1), f32),
        onesrow_bf=np.ones((96, 128), ml_dtypes.bfloat16),
        onesrow_f=np.ones((96, 128), f32),
        ident128=np.eye(128, dtype=f32),
        ident8=np.eye(8, dtype=f32),
        ident1=np.ones((1, 1), f32),
        ident1b=np.ones((1, 1), ml_dtypes.bfloat16),
    )


def kernel(L0_attn_x, L0_attn_w_q, L0_attn_w_k, L0_attn_w_v, L0_attn_w_o,
           L0_ffn_gate_w, L0_ffn_w1, L0_ffn_w2, L0_ffn_w3,
           L0_ffn_ws1, L0_ffn_ws2, L0_ffn_ws3):
    f32 = np.float32
    x = np.asarray(L0_attn_x, f32)
    wq = np.asarray(L0_attn_w_q, f32)
    wk = np.asarray(L0_attn_w_k, f32)
    wv = np.asarray(L0_attn_w_v, f32)
    wo = np.asarray(L0_attn_w_o, f32)
    gw = np.asarray(L0_ffn_gate_w, f32)
    w1 = np.asarray(L0_ffn_w1, f32)
    w2 = np.asarray(L0_ffn_w2, f32)
    w3 = np.asarray(L0_ffn_w3, f32)
    ws1 = np.asarray(L0_ffn_ws1, f32)
    ws2 = np.asarray(L0_ffn_ws2, f32)
    ws3 = np.asarray(L0_ffn_ws3, f32)

    in_maps = _make_in_maps(x, wq, wk, wv, wo, gw, w1, w2, w3, ws1, ws2, ws3)

    nc = _get_nc()
    res = run_bass_kernel_spmd(nc, in_maps, core_ids=list(range(8)))
    outp = np.empty((4, T, D), f32)
    for c in range(8):
        b, h = c // 2, c % 2
        outp[b, h * 2048:(h + 1) * 2048, :] = np.asarray(res.results[c]["out"]).T
    return outp


def _make_in_maps(x, wq, wk, wv, wo, gw, w1, w2, w3, ws1, ws2, ws3):
    w1T = np.ascontiguousarray(np.concatenate(
        [np.transpose(w1, (0, 2, 1)), ws1.T[None]], axis=0))
    w3T = np.ascontiguousarray(np.concatenate(
        [np.transpose(w3, (0, 2, 1)), ws3.T[None]], axis=0))
    w2T = np.ascontiguousarray(np.concatenate(
        [np.transpose(w2, (0, 2, 1)), ws2.T[None]], axis=0))
    gwT = np.ascontiguousarray(gw.T)
    consts = _host_consts()

    in_maps = []
    for c in range(8):
        b, h = c // 2, c % 2
        m = dict(consts)
        m["xT"] = np.ascontiguousarray(x[b].T)
        m["xTo"] = np.ascontiguousarray(x[b, h * 2048:(h + 1) * 2048, :].T)
        gp = c // 2
        m["wqT"] = np.ascontiguousarray(
            wq[h * 1024:(h + 1) * 1024, :].T[512 * gp:512 * (gp + 1), :])
        m["wkT"] = np.ascontiguousarray(
            wk[h * 1024:(h + 1) * 1024, :].T[512 * gp:512 * (gp + 1), :])
        m["wvT"] = np.ascontiguousarray(
            wv[h * 1024:(h + 1) * 1024, :].T[512 * gp:512 * (gp + 1), :])
        m["woT"] = np.ascontiguousarray(
            wo[:, h * 1024:(h + 1) * 1024].T[256 * gp:256 * (gp + 1), :])
        m["gwT"] = gwT
        m["w1T"] = w1T
        m["w3T"] = w3T
        m["w2T"] = w2T
        in_maps.append(m)
    return in_maps


# version 12
# speedup vs baseline: 1.0455x; 1.0084x over previous
import contextlib
import numpy as np
import ml_dtypes
import concourse.bass as bass
import concourse.mybir as mybir
from concourse.bass_utils import run_bass_kernel_spmd

F32 = mybir.dt.float32
BF16 = mybir.dt.bfloat16
AF = mybir.ActivationFunctionType
ALU = mybir.AluOpType
AX = mybir.AxisListType.X

ENGINES = ("sync", "scalar", "tensor", "vector", "gpsimd")
ATT_SCALE = float(1.0 / np.sqrt(128.0))
D = 2048
T = 4096
NB1 = 16
B1 = 256
NQB = 8


class Sem:
    __slots__ = ("h", "n")

    def __init__(self, h):
        self.h = h
        self.n = 0


def _flat(ws, out):
    for w in ws:
        if w is None:
            continue
        if isinstance(w, list):
            _flat(w, out)
        else:
            out.append(w)


class Prog:
    def __init__(self):
        self.q = {e: [] for e in ENGINES}
        self.wm = {e: {} for e in ENGINES}

    def op(self, eng, fn, waits=(), sem=None, inc=1):
        raw = []
        _flat(list(waits), raw)
        ws = []
        for s, v in raw:
            if v <= 0:
                continue
            k = id(s)
            if self.wm[eng].get(k, 0) >= v:
                continue
            self.wm[eng][k] = v
            ws.append((s, v))
        self.q[eng].append((fn, ws, sem, inc))
        if sem is None:
            return None
        sem.n += inc
        return (sem, sem.n)

    def fence(self, eng, toks):
        self.op(eng, lambda e: None, waits=toks, sem=None)

    def flush(self, blk):
        for e in ENGINES:
            ops = self.q[e]
            if not ops:
                continue
            self.q[e] = []

            def mk(ops):
                def body(eng):
                    for fn, ws, sem, inc in ops:
                        for s, v in ws:
                            eng.wait_ge(s.h, v)
                        inst = fn(eng)
                        if sem is not None:
                            inst.then_inc(sem.h, inc)
                return body
            getattr(blk, e)(mk(ops))


def build_program(dbg=False, upto="full"):
    nc = bass.Bass("TRN2", target_bir_lowering=False)

    xT = nc.dram_tensor("xT", [D, T], F32, kind="ExternalInput")
    xTo = nc.dram_tensor("xTo", [D, D], F32, kind="ExternalInput")
    wqT = nc.dram_tensor("wqT", [512, 1024], F32, kind="ExternalInput")
    wkT = nc.dram_tensor("wkT", [512, 1024], F32, kind="ExternalInput")
    wvT = nc.dram_tensor("wvT", [512, 1024], F32, kind="ExternalInput")
    woT = nc.dram_tensor("woT", [256, D], F32, kind="ExternalInput")
    gwT = nc.dram_tensor("gwT", [D, 8], F32, kind="ExternalInput")
    w1T = nc.dram_tensor("w1T", [9, D, 1024], F32, kind="ExternalInput")
    w3T = nc.dram_tensor("w3T", [9, D, 1024], F32, kind="ExternalInput")
    w2T = nc.dram_tensor("w2T", [9, 1024, D], F32, kind="ExternalInput")
    c2s2 = nc.dram_tensor("c2s2", [256, T], F32, kind="ExternalInput")
    permM = nc.dram_tensor("permM", [128, 128], F32, kind="ExternalInput")
    masksI = nc.dram_tensor("masksI", [4, 128, 512], BF16, kind="ExternalInput")
    onescol_bf_i = nc.dram_tensor("onescol_bf", [128, 1], BF16, kind="ExternalInput")
    onescol_f_i = nc.dram_tensor("onescol_f", [128, 1], F32, kind="ExternalInput")
    onesrow_bf_i = nc.dram_tensor("onesrow_bf", [96, 128], BF16, kind="ExternalInput")
    onesrow_f_i = nc.dram_tensor("onesrow_f", [96, 128], F32, kind="ExternalInput")
    ident128_i = nc.dram_tensor("ident128", [128, 128], F32, kind="ExternalInput")
    ident8_i = nc.dram_tensor("ident8", [8, 8], F32, kind="ExternalInput")
    ident1_i = nc.dram_tensor("ident1", [1, 1], F32, kind="ExternalInput")
    ident1b_i = nc.dram_tensor("ident1b", [1, 1], BF16, kind="ExternalInput")

    out = nc.dram_tensor("out", [D, D], F32, kind="ExternalOutput")

    skind = dict(kind="ExternalOutput") if dbg else {}
    xh1_sp = nc.dram_tensor("xh1_sp", [16, 128, T], BF16, **skind)
    q_sp = nc.dram_tensor("q_sp", [8, 128, T], BF16, **skind)
    k_sp = nc.dram_tensor("k_sp", [8, 128, T], BF16, **skind)
    v_sp = nc.dram_tensor("v_sp", [8, 128, T], BF16, **skind)
    o_part = nc.dram_tensor("o_part", [4, 4096, 512], F32)
    or_red = nc.dram_tensor("or_red", [4, 2048, 512], F32)
    or_dbg = (nc.dram_tensor("or_dbg", [4, 2048, 512], F32,
                             kind="ExternalOutput") if dbg else None)
    comb_dram = nc.dram_tensor("comb_dram", [8, 1024], BF16, **skind)
    cw_in = {"q": nc.dram_tensor("cwq_in", [512, 1024], BF16),
             "k": nc.dram_tensor("cwk_in", [512, 1024], BF16),
             "v": nc.dram_tensor("cwv_in", [512, 1024], BF16),
             "o": nc.dram_tensor("cwo_in", [256, D], BF16)}
    cw_out = {"q": nc.dram_tensor("cwq_out", [D, 1024], BF16),
              "k": nc.dram_tensor("cwk_out", [D, 1024], BF16),
              "v": nc.dram_tensor("cwv_out", [D, 1024], BF16),
              "o": nc.dram_tensor("cwo_out", [1024, D], BF16)}

    P = Prog()

    with contextlib.ExitStack() as glob:
        ec = glob.enter_context
        sPE = Sem(ec(nc.semaphore("sPE")))
        sPL = Sem(ec(nc.semaphore("sPL")))
        sV = Sem(ec(nc.semaphore("sV")))
        sSC = Sem(ec(nc.semaphore("sSC")))
        dCON = Sem(ec(nc.semaphore("dCON")))
        dST = [Sem(ec(nc.semaphore(f"dST{i}"))) for i in range(2)]
        dXB = [Sem(ec(nc.semaphore(f"dXB{i}"))) for i in range(4)]
        dSPL = Sem(ec(nc.semaphore("dSPL")))
        dQB = [Sem(ec(nc.semaphore(f"dQB{i}"))) for i in range(2)]
        dKV = [Sem(ec(nc.semaphore(f"dKV{i}"))) for i in range(3)]
        dOP = [Sem(ec(nc.semaphore(f"dOP{i}"))) for i in range(4)]
        sCC = [Sem(ec(nc.semaphore(f"sCC{i}"))) for i in range(4)]
        dW13 = [Sem(ec(nc.semaphore(f"dW13{i}"))) for i in range(2)]
        dW2 = [Sem(ec(nc.semaphore(f"dW2{i}"))) for i in range(2)]
        dX3 = [Sem(ec(nc.semaphore(f"dX3{i}"))) for i in range(2)]
        dCB = Sem(ec(nc.semaphore("dCB")))
        sCW = [Sem(ec(nc.semaphore(f"sCW{i}"))) for i in range(4)]
        dWG = [Sem(ec(nc.semaphore(f"dWG{i}"))) for i in range(3)]
        dCBW = Sem(ec(nc.semaphore("dCBW")))
        dOUT = Sem(ec(nc.semaphore("dOUT")))

        onescol_bf = ec(nc.sbuf_tensor("onescol_bf_s", [128, 1], BF16))
        onescol_f = ec(nc.sbuf_tensor("onescol_f_s", [128, 1], F32))
        onesrow_bf = ec(nc.sbuf_tensor("onesrow_bf_s", [96, 128], BF16))
        onesrow_f = ec(nc.sbuf_tensor("onesrow_f_s", [96, 128], F32))
        ident128 = ec(nc.sbuf_tensor("ident128_s", [128, 128], F32))
        ident8 = ec(nc.sbuf_tensor("ident8_s", [8, 8], F32))
        ident1 = ec(nc.sbuf_tensor("ident1_s", [1, 1], F32))
        ident1b = ec(nc.sbuf_tensor("ident1b_s", [1, 1], BF16))
        perm_bf = ec(nc.sbuf_tensor("perm_bf", [128, 128], BF16))

        blk = glob.enter_context(nc.Block())

        for dst, src in [
            (onescol_bf[:, :], onescol_bf_i[:, :]),
            (onescol_f[:, :], onescol_f_i[:, :]),
            (onesrow_bf[:, :], onesrow_bf_i[:, :]),
            (onesrow_f[:, :], onesrow_f_i[:, :]),
            (ident128[:, :], ident128_i[:, :]),
            (ident8[:, :], ident8_i[:, :]),
            (ident1[:, :], ident1_i[:, :]),
            (ident1b[:, :], ident1b_i[:, :]),
        ]:
            P.op("sync", lambda e, d=dst, s=src: e.dma_start(out=d, in_=s),
                 sem=dCON, inc=16)
        P.op("gpsimd", lambda e: e.dma_start(out=perm_bf[:, :], in_=permM[:, :]),
             sem=dCON, inc=16)
        con_tok = (dCON, dCON.n)

        def self_tile(i, ic, oc, sl, ld, t_abs, ctx, w_bf, spill=None):
            stage, a_bf, s_scr, g1_scr, g2_scr, thr_rows, p_dqs, p_thr = ctx["bufs"]
            t_sum = P.op("tensor", lambda e, sl=sl:
                         e.matmul(p_dqs[0:1, 0:512],
                                  onescol_bf[:, :], a_bf[sl][:, :],
                                  start=True, stop=True, skip_group_check=True),
                         waits=[t_abs, con_tok, ctx["dqs_free"]], sem=sPE)
            ctx["abs_free"][sl] = (sPE, sPE.n)
            t_thr = P.op("vector", lambda e, sl=sl:
                         e.tensor_scalar(thr_rows[sl][:, :],
                                         p_dqs[0:1, 0:512],
                                         2.0 ** -8, 5e-9,
                                         op0=ALU.mult, op1=ALU.max),
                         waits=[t_sum], sem=sV)
            ctx["dqs_free"] = (sV, sV.n)
            t_bc = P.op("tensor", lambda e, sl=sl:
                        e.matmul(p_thr[sl][:, :], onesrow_f[0:1, :],
                                 thr_rows[sl][:, :],
                                 start=True, stop=True, skip_group_check=True),
                        waits=[t_thr, ctx["thr_free"][sl]], sem=sPE)
            t_cp = P.op("vector", lambda e, sl=sl:
                        e.tensor_copy(thr_sb[sl][:, :], p_thr[sl][:, :]),
                        waits=[t_bc, ctx["thrsb_free"][sl]], sem=sV)
            ctx["thr_free"][sl] = (sV, sV.n)
            t_s1 = P.op("gpsimd", lambda e, sl=sl:
                        e.tensor_tensor(s_scr[0][:, :], stage[sl][:, :],
                                        thr_sb[sl][:, :], op=ALU.subtract),
                        waits=[t_cp, ctx["sscr_free"][0]], sem=sPL)
            t_g1 = P.op("scalar", lambda e:
                        e.activation(g1_scr[:, :], s_scr[0][:, :], AF.Sign),
                        waits=[t_s1], sem=sSC)
            t_s2 = P.op("gpsimd", lambda e, sl=sl:
                        e.tensor_tensor(s_scr[1][:, :], stage[sl][:, :],
                                        thr_sb[sl][:, :], op=ALU.add),
                        waits=[t_g1, ctx["sscr_free"][1]], sem=sPL)
            ctx["stage_free"][sl] = (sPL, sPL.n)
            ctx["thrsb_free"][sl] = (sPL, sPL.n)
            t_g2 = P.op("scalar", lambda e:
                        e.activation(g2_scr[:, :], s_scr[1][:, :], AF.Sign),
                        waits=[t_s2], sem=sSC)
            ctx["sscr_free"][1] = (sSC, sSC.n)
            t_h = P.op("vector", lambda e:
                       e.tensor_tensor(s_scr[0][:, :], g1_scr[:, :],
                                       g2_scr[:, :], op=ALU.add),
                       waits=[t_g2], sem=sV)
            ctx["sscr_free"][0] = (sV, sV.n)
            if spill is None:
                t_wq = P.op("vector", lambda e, sl=sl, ic=ic, oc=oc:
                            e.tensor_tensor(w_bf[:, ic, oc * 512:(oc + 1) * 512],
                                            s_scr[0][:, :], thr_sb[sl][:, :],
                                            op=ALU.mult),
                            waits=[t_h], sem=sV)
                ctx["thrsb_free"][sl] = [ctx["thrsb_free"][sl], (sV, sV.n)]
                ctx["sscr_free"][0] = (sV, sV.n)
                return t_wq
            slots, spill_dram = spill
            t_wq = P.op("vector", lambda e, sl=sl:
                        e.tensor_tensor(slots[sl][:, :],
                                        s_scr[0][:, :], thr_sb[sl][:, :],
                                        op=ALU.mult),
                        waits=[t_h, ctx["wqs_free"][sl]], sem=sV)
            ctx["thrsb_free"][sl] = [ctx["thrsb_free"][sl], (sV, sV.n)]
            ctx["sscr_free"][0] = (sV, sV.n)
            sp = P.op("scalar", lambda e, sl=sl, ic=ic, oc=oc:
                      e.dma_start(out=spill_dram[ic * 128:(ic + 1) * 128,
                                                 oc * 512:(oc + 1) * 512],
                                  in_=slots[sl][:, :]),
                      waits=[t_wq], sem=dSPL, inc=16)
            ctx["wqs_free"][sl] = sp
            return sp

        def dequant_gen(w_in, w_bf, IC, OC, ctx, spill=None):
            tiles = [(ic, oc) for oc in range(OC // 512) for ic in range(IC)]
            stage = ctx["bufs"][0]
            a_bf = ctx["bufs"][1]
            for i, (ic, oc) in enumerate(tiles):
                sl = i % 2
                ld = P.op("sync", lambda e, sl=sl, ic=ic, oc=oc, w_in=w_in:
                          e.dma_start(out=stage[sl][:, :],
                                      in_=w_in[ic * 128:(ic + 1) * 128,
                                               oc * 512:(oc + 1) * 512]),
                          waits=[ctx["stage_free"][sl]], sem=dST[sl], inc=16)
                t_abs = P.op("scalar", lambda e, sl=sl:
                             e.activation(a_bf[sl][:, :], stage[sl][:, :],
                                          AF.Abs),
                             waits=[ld, ctx["abs_free"][sl]], sem=sSC)
                yield self_tile(i, ic, oc, sl, ld, t_abs, ctx, w_bf, spill)

        def dequant2(w_in, w_bf, IC, OC, ctx, spill=None):
            last = None
            for last in dequant_gen(w_in, w_bf, IC, OC, ctx, spill):
                pass
            return last

        with contextlib.ExitStack() as sa:
            ea = sa.enter_context
            wq_bf = ea(nc.sbuf_tensor("wq_bf", [128, 16, 1024], BF16))
            wk_bf = ea(nc.sbuf_tensor("wk_bf", [128, 16, 1024], BF16))
            wv_bf = ea(nc.sbuf_tensor("wv_bf", [128, 16, 1024], BF16))
            stage = [ea(nc.sbuf_tensor(f"stage{i}", [128, 512], F32)) for i in range(2)]
            a_bf = [ea(nc.sbuf_tensor(f"a_bf{i}", [128, 512], BF16)) for i in range(2)]
            s_scr = [ea(nc.sbuf_tensor(f"s_scr{i}", [128, 512], F32)) for i in range(2)]
            g1_scr = ea(nc.sbuf_tensor("g1_scr", [128, 512], F32))
            g2_scr = ea(nc.sbuf_tensor("g2_scr", [128, 512], F32))
            thr_rows = [ea(nc.sbuf_tensor(f"thr_r{i}", [1, 512], F32)) for i in range(2)]
            x_bf = [ea(nc.sbuf_tensor(f"x_bf{i}", [128, 16, B1], BF16)) for i in range(4)]
            xh = [ea(nc.sbuf_tensor(f"xh{i}", [128, 16, B1], BF16)) for i in range(2)]
            sq_scr = [ea(nc.sbuf_tensor(f"sq_scr{i}", [128, B1], BF16)) for i in range(2)]
            prj_bf = [ea(nc.sbuf_tensor(f"prj_bf{i}", [128, B1], BF16)) for i in range(2)]
            swp_bf = [ea(nc.sbuf_tensor(f"swp_bf{i}", [128, B1], BF16)) for i in range(2)]
            wqs = [ea(nc.sbuf_tensor(f"wqs{i}", [128, 512], BF16)) for i in range(2)]
            thr_sb = [ea(nc.sbuf_tensor(f"thr_sb{i}", [128, 512], F32)) for i in range(2)]
            c2_bf = ea(nc.sbuf_tensor("c2_bf", [128, T], BF16))
            s2t_bf = ea(nc.sbuf_tensor("s2t_bf", [128, T], BF16))
            rows_r = ea(nc.sbuf_tensor("rows_r", [1, T], BF16))
            rowf1 = ea(nc.sbuf_tensor("rowf1", [96, B1], F32))
            rowf2 = ea(nc.sbuf_tensor("rowf2", [64, B1], F32))
            rowbf = ea(nc.sbuf_tensor("rowbf", [64, B1], BF16))
            m_sb = ea(nc.sbuf_tensor("m_sb", [128, B1], BF16))
            r_sb = ea(nc.sbuf_tensor("r_sb", [128, B1], F32))

            p_swp = [ea(nc.psum_tensor(f"p_swp{i}", [128, B1], F32)) for i in range(3)]
            p_stats = [ea(nc.psum_tensor(f"p_stats{i}", [1, 512], F32)) for i in range(2)]
            p_dq = ea(nc.psum_tensor("p_dq", [1, 512], F32))
            p_thr = [ea(nc.psum_tensor(f"p_thr{i}", [128, 512], F32)) for i in range(2)]

            ln_t0 = [0]
            r_tok = [None]
            rows_r_free = None
            mrow = rowf1[0:1, :]
            msqr = rowf1[32:33, :]
            varr = rowf1[64:65, :]
            rstdr = rowf2[0:1, :]
            mrr = rowf2[32:33, :]
            rstdr_bf = rowbf[0:1, :]
            mrr_bf = rowbf[32:33, :]

            P.op("gpsimd", lambda e: e.dma_start(out=c2_bf[:, :],
                                                 in_=c2s2[0:128, :]),
                 sem=dCON, inc=16)
            P.op("gpsimd", lambda e: e.dma_start(out=s2t_bf[:, :],
                                                 in_=c2s2[128:256, :]),
                 sem=dCON, inc=16)
            rope_tok = (dCON, dCON.n)

            dq_ctx = {"bufs": (stage, a_bf, s_scr, g1_scr, g2_scr, thr_rows,
                               p_dq, p_thr),
                      "stage_free": [None, None], "abs_free": [None, None],
                      "dqs_free": None, "thr_free": [None, None],
                      "wqs_free": [None, None],
                      "thrsb_free": [None, None], "sscr_free": [None, None]}
            xb_free = [None, None, None, None]
            xld_toks = {}
            for i in range(4):
                xld_toks[i] = P.op("gpsimd", lambda e, sl=i, t0=i * B1:
                                   e.dma_start(out=x_bf[sl][:, :, :],
                                               in_=xT[:, t0:t0 + B1].rearrange(
                                                   "(c p) t -> p c t", p=128)),
                                   waits=[], sem=dXB[i], inc=16)

            wg_groups = [[0, 2, 4, 6], [1, 3, 5, 7]]
            gather_toks = {}
            for wi, (nm, w_in, ICg) in enumerate(
                    [("q", wqT, 4), ("k", wkT, 4), ("v", wvT, 4),
                     ("o", woT, 2)]):
                OCg = 1024 if nm != "o" else 2048
                dequant2(w_in[:, :], None, ICg, OCg, dq_ctx,
                         spill=(wqs, cw_in[nm]))
                gather_toks[nm] = P.op(
                    "gpsimd", lambda e, nm=nm:
                    e.collective_compute("AllGather", ALU.bypass,
                                         replica_groups=wg_groups,
                                         ins=[cw_in[nm][:, :]],
                                         outs=[cw_out[nm][:, :]]),
                    waits=[(dSPL, dSPL.n)], sem=sCW[wi])
            ld_q = P.op("sync", lambda e: e.dma_start(
                out=wq_bf[:, :, :],
                in_=cw_out["q"][:, :].rearrange("(c p) n -> p c n", p=128)),
                waits=[gather_toks["q"]], sem=dWG[0], inc=16)
            ld_k = P.op("sync", lambda e: e.dma_start(
                out=wk_bf[:, :, :],
                in_=cw_out["k"][:, :].rearrange("(c p) n -> p c n", p=128)),
                waits=[gather_toks["k"]], sem=dWG[1], inc=16)
            ld_v = P.op("sync", lambda e: e.dma_start(
                out=wv_bf[:, :, :],
                in_=cw_out["v"][:, :].rearrange("(c p) n -> p c n", p=128)),
                waits=[gather_toks["v"]], sem=dWG[2], inc=16)
            dq_wq = ld_q
            dq_kv_tok = [ld_k, ld_v]

            xh_free = [None, None]
            psw_free = [None, None, None]
            prj_free = [None, None]
            swb_free = [None, None]
            stats_free = [None, None]
            pv_free = None
            sq_free = [None, None]
            dqt = dq_ctx

            def ln_stats(par, xb_sl, xld):
                t = None
                for ic in range(16):
                    t = P.op("tensor", lambda e, ic=ic, sl=xb_sl, par=par:
                             e.matmul(p_stats[par][0:1, 0:B1], onescol_bf[:, :],
                                      x_bf[sl][:, ic, :], start=(ic == 0),
                                      stop=(ic == 15), skip_group_check=True),
                             waits=[xld, con_tok,
                                    stats_free[par] if ic == 0 else None],
                             sem=sPE)
                mean_mm = t
                for ic in range(16):
                    sl2 = ic % 2
                    tsq = P.op("vector", lambda e, ic=ic, sl=xb_sl, sl2=sl2:
                               e.tensor_tensor(sq_scr[sl2][:, :],
                                               x_bf[sl][:, ic, :],
                                               x_bf[sl][:, ic, :], op=ALU.mult),
                               waits=[xld, sq_free[sl2]], sem=sV)
                    t = P.op("tensor", lambda e, ic=ic, sl2=sl2, par=par:
                             e.matmul(p_stats[par][0:1, B1:2 * B1],
                                      onescol_bf[:, :],
                                      sq_scr[sl2][:, :], start=(ic == 0),
                                      stop=(ic == 15), skip_group_check=True),
                             waits=[tsq], sem=sPE)
                    sq_free[sl2] = (sPE, sPE.n)
                return (mean_mm, t)

            def ln_norm(par, xb_sl, xh_sl, mean_mm, sq_mm):
                t = P.op("vector", lambda e, par=par:
                         e.tensor_scalar(mrow, p_stats[par][0:1, 0:B1],
                                         1.0 / 2048.0, None, op0=ALU.mult),
                         waits=[mean_mm, sq_mm], sem=sV)
                t = P.op("vector", lambda e:
                         e.tensor_tensor(msqr, mrow, mrow, op=ALU.mult),
                         waits=[t], sem=sV)
                t = P.op("vector", lambda e, par=par:
                         e.scalar_tensor_tensor(varr, p_stats[par][0:1, B1:2 * B1],
                                                1.0 / 2048.0, msqr,
                                                op0=ALU.mult, op1=ALU.subtract),
                         waits=[t], sem=sV)
                stats_free[par] = (sV, sV.n)
                t = P.op("vector", lambda e:
                         e.tensor_scalar(varr, varr, 1e-5, None, op0=ALU.add),
                         waits=[t], sem=sV)
                t = P.op("scalar", lambda e:
                         e.activation(varr, varr, AF.Ln),
                         waits=[t], sem=sSC)
                t = P.op("scalar", lambda e:
                         e.activation(rstdr, varr, AF.Exp, scale=-0.5),
                         waits=[t], sem=sSC)
                t1 = P.op("vector", lambda e, t0=ln_t0[0]:
                          e.tensor_copy(rows_r[0:1, t0:t0 + B1], rstdr),
                          waits=[t, rows_r_free], sem=sV)

                t2 = P.op("vector", lambda e: e.tensor_copy(mrr_bf, mrow),
                          waits=[t], sem=sV)
                b1 = P.op("tensor", lambda e:
                          e.matmul(p_thr[0][:, 0:B1], onesrow_bf[32:33, :],
                                   mrr_bf,
                                   start=True, stop=True, skip_group_check=True),
                          waits=[t2, dqt["thr_free"][0]], sem=sPE)
                b2 = P.op("tensor", lambda e, t0=ln_t0[0]:
                          e.matmul(p_thr[1][:, 0:B1], onesrow_bf[0:1, :],
                                   rows_r[0:1, t0:t0 + B1],
                                   start=True, stop=True, skip_group_check=True),
                          waits=[t1, dqt["thr_free"][1]], sem=sPE)
                c1 = P.op("vector", lambda e: e.tensor_copy(m_sb[:, :],
                                                            p_thr[0][:, 0:B1]),
                          waits=[b1], sem=sV)
                dqt["thr_free"][0] = (sV, sV.n)
                c2t = P.op("vector", lambda e: e.tensor_copy(r_sb[:, :],
                                                             p_thr[1][:, 0:B1]),
                           waits=[b2], sem=sV)
                dqt["thr_free"][1] = (sV, sV.n)
                r_tok[0] = c2t
                last = None
                for ic in range(16):
                    eng = "gpsimd" if ic % 3 == 2 else "vector"
                    sem_ = sPL if ic % 3 == 2 else sV
                    last = P.op(eng, lambda e, ic=ic, sl=xb_sl, so=xh_sl:
                                e.tensor_tensor(xh[so][:, ic, :],
                                                x_bf[sl][:, ic, :],
                                                m_sb[:, :], op=ALU.subtract),
                                waits=[c1, xh_free[xh_sl] if ic == 0 else None],
                                sem=sem_)
                last = [(sV, sV.n), (sPL, sPL.n)]
                return last

            def proj_rope(w_bf, xh_sl, xh_tok, t0, dst_sp, do_rope):
                def emit_sw(hc, pp, pb, ev):
                    sw = P.op("tensor", lambda e, pb=pb, pp=pp:
                              e.matmul(p_swp[pb][:, :],
                                       perm_bf[:, :], prj_bf[pp][:, :],
                                       start=True, stop=True,
                                       skip_group_check=True),
                              waits=[ev, con_tok], sem=sPE)
                    sv = P.op("vector", lambda e, pp=pp, pb=pb:
                              e.tensor_copy(swp_bf[pp][:, :],
                                            p_swp[pb][:, :]),
                              waits=[sw, swb_free[pp]], sem=sV)
                    psw_free[pb] = (sV, sV.n)
                    t1 = P.op("gpsimd", lambda e, pp=pp, t0=t0:
                              e.tensor_tensor(prj_bf[pp][:, :], prj_bf[pp][:, :],
                                              c2_bf[:, t0:t0 + B1], op=ALU.mult),
                              waits=[sw, rope_tok], sem=sPL)
                    t2 = P.op("gpsimd", lambda e, pp=pp, t0=t0:
                              e.tensor_tensor(swp_bf[pp][:, :], swp_bf[pp][:, :],
                                              s2t_bf[:, t0:t0 + B1], op=ALU.mult),
                              waits=[sv], sem=sPL)
                    t3 = P.op("gpsimd", lambda e, pp=pp:
                              e.tensor_tensor(swp_bf[pp][:, :], prj_bf[pp][:, :],
                                              swp_bf[pp][:, :], op=ALU.add),
                              waits=[t1, t2], sem=sPL)
                    prj_free[pp] = (sPL, sPL.n)
                    st = P.op("scalar", lambda e, hc=hc, t0=t0, pp=pp,
                              dst_sp=dst_sp:
                              e.dma_start(out=dst_sp[hc, :, t0:t0 + B1],
                                          in_=swp_bf[pp][:, :]),
                              waits=[t3], sem=dSPL, inc=16)
                    swb_free[pp] = (dSPL, dSPL.n)

                pend_sw = None
                for hc in range(8):
                    pp = hc % 2
                    pb = hc % 3
                    mt = None
                    for ic in range(16):
                        w = [xh_tok]
                        if ic == 0:
                            w.append(psw_free[pb])
                        mt = P.op("tensor", lambda e, hc=hc, ic=ic, sl=xh_sl,
                                  pp=pb:
                                  e.matmul(p_swp[pp][:, :],
                                           w_bf[:, ic, hc * 128:(hc + 1) * 128],
                                           xh[sl][:, ic, :], start=(ic == 0),
                                           stop=(ic == 15),
                                           skip_group_check=True),
                                  waits=w, sem=sPE)
                    ev = P.op("vector", lambda e, pp=pp, pb=pb:
                              e.tensor_tensor(prj_bf[pp][:, :], p_swp[pb][:, :],
                                              r_sb[:, :], op=ALU.mult),
                              waits=[mt, prj_free[pp], r_tok[0]], sem=sV)
                    if not do_rope:
                        psw_free[pb] = (sV, sV.n)
                        st = P.op("scalar", lambda e, hc=hc, t0=t0, pp=pp,
                                  dst_sp=dst_sp:
                                  e.dma_start(out=dst_sp[hc, :, t0:t0 + B1],
                                              in_=prj_bf[pp][:, :]),
                                  waits=[ev], sem=dSPL, inc=16)
                        prj_free[pp] = [(sV, sV.n), (dSPL, dSPL.n)]
                        continue
                    if pend_sw is not None:
                        emit_sw(*pend_sw)
                    pend_sw = (hc, pp, pb, ev)
                if pend_sw is not None:
                    emit_sw(*pend_sw)

            def xload(b):
                sl = b % 4
                t0 = b * B1
                return P.op("gpsimd", lambda e, sl=sl, t0=t0:
                            e.dma_start(out=x_bf[sl][:, :, :],
                                        in_=xT[:, t0:t0 + B1].rearrange(
                                            "(c p) t -> p c t", p=128)),
                            waits=[xb_free[sl]], sem=dXB[sl], inc=16)

            for b in range(NB1 + 1):
                if b < NB1:
                    stats_toks_b = ln_stats(b % 2, b % 4, xld_toks.pop(b))
                if b > 0:
                    pend = b - 1
                    pxl = pend % 4
                    psl = pend % 2
                    pt0 = pend * B1
                    mm_m, mm_s = pend_stats
                    ln_t0[0] = pt0
                    xht = ln_norm(pend % 2, pxl, psl, mm_m, mm_s)
                    xb_free[pxl] = [(sV, sV.n), (sPL, sPL.n), (sPE, sPE.n)]
                    P.op("scalar", lambda e, psl=psl, pt0=pt0:
                         e.dma_start(out=xh1_sp[:, :, pt0:pt0 + B1].rearrange(
                             "c p t -> p c t"), in_=xh[psl][:, :, :]),
                         waits=[xht], sem=dSPL, inc=16)
                    if pend + 4 < NB1:
                        xld_toks[pend + 4] = xload(pend + 4)
                    if pend == 0:
                        xht = [xht, dq_wq]
                    proj_rope(wq_bf, psl, xht, pt0, q_sp, True)
                    xh_free[psl] = [(sPE, sPE.n), (dSPL, dSPL.n)]
                if b < NB1:
                    pend_stats = stats_toks_b

            for b in range(NB1):
                t0 = b * B1
                sl = b % 2
                xrd = P.op("sync", lambda e, sl=sl, t0=t0:
                           e.dma_start(out=xh[sl][:, :, :],
                                       in_=xh1_sp[:, :, t0:t0 + B1].rearrange(
                                           "c p t -> p c t")),
                           waits=[xh_free[sl],
                                  (dSPL, dSPL.n) if b == 0 else None,
                                  dq_kv_tok if b == 0 else None],
                           sem=dXB[sl], inc=16)
                xh_tok = (dXB[sl], dXB[sl].n)
                b2 = P.op("tensor", lambda e, t0=t0:
                          e.matmul(p_thr[1][:, 0:B1], onesrow_bf[0:1, :],
                                   rows_r[0:1, t0:t0 + B1],
                                   start=True, stop=True, skip_group_check=True),
                          waits=[dqt["thr_free"][1],
                                 (dSPL, dSPL.n) if b == 0 else None], sem=sPE)
                c2t = P.op("vector", lambda e: e.tensor_copy(r_sb[:, :],
                                                             p_thr[1][:, 0:B1]),
                           waits=[b2, r_tok[0]], sem=sV)
                dqt["thr_free"][1] = (sV, sV.n)
                r_tok[0] = c2t
                proj_rope(wk_bf, sl, xh_tok, t0, k_sp, True)
                proj_rope(wv_bf, sl, xh_tok, t0, v_sp, False)
                xh_free[sl] = [(sPE, sPE.n)]

            p1 = dict(pe=(sPE, sPE.n), v=(sV, sV.n), sc=(sSC, sSC.n),
                      spl=(dSPL, dSPL.n))
            if upto == "p1":
                nspl = dSPL.n
                P.op("sync", lambda e: e.wait_ge(dSPL.h, nspl),
                     waits=[p1["pe"], p1["v"], p1["sc"]])
            P.flush(blk)
        if upto == "p1":
            return nc

        with contextlib.ExitStack() as sb:
            eb = sb.enter_context
            wo_bf = eb(nc.sbuf_tensor("wo_bf", [128, 8, 2048], BF16))
            masks = eb(nc.sbuf_tensor("masks_s", [128, 4, 512], BF16))
            qblk = [eb(nc.sbuf_tensor(f"qblk{i}", [128, 8, 512], BF16)) for i in range(2)]
            kslab = [eb(nc.sbuf_tensor(f"kslab{i}", [128, T], BF16)) for i in range(3)]
            vslab = [eb(nc.sbuf_tensor(f"vslab{i}", [128, 32, 128], BF16)) for i in range(3)]
            ptb = [eb(nc.sbuf_tensor(f"ptb{i}", [128, 1024], BF16)) for i in range(2)]
            ptsum = [eb(nc.sbuf_tensor(f"ptsum{i}", [128, 512], BF16)) for i in range(2)]
            yblk = eb(nc.sbuf_tensor("yblk", [128, 8, 512], BF16))
            osb = eb(nc.sbuf_tensor("osb", [128, 16, 512], F32))
            rb_sb = eb(nc.sbuf_tensor("rb_sb", [128, 512], F32))
            row2 = eb(nc.sbuf_tensor("row2", [64, 512], F32))
            p_sc = [eb(nc.psum_tensor(f"p_sc{i}", [128, 1024], F32)) for i in range(2)]
            p_av = [eb(nc.psum_tensor(f"p_av{i}", [128, 512], F32)) for i in range(2)]
            p_sum = eb(nc.psum_tensor("p_sum", [1, 512], F32))
            p_rbo = eb(nc.psum_tensor("p_rbo", [128, 512], F32))
            sum_row = row2[0:1, :]
            r_row = row2[32:33, :]

            p1toks = [p1["pe"], p1["v"], p1["sc"], p1["spl"]]
            for eng in ENGINES:
                P.fence(eng, p1toks)

            P.op("sync", lambda e: e.dma_start(
                out=masks[:, :, :], in_=masksI[:, :, :].rearrange("m p t -> p m t")),
                 sem=dCON, inc=16)
            mask_tok = (dCON, dCON.n)

            P.op("sync", lambda e: e.dma_start(
                out=wo_bf[:, :, :],
                in_=cw_out["o"][:, :].rearrange("(c p) n -> p c n", p=128)),
                 waits=[gather_toks["o"]], sem=dST[0], inc=16)
            dq_wo = (dST[0], dST[0].n)

            sc_free = [None, None]
            av_free = [None, None]
            sum_free = None
            rbo_free = None
            qb_free = [None, None]
            kv_free = [None, None, None]
            ptb_free = [None, None]
            ptsum_free = [None, None]
            yblk_free = None
            osb_free = None

            def mk_score(col, ktile, psl, ksl, qsl, h):
                def f(e):
                    return e.matmul(p_sc[psl][:, col * 512:(col + 1) * 512],
                                    kslab[ksl][:, ktile * 128:(ktile + 1) * 128],
                                    qblk[qsl][:, h, :],
                                    start=True, stop=True, skip_group_check=True)
                return f

            for qb in range(NQB):
                q0 = qb * 512
                KT = (qb + 1) * 4
                qsl = qb % 2
                P.op("sync", lambda e, qsl=qsl, q0=q0:
                     e.dma_start(out=qblk[qsl][:, :, :],
                                 in_=q_sp[:, :, q0:q0 + 512].rearrange(
                                     "h p t -> p h t")),
                     waits=[qb_free[qsl], p1["spl"] if qb == 0 else None],
                     sem=dQB[qsl], inc=16)
                qtok = (dQB[qsl], dQB[qsl].n)
                for h in range(8):
                    it = qb * 8 + h
                    ksl = it % 3
                    KH = KT // 2
                    kvtok_h = None
                    for half in range(2):
                        P.op("sync", lambda e, ksl=ksl, h=h, KT=KT, half=half,
                             KH=KH:
                             e.dma_start(out=kslab[ksl][:, half * KH * 128:
                                                        (KT if half else KH)
                                                        * 128],
                                         in_=k_sp[h, :, half * KH * 128:
                                                  (KT if half else KH) * 128]),
                             waits=[kv_free[ksl]
                                    if half == 0 else None,
                                    p1["spl"] if it == 0 else None],
                             sem=dKV[ksl], inc=16)
                        P.op("sync", lambda e, ksl=ksl, h=h, KT=KT, half=half,
                             KH=KH:
                             e.dma_start(out=vslab[ksl][:, half * KH:
                                                        (KT if half else KH),
                                                        :],
                                         in_=v_sp[h, :, half * KH * 128:
                                                  (KT if half else KH) * 128],
                                         transpose=True),
                             waits=[kvtok_h], sem=dKV[ksl], inc=16)
                        if half == 0:
                            kvtok_h = (dKV[ksl], dKV[ksl].n)
                    kvtok = (dKV[ksl], dKV[ksl].n)
                    hpar = h % 2
                    avmm = None
                    summ = None
                    mt_pend = None
                    for kp in range(KT // 2 + 1):
                        if kp < KT // 2:
                            psl = kp % 2
                            ktok = kvtok_h if 2 * kp + 1 < KH else kvtok
                            P.op("tensor", mk_score(0, 2 * kp, psl, ksl, qsl, h),
                                 waits=[ktok, qtok, sc_free[psl]], sem=sPE)
                            mm2 = P.op("tensor", mk_score(1, 2 * kp + 1, psl,
                                                          ksl, qsl, h),
                                       waits=[], sem=sPE)
                            ex = P.op("scalar", lambda e, psl=psl:
                                      e.activation(ptb[psl][:, :],
                                                   p_sc[psl][:, :],
                                                   AF.Exp, scale=ATT_SCALE),
                                      waits=[mm2, ptb_free[psl]], sem=sSC)
                            sc_free[psl] = (sSC, sSC.n)
                            mtoks = [ex, ex]
                            for j, kt in enumerate((2 * kp, 2 * kp + 1)):
                                di = kt - (KT - 4)
                                if 0 <= di < 4:
                                    mtoks[j] = P.op(
                                        "vector",
                                        lambda e, psl=psl, j=j, di=di:
                                        e.tensor_tensor(
                                            ptb[psl][:, j * 512:(j + 1) * 512],
                                            ptb[psl][:, j * 512:(j + 1) * 512],
                                            masks[:, di, :], op=ALU.mult),
                                        waits=[ex, mask_tok], sem=sV)
                            mt_cur = (kp, mtoks)
                        else:
                            mt_cur = None
                        if mt_pend is not None:
                            kp0, mtoks0 = mt_pend
                            psl0 = kp0 % 2
                            padd = P.op("vector",
                                        lambda e, psl=psl0:
                                        e.tensor_tensor(
                                            ptsum[psl][:, :],
                                            ptb[psl][:, 0:512],
                                            ptb[psl][:, 512:1024],
                                            op=ALU.add),
                                        waits=[mtoks0[0], mtoks0[1],
                                               ptsum_free[psl0]], sem=sV)
                            for j, kt in enumerate((2 * kp0, 2 * kp0 + 1)):
                                avmm = P.op(
                                    "tensor",
                                    lambda e, psl=psl0, j=j, kt=kt, ksl=ksl,
                                    KT=KT, hpar=hpar:
                                    e.matmul(p_av[hpar][:, :],
                                             vslab[ksl][:, kt, :],
                                             ptb[psl][:, j * 512:(j + 1) * 512],
                                             start=(kt == 0),
                                             stop=(kt == KT - 1),
                                             skip_group_check=True),
                                    waits=[mtoks0[j],
                                           av_free[hpar] if kt == 0 else None],
                                    sem=sPE)
                            summ = P.op(
                                "tensor",
                                lambda e, psl=psl0, kp0=kp0, KT=KT:
                                e.matmul(p_sum[:, :], onescol_bf[:, :],
                                         ptsum[psl][:, :],
                                         start=(kp0 == 0),
                                         stop=(kp0 == KT // 2 - 1),
                                         skip_group_check=True),
                                waits=[padd,
                                       sum_free if kp0 == 0 else None],
                                sem=sPE)
                            ptb_free[psl0] = (sPE, sPE.n)
                            ptsum_free[psl0] = (sPE, sPE.n)
                        mt_pend = mt_cur
                    kv_free[ksl] = (sPE, sPE.n)
                    t = P.op("scalar", lambda e: e.activation(r_row,
                                                              p_sum[0:1, :],
                                                              AF.Ln),
                             waits=[summ], sem=sSC)
                    sum_free = (sSC, sSC.n)
                    t = P.op("scalar", lambda e: e.activation(r_row, r_row,
                                                              AF.Exp,
                                                              scale=-1.0),
                             waits=[t], sem=sSC)
                    rb = P.op("tensor", lambda e:
                              e.matmul(p_rbo[:, :], onesrow_f[32:33, :], r_row,
                                       start=True, stop=True,
                                       skip_group_check=True),
                              waits=[t, rbo_free], sem=sPE)
                    t = P.op("vector", lambda e: e.tensor_copy(rb_sb[:, :],
                                                               p_rbo[:, :]),
                             waits=[rb], sem=sV)
                    rbo_free = (sV, sV.n)
                    P.op("vector", lambda e, h=h, hpar=hpar:
                         e.tensor_tensor(yblk[:, h, :], p_av[hpar][:, :],
                                         rb_sb[:, :], op=ALU.mult),
                         waits=[avmm, yblk_free if h == 0 else None], sem=sV)
                    av_free[hpar] = (sV, sV.n)
                y_tok = (sV, sV.n)
                ev_last = None
                for oc in range(16):
                    r3 = oc % 3
                    pt = [p_av[0], p_av[1], p_rbo][r3]
                    mt = None
                    for ic in range(8):
                        w = [y_tok] if ic == 0 else []
                        if qb == 0 and oc == 0 and ic == 0:
                            w.append(dq_wo)
                        if ic == 0:
                            w.append(rbo_free if r3 == 2 else av_free[r3])
                        mt = P.op("tensor", lambda e, oc=oc, ic=ic, pt=pt:
                                  e.matmul(pt[:, :],
                                           wo_bf[:, ic, oc * 128:(oc + 1) * 128],
                                           yblk[:, ic, :], start=(ic == 0),
                                           stop=(ic == 7),
                                           skip_group_check=True),
                                  waits=w, sem=sPE)
                    ev = P.op("vector", lambda e, oc=oc, pt=pt:
                              e.tensor_copy(osb[:, oc, :], pt[:, :]),
                              waits=[mt, osb_free if oc == 0 else None],
                              sem=sV)
                    if r3 == 2:
                        rbo_free = (sV, sV.n)
                    else:
                        av_free[r3] = (sV, sV.n)
                    ev_last = ev
                yblk_free = (sPE, sPE.n)
                qb_free[qsl] = (sPE, sPE.n)
                c = qb % 4
                hf = qb // 4
                P.op("gpsimd", lambda e, c=c, hf=hf:
                     e.dma_start(out=o_part[c, hf * 2048:(hf + 1) * 2048, :]
                                 .rearrange("(o p) t -> p o t", p=128),
                                 in_=osb[:, :, :]),
                     waits=[ev_last], sem=dOP[c], inc=16)
                osb_free = (dOP[c], dOP[c].n)
                if hf == 1:
                    P.op("gpsimd", lambda e, c=c:
                         e.collective_compute(
                             "ReduceScatter", ALU.add,
                             replica_groups=[[0, 1], [2, 3], [4, 5], [6, 7]],
                             ins=[o_part[c, :, :]], outs=[or_red[c, :, :]]),
                         waits=[(dOP[c], dOP[c].n)], sem=sCC[c])

            p2 = dict(pe=(sPE, sPE.n), v=(sV, sV.n), sc=(sSC, sSC.n))
            if upto == "p2":
                if or_dbg is not None:
                    for ch in range(4):
                        P.op("sync", lambda e, ch=ch:
                             e.dma_start(out=or_dbg[ch], in_=or_red[ch]),
                             waits=[(sCC[ch], 1)], sem=dOUT, inc=16)
                P.op("sync", lambda e: e.wait_ge(dOUT.h, 64),
                     waits=[p2["pe"], p2["v"], p2["sc"], (sCC[3], 1)])
            P.flush(blk)
        if upto == "p2":
            return nc

        GRPS = [[0, 1], [2, 3], [4, 5], [6, 7], [8]]
        with contextlib.ExitStack() as s3:
            e3 = s3.enter_context
            x1 = e3(nc.sbuf_tensor("x1", [128, 16, 1024], F32))
            x2bf = e3(nc.sbuf_tensor("x2bf", [128, 16, 1024], BF16))
            a3 = [e3(nc.sbuf_tensor(f"a3_{i}", [128, 8, 1024], BF16)) for i in range(2)]
            w13b = [e3(nc.sbuf_tensor(f"w13b{i}", [128, 16, 256], BF16)) for i in range(2)]
            w2b = [e3(nc.sbuf_tensor(f"w2b{i}", [128, 8, 256], BF16)) for i in range(2)]
            comb_cur = e3(nc.sbuf_tensor("combc", [128, 1024], BF16))
            comb_row = e3(nc.sbuf_tensor("combr", [1, 1024], BF16))
            s1f = [e3(nc.sbuf_tensor(f"s1f{i}", [128, 512], F32)) for i in range(2)]
            sq3 = e3(nc.sbuf_tensor("sq3", [128, 1024], BF16))
            xb3 = e3(nc.sbuf_tensor("xb3", [128, 1024], F32))
            ob3 = e3(nc.sbuf_tensor("ob3", [128, 1024], F32))
            m3_sb = e3(nc.sbuf_tensor("m3_sb", [128, 1024], BF16))
            r3_sb = e3(nc.sbuf_tensor("r3_sb", [128, 1024], BF16))
            g_sb = e3(nc.sbuf_tensor("g_sb", [8, 1024], F32))
            gt_sb = e3(nc.sbuf_tensor("gt_sb", [128, 64], F32))
            rtm = e3(nc.sbuf_tensor("rtm", [128, 8], F32))
            comb_tm = e3(nc.sbuf_tensor("comb_tm", [128, 8, 8], F32))
            comb_fm = e3(nc.sbuf_tensor("comb_fm", [8, 1024], BF16))
            gs_scr = [e3(nc.sbuf_tensor(f"gs{i}", [128, 8], F32)) for i in range(4)]
            gm_scr = [e3(nc.sbuf_tensor(f"gm{i}", [128, 1], F32)) for i in range(3)]
            gsum_r = e3(nc.sbuf_tensor("gsum_r", [1, 8], F32))
            gw_f = e3(nc.sbuf_tensor("gw_f", [128, 16, 8], F32))
            rp1 = e3(nc.sbuf_tensor("rp1", [96, 1024], F32))
            rp2 = e3(nc.sbuf_tensor("rp2", [64, 1024], F32))
            rpb = e3(nc.sbuf_tensor("rpb", [64, 1024], BF16))
            ph = e3(nc.psum_tensor("ph", [128, 1024], F32))
            pg = e3(nc.psum_tensor("pg", [128, 1024], F32))
            pd = [e3(nc.psum_tensor(f"pd{i}", [128, 1024], F32)) for i in range(2)]

            mean3 = rp1[0:1, :]
            msq3 = rp1[32:33, :]
            var3 = rp1[64:65, :]
            rstd3 = rp2[0:1, :]
            mr3 = rp2[32:33, :]
            r3bf = rpb[0:1, :]
            mr3bf = rpb[32:33, :]

            p2toks = [p2["pe"], p2["v"], p2["sc"]]
            for eng in ENGINES:
                P.fence(eng, p2toks)

            P.op("sync", lambda e: e.dma_start(
                out=gw_f[:, :, :],
                in_=gwT[:, :].rearrange("(c p) e -> p c e", p=128)),
                 sem=dCON, inc=16)
            gw_tok = (dCON, dCON.n)

            ph_free = None
            pg_free = None
            pd_free = [None, None]
            w13_free = [None, None]
            w2_free = [None, None]
            x3_free = [None, None]
            a3_free = [None, None]
            sq3_free = None
            s1_free = [None, None]
            crow_free = None
            out_cnt = 0
            store_toks = [None] * 16
            nx_add = [None] * 16
            nx_msq = {"mean": None, "sq": None}

            gmm = None
            for ic in range(16):
                gmm = P.op("tensor", lambda e, ic=ic:
                           e.matmul(ph[0:1, 0:8], onescol_f[:, :],
                                    gw_f[:, ic, :],
                                    start=(ic == 0), stop=(ic == 15),
                                    skip_group_check=True),
                           waits=[gw_tok, con_tok,
                                  ph_free if ic == 0 else None], sem=sPE)
            gr = P.op("vector", lambda e:
                      e.tensor_scalar(gsum_r[:, :], ph[0:1, 0:8], -1.0, None,
                                      op0=ALU.mult),
                      waits=[gmm], sem=sV)
            ph_free = gr

            def build_chunk(sh, ic, add_list, msq, store_tok=None):
                nonlocal sq3_free, ph_free, pg_free
                tl_ = sh * 1024
                P.op("sync", lambda e, ic=ic, tl=tl_:
                     e.dma_start(out=xb3[:, :],
                                 in_=xTo[ic * 128:(ic + 1) * 128,
                                         tl:tl + 1024]),
                     waits=[x3_free[0]], sem=dX3[0], inc=16)
                P.op("sync", lambda e, ic=ic, sh=sh:
                     e.dma_start(out=ob3[:, 0:512],
                                 in_=or_red[2 * sh,
                                            ic * 128:(ic + 1) * 128, :]),
                     waits=[(sCC[2 * sh], 1), x3_free[1]],
                     sem=dX3[1], inc=16)
                P.op("sync", lambda e, ic=ic, sh=sh:
                     e.dma_start(out=ob3[:, 512:1024],
                                 in_=or_red[2 * sh + 1,
                                            ic * 128:(ic + 1) * 128, :]),
                     waits=[(sCC[2 * sh + 1], 1)], sem=dX3[1], inc=16)
                P.op("vector", lambda e, ic=ic:
                     e.tensor_tensor(x1[:, ic, :], xb3[:, :], ob3[:, :],
                                     op=ALU.add),
                     waits=[(dX3[0], dX3[0].n), (dX3[1], dX3[1].n),
                            store_tok],
                     sem=sV)
                x3_free[0] = (sV, sV.n)
                x3_free[1] = (sV, sV.n)
                tc = P.op("vector", lambda e, ic=ic:
                          e.tensor_copy(x2bf[:, ic, :], x1[:, ic, :]),
                          waits=[], sem=sV)
                add_list[ic] = tc
                for cl in range(2):
                    msq["mean"] = P.op("tensor", lambda e, ic=ic, cl=cl:
                                       e.matmul(ph[0:1, cl * 512:(cl + 1) * 512],
                                                onescol_bf[:, :],
                                                x2bf[:, ic,
                                                     cl * 512:(cl + 1) * 512],
                                                start=(ic == 0), stop=(ic == 15),
                                                skip_group_check=True),
                                       waits=[tc,
                                              ph_free if ic == 0 and cl == 0
                                              else None],
                                       sem=sPE)
                tsq = P.op("vector", lambda e, ic=ic:
                           e.tensor_tensor(sq3[:, :], x2bf[:, ic, :],
                                           x2bf[:, ic, :], op=ALU.mult),
                           waits=[tc, sq3_free], sem=sV)
                for cl in range(2):
                    msq["sq"] = P.op("tensor", lambda e, ic=ic, cl=cl:
                                     e.matmul(pg[0:1, cl * 512:(cl + 1) * 512],
                                              onescol_bf[:, :],
                                              sq3[:, cl * 512:(cl + 1) * 512],
                                              start=(ic == 0), stop=(ic == 15),
                                              skip_group_check=True),
                                     waits=[tsq,
                                            pg_free if ic == 0 and cl == 0
                                            else None],
                                     sem=sPE)
                sq3_free = (sPE, sPE.n)
                if ic == 15:
                    ph_free = None
                    pg_free = None

            pre = {0: None, 1: None}
            for sh in range(2):
                tl = sh * 1024
                if pre[sh] is None:
                    add_toks = [None] * 16
                    msq = {"mean": None, "sq": None}
                    for ic in range(16):
                        build_chunk(sh, ic, add_toks, msq)
                else:
                    add_toks, msq = pre[sh]
                mean_mm = msq["mean"]
                sq_mm = msq["sq"]
                t = P.op("vector", lambda e:
                         e.tensor_scalar(mean3, ph[0:1, :], 1.0 / 2048.0,
                                         None, op0=ALU.mult),
                         waits=[mean_mm], sem=sV)
                mrow_tok = t
                ph_free = (sV, sV.n)
                t = P.op("vector", lambda e:
                         e.tensor_tensor(msq3, mean3, mean3, op=ALU.mult),
                         waits=[t], sem=sV)
                t = P.op("vector", lambda e:
                         e.scalar_tensor_tensor(var3, pg[0:1, :],
                                                1.0 / 2048.0, msq3,
                                                op0=ALU.mult, op1=ALU.subtract),
                         waits=[sq_mm, t], sem=sV)
                pg_free = (sV, sV.n)
                t = P.op("vector", lambda e:
                         e.tensor_scalar(var3, var3, 1e-5, None, op0=ALU.add),
                         waits=[t], sem=sV)
                t = P.op("scalar", lambda e:
                         e.activation(var3, var3, AF.Ln),
                         waits=[t], sem=sSC)
                rrow_tok = P.op("scalar", lambda e:
                                e.activation(rstd3, var3, AF.Exp, scale=-0.5),
                                waits=[t], sem=sSC)
                t = P.op("vector", lambda e:
                         e.tensor_tensor(mr3, mean3, rstd3, op=ALU.mult),
                         waits=[rrow_tok], sem=sV)
                t1 = P.op("vector", lambda e: e.tensor_copy(r3bf, rstd3),
                          waits=[t], sem=sV)
                t2 = P.op("vector", lambda e: e.tensor_copy(mr3bf, mr3),
                          waits=[t], sem=sV)
                b1 = b2 = None
                for cl in range(2):
                    b1 = P.op("tensor", lambda e, cl=cl:
                              e.matmul(pd[0][:, cl * 512:(cl + 1) * 512],
                                       onesrow_bf[32:33, :],
                                       mr3bf[0:1, cl * 512:(cl + 1) * 512],
                                       start=True, stop=True,
                                       skip_group_check=True),
                              waits=[t2, pd_free[0] if cl == 0 else None],
                              sem=sPE)
                    b2 = P.op("tensor", lambda e, cl=cl:
                              e.matmul(pd[1][:, cl * 512:(cl + 1) * 512],
                                       onesrow_bf[0:1, :],
                                       r3bf[0:1, cl * 512:(cl + 1) * 512],
                                       start=True, stop=True,
                                       skip_group_check=True),
                              waits=[t1, pd_free[1] if cl == 0 else None],
                              sem=sPE)
                c1 = P.op("vector", lambda e: e.tensor_copy(m3_sb[:, :],
                                                            pd[0][:, :]),
                          waits=[b1], sem=sV)
                pd_free[0] = (sV, sV.n)
                c2_ = P.op("vector", lambda e: e.tensor_copy(r3_sb[:, :],
                                                             pd[1][:, :]),
                           waits=[b2], sem=sV)
                pd_free[1] = (sV, sV.n)
                xh_last = None
                for ic in range(16):
                    t = P.op("vector", lambda e, ic=ic:
                             e.tensor_tensor(sq3[:, :], x2bf[:, ic, :],
                                             r3_sb[:, :], op=ALU.mult),
                             waits=[c2_, sq3_free], sem=sV)
                    xh_last = P.op("vector", lambda e, ic=ic:
                                   e.tensor_tensor(x2bf[:, ic, :], sq3[:, :],
                                                   m3_sb[:, :], op=ALU.subtract),
                                   waits=[c1, t], sem=sV)
                    sq3_free = (sV, sV.n)
                xhat_tok = xh_last
                gmm = None
                for ic in range(16):
                    for cl in range(2):
                        gmm = P.op("tensor", lambda e, ic=ic, cl=cl:
                                   e.matmul(ph[0:8, cl * 512:(cl + 1) * 512],
                                            gw_f[:, ic, :],
                                            x1[:, ic, cl * 512:(cl + 1) * 512],
                                            start=(ic == 0), stop=False,
                                            skip_group_check=True),
                                   waits=[add_toks[ic], gr,
                                          ph_free if ic == 0 and cl == 0
                                          else None],
                                   sem=sPE)
                for cl in range(2):
                    gmm = P.op("tensor", lambda e, cl=cl:
                               e.matmul(ph[0:8, cl * 512:(cl + 1) * 512],
                                        gsum_r[:, :],
                                        mean3[0:1, cl * 512:(cl + 1) * 512],
                                        start=False, stop=True,
                                        skip_group_check=True),
                               waits=[mrow_tok], sem=sPE)
                tg = P.op("vector", lambda e: e.tensor_copy(g_sb[:, :],
                                                            ph[0:8, :]),
                          waits=[gmm], sem=sV)
                ph_free = (sV, sV.n)
                rt = None
                for s8 in range(8):
                    rt = P.op("tensor", lambda e, s8=s8:
                              e.transpose(pd[0][:, s8:s8 + 1],
                                          rstd3[0:1, s8 * 128:(s8 + 1) * 128],
                                          ident1[:, :]),
                              waits=[rrow_tok, con_tok,
                                     pd_free[0] if s8 == 0 else None], sem=sPE)
                rt = P.op("vector", lambda e: e.tensor_copy(rtm[:, :],
                                                            pd[0][:, 0:8]),
                          waits=[rt], sem=sV)
                pd_free[0] = (sV, sV.n)
                gt = None
                for s8 in range(8):
                    gt = P.op("tensor", lambda e, s8=s8:
                              e.transpose(pd[1][:, s8 * 8:(s8 + 1) * 8],
                                          g_sb[:, s8 * 128:(s8 + 1) * 128],
                                          ident8[:, :]),
                              waits=[tg, con_tok,
                                     pd_free[1] if s8 == 0 else None], sem=sPE)
                gt = P.op("vector", lambda e: e.tensor_copy(gt_sb[:, :],
                                                            pd[1][:, 0:64]),
                          waits=[gt], sem=sV)
                pd_free[1] = (sV, sV.n)
                cb_toks = []
                prev_ax = None
                for s8 in range(8):
                    g0 = P.op("vector", lambda e, s8=s8:
                              e.tensor_scalar(gs_scr[0][:, :],
                                              gt_sb[:, s8 * 8:(s8 + 1) * 8],
                                              rtm[:, s8:s8 + 1], None,
                                              op0=ALU.mult),
                              waits=[gt, rt, prev_ax], sem=sV)
                    m1 = P.op("vector", lambda e:
                              e.reduce_max(gm_scr[0][:, :], gs_scr[0][:, :],
                                           axis=AX),
                              waits=[g0], sem=sV)
                    nm = P.op("vector", lambda e:
                              e.tensor_scalar(gm_scr[1][:, :], gm_scr[0][:, :],
                                              -1.0, None, op0=ALU.mult),
                              waits=[m1], sem=sV)
                    ax = P.op("scalar", lambda e:
                              e.activation(gs_scr[1][:, :], gs_scr[0][:, :],
                                           AF.Exp, bias=gm_scr[1][:, :]),
                              waits=[nm], sem=sSC)
                    prev_ax = ax
                    sm = P.op("vector", lambda e:
                              e.reduce_sum(gm_scr[2][:, :], gs_scr[1][:, :],
                                           axis=AX),
                              waits=[ax], sem=sV)
                    eq = P.op("vector", lambda e:
                              e.tensor_scalar(gs_scr[2][:, :], gs_scr[0][:, :],
                                              gm_scr[0][:, :], None,
                                              op0=ALU.is_ge),
                              waits=[m1], sem=sV)
                    cx = P.op("vector", lambda e:
                              e.scalar_tensor_tensor(gs_scr[3][:, :],
                                                     gs_scr[2][:, :], -1e30,
                                                     gs_scr[0][:, :],
                                                     op0=ALU.mult, op1=ALU.add),
                              waits=[eq], sem=sV)
                    m2 = P.op("vector", lambda e:
                              e.reduce_max(gm_scr[1][:, :], gs_scr[3][:, :],
                                           axis=AX),
                              waits=[cx], sem=sV)
                    msk = P.op("vector", lambda e:
                               e.tensor_scalar(gs_scr[2][:, :], gs_scr[0][:, :],
                                               gm_scr[1][:, :], None,
                                               op0=ALU.is_ge),
                               waits=[m2], sem=sV)
                    rc = P.op("vector", lambda e:
                              e.reciprocal(gm_scr[2][:, :], gm_scr[2][:, :]),
                              waits=[sm], sem=sV)
                    ws_ = P.op("vector", lambda e:
                               e.tensor_tensor(gs_scr[1][:, :], gs_scr[1][:, :],
                                               gs_scr[2][:, :], op=ALU.mult),
                               waits=[msk], sem=sV)
                    cb = P.op("vector", lambda e, s8=s8:
                              e.tensor_scalar(comb_tm[:, s8, :],
                                              gs_scr[1][:, :],
                                              gm_scr[2][:, :], None,
                                              op0=ALU.mult),
                              waits=[ws_, rc], sem=sV)
                    cb_toks.append(cb)
                ct = None
                for s8 in range(8):
                    ct = P.op("tensor", lambda e, s8=s8:
                              e.transpose(pd[0][0:8, s8 * 128:(s8 + 1) * 128],
                                          comb_tm[:, s8, :], ident128[:, :]),
                              waits=[cb_toks[s8], con_tok,
                                     pd_free[0] if s8 == 0 else None], sem=sPE)
                ct = P.op("vector", lambda e: e.tensor_copy(comb_fm[:, :],
                                                            pd[0][0:8, :]),
                          waits=[ct], sem=sV)
                pd_free[0] = (sV, sV.n)
                cw = P.op("scalar", lambda e:
                          e.dma_start(out=comb_dram[:, :], in_=comb_fm[:, :]),
                          waits=[ct, (dCB, dCB.n)], sem=dCBW, inc=16)
                cw_tok = (dCBW, dCBW.n)

                for grp, exs in enumerate(GRPS):
                    for el, ex in enumerate(exs):
                        if ex < 8:
                            P.op("sync", lambda e, ex=ex:
                                 e.dma_start(out=comb_row[:, :],
                                             in_=comb_dram[ex:ex + 1, :]),
                                 waits=[cw_tok, crow_free], sem=dCB, inc=16)
                            crd_tok = (dCB, dCB.n)
                            cbm = None
                            for cl in range(2):
                                cbm = P.op("tensor", lambda e, cl=cl:
                                           e.matmul(
                                               pg[:, cl * 512:(cl + 1) * 512],
                                               onesrow_bf[0:1, :],
                                               comb_row[0:1,
                                                        cl * 512:(cl + 1) * 512],
                                               start=True, stop=True,
                                               skip_group_check=True),
                                           waits=[crd_tok,
                                                  pg_free if cl == 0 else None],
                                           sem=sPE)
                            crow_free = (sPE, sPE.n)
                            cbv = P.op("vector", lambda e:
                                       e.tensor_copy(comb_cur[:, :], pg[:, :]),
                                       waits=[cbm], sem=sV)
                            pg_free = (sV, sV.n)
                        else:
                            cbv = None
                        for i8 in range(8):
                            wsl = i8 % 2
                            P.op("gpsimd", lambda e, ex=ex, i8=i8, wsl=wsl:
                                 e.dma_start(
                                     out=w13b[wsl][:, :, 0:128],
                                     in_=w1T[ex, :, i8 * 128:(i8 + 1) * 128]
                                     .rearrange("(c p) i -> p c i", p=128)),
                                 waits=[w13_free[wsl]], sem=dW13[wsl], inc=16)
                            P.op("gpsimd", lambda e, ex=ex, i8=i8, wsl=wsl:
                                 e.dma_start(
                                     out=w13b[wsl][:, :, 128:256],
                                     in_=w3T[ex, :, i8 * 128:(i8 + 1) * 128]
                                     .rearrange("(c p) i -> p c i", p=128)),
                                 waits=[], sem=dW13[wsl], inc=16)
                            wtok = (dW13[wsl], dW13[wsl].n)
                            hm = gm = None
                            for ic in range(16):
                                for cl in range(2):
                                    hm = P.op("tensor",
                                              lambda e, ic=ic, cl=cl, wsl=wsl:
                                              e.matmul(
                                                  ph[:, cl * 512:(cl + 1) * 512],
                                                  w13b[wsl][:, ic, 0:128],
                                                  x2bf[:, ic,
                                                       cl * 512:(cl + 1) * 512],
                                                  start=(ic == 0),
                                                  stop=(ic == 15),
                                                  skip_group_check=True),
                                              waits=[wtok, xhat_tok,
                                                     ph_free
                                                     if ic == 0 and cl == 0
                                                     else None],
                                              sem=sPE)
                            for ic in range(16):
                                for cl in range(2):
                                    gm = P.op("tensor",
                                              lambda e, ic=ic, cl=cl, wsl=wsl:
                                              e.matmul(
                                                  pg[:, cl * 512:(cl + 1) * 512],
                                                  w13b[wsl][:, ic, 128:256],
                                                  x2bf[:, ic,
                                                       cl * 512:(cl + 1) * 512],
                                                  start=(ic == 0),
                                                  stop=(ic == 15),
                                                  skip_group_check=True),
                                              waits=[pg_free
                                                     if ic == 0 and cl == 0
                                                     else None],
                                              sem=sPE)
                            w13_free[wsl] = (sPE, sPE.n)
                            am = None
                            for cl in range(2):
                                si = P.op("scalar", lambda e, cl=cl:
                                          e.activation(
                                              s1f[cl][:, :],
                                              ph[:, cl * 512:(cl + 1) * 512],
                                              AF.Silu),
                                          waits=[hm, s1_free[cl]], sem=sSC)
                                am = P.op("vector",
                                          lambda e, el=el, i8=i8, cl=cl:
                                          e.tensor_tensor(
                                              a3[el][:, i8,
                                                     cl * 512:(cl + 1) * 512],
                                              s1f[cl][:, :],
                                              pg[:, cl * 512:(cl + 1) * 512],
                                              op=ALU.mult),
                                          waits=[si, gm,
                                                 a3_free[el]
                                                 if i8 == 0 and cl == 0
                                                 else None],
                                          sem=sV)
                                s1_free[cl] = (sV, sV.n)
                            ph_free = (sSC, sSC.n)
                            pg_free = (sV, sV.n)
                            if ex < 8:
                                P.op("vector", lambda e, el=el, i8=i8:
                                     e.tensor_tensor(a3[el][:, i8, :],
                                                     a3[el][:, i8, :],
                                                     comb_cur[:, :],
                                                     op=ALU.mult),
                                     waits=[cbv], sem=sV)
                    a_tok = (sV, sV.n)
                    nel = len(exs)
                    for dc in range(16):
                        dsl = dc % 2
                        for el, ex in enumerate(exs):
                            P.op("gpsimd", lambda e, ex=ex, el=el, dc=dc,
                                 dsl=dsl:
                                 e.dma_start(
                                     out=w2b[dsl][:, :, el * 128:(el + 1) * 128],
                                     in_=w2T[ex, :, dc * 128:(dc + 1) * 128]
                                     .rearrange("(c p) d -> p c d", p=128)),
                                 waits=[w2_free[dsl] if el == 0 else None],
                                 sem=dW2[dsl], inc=16)
                        w2tok = (dW2[dsl], dW2[dsl].n)
                        bm = None
                        for el in range(nel):
                            for i8 in range(8):
                                for cl in range(2):
                                    first = (el == 0 and i8 == 0 and cl == 0)
                                    bm = P.op("tensor",
                                              lambda e, el=el, i8=i8, cl=cl,
                                              dsl=dsl, nel=nel:
                                              e.matmul(
                                                  pd[dsl][:, cl * 512:
                                                          (cl + 1) * 512],
                                                  w2b[dsl][:, i8,
                                                           el * 128:
                                                           (el + 1) * 128],
                                                  a3[el][:, i8,
                                                         cl * 512:
                                                         (cl + 1) * 512],
                                                  start=(i8 == 0 and el == 0),
                                                  stop=(i8 == 7 and
                                                        el == nel - 1),
                                                  skip_group_check=True),
                                              waits=[w2tok, a_tok,
                                                     pd_free[dsl] if first
                                                     else None],
                                              sem=sPE)
                        w2_free[dsl] = (sPE, sPE.n)
                        ev3 = P.op("vector", lambda e, dc=dc, dsl=dsl:
                                   e.tensor_tensor(x1[:, dc, :], x1[:, dc, :],
                                                   pd[dsl][:, :], op=ALU.add),
                                   waits=[bm], sem=sV)
                        pd_free[dsl] = (sV, sV.n)
                        if grp == 4:
                            P.op("scalar", lambda e, dc=dc, tl=tl:
                                 e.dma_start(
                                     out=out[dc * 128:(dc + 1) * 128,
                                             tl:tl + 1024],
                                     in_=x1[:, dc, :]),
                                 waits=[ev3], sem=dOUT, inc=16)
                            store_toks[dc] = (dOUT, dOUT.n)
                            if sh == 0 and dc >= 2:
                                build_chunk(1, dc - 2, nx_add, nx_msq,
                                            store_tok=store_toks[dc - 2])
                    for el in range(nel):
                        a3_free[el] = (sPE, sPE.n)
                    if grp == 4 and sh == 0:
                        for ic in (14, 15):
                            build_chunk(1, ic, nx_add, nx_msq,
                                        store_tok=store_toks[ic])
                        pre[1] = (nx_add, nx_msq)
                out_cnt += 1

            final_cnt = 2 * 16 * 16
            P.op("sync", lambda e: e.wait_ge(dOUT.h, final_cnt), waits=[])
            P.flush(blk)

    return nc


_NC = None


def _get_nc():
    global _NC
    if _NC is None:
        _NC = build_program()
    return _NC


def _host_consts():
    f32 = np.float32
    pos = np.arange(T, dtype=np.float64)
    inv = 10000.0 ** (-np.arange(64, dtype=np.float64) / 64.0)
    fr = np.outer(pos, inv)
    cos = np.cos(fr).T
    sin = np.sin(fr).T
    c2s2 = np.empty((256, T), f32)
    c2s2[0:64] = cos
    c2s2[64:128] = cos
    c2s2[128:192] = -sin
    c2s2[192:256] = sin
    perm = np.roll(np.eye(128, dtype=f32), 64, axis=0)
    masks = np.zeros((4, 128, 512), dtype=ml_dtypes.bfloat16)
    ki = np.arange(128)[:, None]
    qj = np.arange(512)[None, :]
    for d in range(4):
        masks[d] = (ki + 128 * d <= qj).astype(ml_dtypes.bfloat16)
    return dict(
        c2s2=c2s2,
        permM=perm,
        masksI=masks,
        onescol_bf=np.ones((128, 1), ml_dtypes.bfloat16),
        onescol_f=np.ones((128, 1), f32),
        onesrow_bf=np.ones((96, 128), ml_dtypes.bfloat16),
        onesrow_f=np.ones((96, 128), f32),
        ident128=np.eye(128, dtype=f32),
        ident8=np.eye(8, dtype=f32),
        ident1=np.ones((1, 1), f32),
        ident1b=np.ones((1, 1), ml_dtypes.bfloat16),
    )


def kernel(L0_attn_x, L0_attn_w_q, L0_attn_w_k, L0_attn_w_v, L0_attn_w_o,
           L0_ffn_gate_w, L0_ffn_w1, L0_ffn_w2, L0_ffn_w3,
           L0_ffn_ws1, L0_ffn_ws2, L0_ffn_ws3):
    f32 = np.float32
    x = np.asarray(L0_attn_x, f32)
    wq = np.asarray(L0_attn_w_q, f32)
    wk = np.asarray(L0_attn_w_k, f32)
    wv = np.asarray(L0_attn_w_v, f32)
    wo = np.asarray(L0_attn_w_o, f32)
    gw = np.asarray(L0_ffn_gate_w, f32)
    w1 = np.asarray(L0_ffn_w1, f32)
    w2 = np.asarray(L0_ffn_w2, f32)
    w3 = np.asarray(L0_ffn_w3, f32)
    ws1 = np.asarray(L0_ffn_ws1, f32)
    ws2 = np.asarray(L0_ffn_ws2, f32)
    ws3 = np.asarray(L0_ffn_ws3, f32)

    in_maps = _make_in_maps(x, wq, wk, wv, wo, gw, w1, w2, w3, ws1, ws2, ws3)

    nc = _get_nc()
    res = run_bass_kernel_spmd(nc, in_maps, core_ids=list(range(8)))
    outp = np.empty((4, T, D), f32)
    for c in range(8):
        b, h = c // 2, c % 2
        outp[b, h * 2048:(h + 1) * 2048, :] = np.asarray(res.results[c]["out"]).T
    return outp


def _make_in_maps(x, wq, wk, wv, wo, gw, w1, w2, w3, ws1, ws2, ws3):
    w1T = np.ascontiguousarray(np.concatenate(
        [np.transpose(w1, (0, 2, 1)), ws1.T[None]], axis=0))
    w3T = np.ascontiguousarray(np.concatenate(
        [np.transpose(w3, (0, 2, 1)), ws3.T[None]], axis=0))
    w2T = np.ascontiguousarray(np.concatenate(
        [np.transpose(w2, (0, 2, 1)), ws2.T[None]], axis=0))
    gwT = np.ascontiguousarray(gw.T)
    consts = _host_consts()

    in_maps = []
    for c in range(8):
        b, h = c // 2, c % 2
        m = dict(consts)
        m["xT"] = np.ascontiguousarray(x[b].T)
        m["xTo"] = np.ascontiguousarray(x[b, h * 2048:(h + 1) * 2048, :].T)
        gp = c // 2
        m["wqT"] = np.ascontiguousarray(
            wq[h * 1024:(h + 1) * 1024, :].T[512 * gp:512 * (gp + 1), :])
        m["wkT"] = np.ascontiguousarray(
            wk[h * 1024:(h + 1) * 1024, :].T[512 * gp:512 * (gp + 1), :])
        m["wvT"] = np.ascontiguousarray(
            wv[h * 1024:(h + 1) * 1024, :].T[512 * gp:512 * (gp + 1), :])
        m["woT"] = np.ascontiguousarray(
            wo[:, h * 1024:(h + 1) * 1024].T[256 * gp:256 * (gp + 1), :])
        m["gwT"] = gwT
        m["w1T"] = w1T
        m["w3T"] = w3T
        m["w2T"] = w2T
        in_maps.append(m)
    return in_maps
